# version 10
# baseline (speedup 1.0000x reference)
import numpy as np
import ml_dtypes
from contextlib import ExitStack

import concourse.bass as bass
import concourse.mybir as mybir
from concourse.bass_utils import run_bass_kernel_spmd

F32 = mybir.dt.float32
BF16 = mybir.dt.bfloat16
I8 = mybir.dt.int8
AF = mybir.ActivationFunctionType
ALU = mybir.AluOpType
AX = mybir.AxisListType.X

D = 2048
S = 8192
HS = 4096
INTER = 8192
NT = S // 128
CH = 512
NCH = S // CH
TPC = CH // 128
NQCH = HS // CH
SCALE = float(128 ** -0.5)
NG = S // 1024
NSB = HS // 512
IPB = NG * 4
TPH = HS // 256
HH2 = HS // 2
NST = HS // 128
BIG = 1.0e30
DBG_NOFLASH = False
DBG_UNNORM = False


class Ctr:
    def __init__(self, nc, ctx, name):
        self.sem = ctx.enter_context(nc.semaphore(name))
        self.n = 0


class DmaRing:

    def __init__(self, ctr, name, depth):
        self.c = [ctr(f"{name}{j}") for j in range(depth)]
        self.depth = depth

    def inc(self, i):
        return (self.c[i % self.depth], 16)

    def wait_done(self, i):
        if i < 0:
            return None
        return (self.c[i % self.depth], 16 * (i // self.depth + 1))

    def totals(self):
        return [(c, c.n) for c in self.c if c.n > 0]


class Sched:
    def __init__(self):
        self.q = {e: [] for e in "tvasg"}

    def add(self, eng, fn, waits=(), incs=()):
        w = [cv for cv in waits if cv is not None and cv[1] > 0]
        for c, k in incs:
            c.n += k
        self.q[eng].append((fn, w, list(incs)))


def _emit(nc, sched):
    with nc.Block() as blk:
        def runner(name):
            def run(h):
                for fn, waits, incs in sched.q[name]:
                    for c, v in waits:
                        h.wait_ge(c.sem, v)
                    last = fn(h)
                    for c, k in incs:
                        last.then_inc(c.sem, k)
            return run
        blk.tensor(runner("t"))
        blk.vector(runner("v"))
        blk.scalar(runner("a"))
        blk.sync(runner("s"))
        blk.gpsimd(runner("g"))


def build_program(phases=("P0", "A", "A2", "F1", "F2"), expose=None):
    expose = expose or {}
    nc = bass.Bass("TRN2", target_bir_lowering=False)

    def din(name, shape, dtype):
        return nc.dram_tensor(name, shape, dtype, kind="ExternalInput").ap()

    def dscr(name, shape, dtype):
        if name in expose:
            kind = "ExternalInput" if expose[name] == "in" else "ExternalOutput"
            return nc.dram_tensor(name, shape, dtype, kind=kind).ap()
        return nc.dram_tensor(name, shape, dtype).ap()

    x_all = din("x_all", [S, D], F32)
    qwT = din("qwT", [D, D], F32)
    kvwT = din("kvwT", [D, 2 * D], F32)
    owT = din("owT", [D, D], F32)
    wgt = din("wgt", [64, 128, D], I8)
    wut = din("wut", [64, 128, D], I8)
    wd = din("wd", [INTER, D], I8)
    qb2 = din("qb2", [128, 16], F32)
    kvb2 = din("kvb2", [128, 32], F32)
    sg2 = din("sg2", [128, 64], F32)
    su2 = din("su2", [128, 64], F32)
    vb_bc = din("vb_bc", [128, D], F32)
    sd_bc = din("sd_bc", [128, D], F32)
    costab = din("costab", [128, S // CH, 2, CH], BF16)
    identt = din("identt", [128, 128], BF16)
    constt = din("constt", [128, 4], F32)

    kTs = dscr("kTs", [128, 16, S], BF16)
    qTs = dscr("qTs", [128, 16, HS], BF16)
    vs = dscr("vs", [S, D], BF16)
    ons = dscr("ons", [HS, D], BF16)
    x2s = dscr("x2s", [HS, D], F32)
    fs = dscr("fs", [128, NST, 64, 128], BF16)
    y = nc.dram_tensor("y", [HS, D], F32, kind="ExternalOutput").ap()

    with ExitStack() as top:
        ident_sb = top.enter_context(nc.sbuf_tensor("ident_sb", [128, 128], BF16))
        const_sb = top.enter_context(nc.sbuf_tensor("const_sb", [128, 4], F32))
        eps_ap = const_sb[:, 0:1]
        ctr = lambda name: Ctr(nc, top, name)

        c_base = ctr("c_base")
        spkq = DmaRing(ctr, "spkq", 2)
        spv = DmaRing(ctr, "spv", 2)
        spo = DmaRing(ctr, "spo", 2)
        spx2 = DmaRing(ctr, "spx2", 2)
        spf = DmaRing(ctr, "spf", 2)
        spy = DmaRing(ctr, "spy", 2)

        base = Sched()
        base.add("g", lambda h: h.dma_start(ident_sb[:, :], identt), incs=[(c_base, 16)])
        base.add("g", lambda h: h.dma_start(const_sb[:, :], constt), incs=[(c_base, 16)])
        BASE_N = c_base.n
        _first = [base]

        def new_sched():
            if _first:
                return _first.pop()
            return Sched()

        def emit_p0_pass(kind):
            with ExitStack() as ph:
                sb = lambda n, shp, dt: ph.enter_context(
                    nc.sbuf_tensor(n + "_" + kind, shp, dt))
                if kind == "kq":
                    qw_sb = sb("qw_sb", [128, 16, D], BF16)
                    wk_sb = sb("wk_sb", [128, 16, D], BF16)
                    qb_sb = sb("qb_sb", [128, 16], F32)
                    kvb_sb = sb("kvb_sb", [128, 32], F32)
                    tb_sl = [sb(f"tb{j}", [128, 2, CH], BF16) for j in range(2)]
                    kq_sb = [sb(f"kq{j}", [128, CH], BF16) for j in range(2)]
                    swp = sb("swp", [128, CH], BF16)
                else:
                    wv_sb = sb("wv_sb", [128, 16, D], BF16)
                    vb_sb = sb("vb_sb", [128, D], F32)
                    v_sb = [sb(f"vsb{j}", [128, CH], BF16) for j in range(2)]
                xnT = [sb(f"xnT{p}", [128, 16, CH], BF16) for p in range(2)]
                xbf = [sb(f"xbf{j}", [128, D], BF16) for j in range(4)]
                junk = sb("junk", [128, D], BF16)
                xn = [sb(f"xn{j}", [128, D], BF16) for j in range(4)]
                sxp = sb("sxp", [128, 2], F32)
                sxxp = sb("sxxp", [128, 2], F32)
                meanp = sb("meanp", [128, 2], F32)
                msqp = sb("msqp", [128, 2], F32)
                mvarp = sb("mvarp", [128, 2], F32)
                stdp = sb("stdp", [128, 2], F32)
                rsp = sb("rsp", [128, 2], F32)
                pst = [ph.enter_context(
                    nc.psum_tensor(f"pst{j}_{kind}", [128, 8, 128], BF16))
                    for j in range(2)]
                psr = [ph.enter_context(
                    nc.psum_tensor(f"psr{j}_{kind}", [128, CH], F32))
                    for j in range(2)]

                c_w = ctr("c_w_" + kind)
                xldr = DmaRing(ctr, "xld_" + kind, 4)
                c_sq = ctr("c_sq_" + kind)
                c_mn = ctr("c_mn_" + kind)
                c_msq = ctr("c_msq_" + kind)
                c_st1 = ctr("c_st1_" + kind)
                c_std = ctr("c_std_" + kind)
                c_xn = ctr("c_xn_" + kind)
                c_T0 = ctr("c_T0_" + kind)
                c_xnTe = ctr("c_xnTe_" + kind)
                c_rnd = ctr("c_rnd_" + kind)
                c_rev = ctr("c_rev_" + kind)
                if kind == "kq":
                    c_rope = ctr("c_rope_" + kind)
                    tbr = DmaRing(ctr, "tbr_" + kind, 2)

                sch = new_sched()
                gate = spkq.totals() if kind == "v" else []

                kv_view = kvwT.rearrange("(c p) j -> p c j", p=128)
                wload_q = []
                if kind == "kq":
                    qview = qwT.rearrange("(c p) j -> p c j", p=128)
                    for pc in range(4):
                        wload_q.append(lambda h, pc=pc: h.dma_start(
                            wk_sb[:, pc * 4:(pc + 1) * 4, :],
                            kv_view[:, pc * 4:(pc + 1) * 4, 0:D]))
                    wload_q.append(lambda h: h.dma_start(kvb_sb[:, :], kvb2))
                    wload_q.append(lambda h: h.dma_start(qb_sb[:, :], qb2))
                    for pc in range(4):
                        wload_q.append(lambda h, pc=pc: h.dma_start(
                            qw_sb[:, pc * 4:(pc + 1) * 4, :],
                            qview[:, pc * 4:(pc + 1) * 4, :]))
                else:
                    for pc in range(4):
                        wload_q.append(lambda h, pc=pc: h.dma_start(
                            wv_sb[:, pc * 4:(pc + 1) * 4, :],
                            kv_view[:, pc * 4:(pc + 1) * 4, D:2 * D]))
                    wload_q.append(lambda h: h.dma_start(vb_sb[:, :], vb_bc))

                def pump_wload():
                    if wload_q:
                        fn = wload_q.pop(0)
                        sch.add("g", fn, waits=gate if not wl_started else [],
                                incs=[(c_w, 16)])
                        wl_started.append(1)
                wl_started = []
                W_N = 16 * len(wload_q)
                WK_N = 16 * 6 if kind == "kq" else W_N

                if kind == "kq":
                    rounds_in = lambda ch: 16 + (16 if ch < NQCH else 0)
                    ropes_in = lambda ch: 16 + (16 if ch < NQCH else 0)
                else:
                    rounds_in = lambda ch: 16
                    ropes_in = lambda ch: 0
                rounds_before = [0]
                ropes_thru = [0]
                for ch in range(NCH):
                    rounds_before.append(rounds_before[-1] + rounds_in(ch))
                    ropes_thru.append(ropes_thru[-1] + ropes_in(ch))

                def ln_tile(i):
                    s3, s2 = i % 4, i % 2

                    def f_load(h, i=i, s3=s3):
                        return h.dma_start(xbf[s3][:, :],
                                           x_all[i * 128:(i + 1) * 128, :])
                    sch.add("g", f_load,
                            waits=[(c_xn, max(0, i - 3)), (c_base, BASE_N)] + gate,
                            incs=[xldr.inc(i)])
                    pump_wload()

                    def f_cp(h, s3=s3, s2=s2):
                        return h.activation(junk[:, :], xbf[s3][:, :], AF.Copy,
                                            accum_out=sxp[:, s2:s2 + 1])
                    sch.add("a", f_cp,
                            waits=[xldr.wait_done(i), (c_st1, max(0, i - 1))])

                    def f_sq(h, s3=s3, s2=s2):
                        return h.activation(junk[:, :], xbf[s3][:, :], AF.Square,
                                            accum_out=sxxp[:, s2:s2 + 1])
                    sch.add("a", f_sq, incs=[(c_sq, 1)])

                    def f_mean(h, s2=s2):
                        return h.tensor_scalar(meanp[:, s2:s2 + 1],
                                               sxp[:, s2:s2 + 1],
                                               1.0 / D, None, ALU.mult)
                    sch.add("v", f_mean,
                            waits=[(c_sq, i + 1), (c_std, max(0, i - 1))],
                            incs=[(c_mn, 1)])

                    def f_msq(h, s2=s2):
                        return h.activation(msqp[:, s2:s2 + 1],
                                            meanp[:, s2:s2 + 1], AF.Square)
                    sch.add("a", f_msq, waits=[(c_mn, i + 1)], incs=[(c_msq, 1)])

                    def f_var(h, s2=s2):
                        return h.scalar_tensor_tensor(
                            mvarp[:, s2:s2 + 1], sxxp[:, s2:s2 + 1], 1.0 / D,
                            msqp[:, s2:s2 + 1], ALU.mult, ALU.subtract)
                    sch.add("v", f_var, waits=[(c_msq, i + 1)], incs=[(c_st1, 1)])

                    def f_sqrt(h, s2=s2):
                        h.activation(stdp[:, s2:s2 + 1], mvarp[:, s2:s2 + 1],
                                     AF.Ln, bias=eps_ap)
                        return h.activation(rsp[:, s2:s2 + 1], stdp[:, s2:s2 + 1],
                                            AF.Exp, scale=-0.5)
                    sch.add("a", f_sqrt,
                            waits=[(c_st1, i + 1), (c_xn, max(0, i - 1))],
                            incs=[(c_std, 1)])

                    def f_xn(h, s3=s3, s2=s2):
                        return h.tensor_scalar(xn[s3][:, :], xbf[s3][:, :],
                                               meanp[:, s2:s2 + 1],
                                               rsp[:, s2:s2 + 1],
                                               ALU.subtract, ALU.mult)
                    sch.add("v", f_xn,
                            waits=[(c_std, i + 1), (c_T0, max(0, 2 * (i - 4) + 2))],
                            incs=[(c_xn, 1)])

                def t_batch(i, b):
                    ch = i // TPC
                    par = ch % 2
                    st = i % TPC

                    def f_T(h, i=i, b=b):
                        last = None
                        for j in range(8):
                            cc0 = 8 * b + j
                            last = h.transpose(pst[b][:, j, :],
                                               xn[i % 4][:, cc0 * 128:(cc0 + 1) * 128],
                                               ident_sb[:, :])
                        return last
                    w = [(c_xn, i + 1), (c_xnTe, max(0, 2 * (i - 1) + b + 1))]
                    if ch >= 2:
                        w.append((c_rnd, rounds_before[ch - 1]))
                    sch.add("t", f_T, waits=w, incs=[(c_T0, 1)])

                    def f_ev(h, par=par, b=b, st=st):
                        return h.tensor_copy(
                            xnT[par][:, 8 * b:8 * b + 8, st * 128:(st + 1) * 128],
                            pst[b][:, :, :])
                    sch.add("v", f_ev, waits=[(c_T0, 2 * i + b + 1)],
                            incs=[(c_xnTe, 1)])

                def tbl_load(ch):
                    sl = ch % 2
                    w = [(c_rope, ropes_thru[ch - 1])] if ch >= 2 else []

                    def f_c(h, ch=ch, sl=sl):
                        return h.dma_start(tb_sl[sl][:, :, :], costab[:, ch, :, :])
                    sch.add("a", f_c, waits=w, incs=[tbr.inc(ch)])

                state = {"r": 0, "skq": 0, "sv": 0, "rp": 0}

                def do_round(knd, ch, a1, a2=0):
                    par = ch % 2
                    rr = state["r"]
                    pp = rr % 2

                    if knd in ("k", "q"):
                        oc = a1
                        wsb = wk_sb if knd == "k" else qw_sb

                        def f_mm(h, wsb=wsb, oc=oc, par=par, pp=pp):
                            last = None
                            for c in range(16):
                                last = h.matmul(psr[pp][:, :],
                                                wsb[:, c, oc * 128:(oc + 1) * 128],
                                                xnT[par][:, c, :],
                                                start=(c == 0), stop=(c == 15))
                            return last
                    else:
                        st4, eq = a1, a2

                        def f_mm(h, st4=st4, eq=eq, par=par, pp=pp):
                            last = None
                            for c in range(16):
                                last = h.matmul(psr[pp][:, :],
                                                xnT[par][:, c, st4 * 128:(st4 + 1) * 128],
                                                wv_sb[:, c, eq * CH:(eq + 1) * CH],
                                                start=(c == 0), stop=(c == 15))
                            return last
                    wneed = WK_N if knd in ("k", "v") else W_N
                    sch.add("t", f_mm,
                            waits=[(c_xnTe, 8 * (ch + 1)), (c_w, wneed),
                                   (c_rev, max(0, rr - 1))],
                            incs=[(c_rnd, 1)])

                    if knd in ("k", "q"):
                        skq = state["skq"]
                        es = skq % 2
                        bias = (kvb_sb[:, a1:a1 + 1] if knd == "k"
                                else qb_sb[:, a1:a1 + 1])

                        def f_ev(h, es=es, pp=pp, bias=bias):
                            return h.activation(kq_sb[es][:, :], psr[pp][:, :],
                                                AF.Identity, bias=bias)
                        sch.add("a", f_ev,
                                waits=[(c_rnd, rr + 1), spkq.wait_done(skq - 2)],
                                incs=[(c_rev, 1)])

                        sl = ch % 2

                        def f_rope(h, es=es, sl=sl):
                            h.tensor_copy(swp[0:64, :], kq_sb[es][64:128, :])
                            h.tensor_copy(swp[64:128, :], kq_sb[es][0:64, :])
                            h.tensor_mul(swp[:, :], swp[:, :], tb_sl[sl][:, 1, :])
                            h.tensor_mul(kq_sb[es][:, :], kq_sb[es][:, :],
                                         tb_sl[sl][:, 0, :])
                            return h.tensor_add(kq_sb[es][:, :], kq_sb[es][:, :],
                                                swp[:, :])
                        sch.add("v", f_rope,
                                waits=[(c_rev, rr + 1), tbr.wait_done(ch)],
                                incs=[(c_rope, 1)])

                        dst = (kTs[:, a1, ch * CH:(ch + 1) * CH] if knd == "k"
                               else qTs[:, a1, ch * CH:(ch + 1) * CH])

                        def f_sp(h, es=es, dst=dst):
                            return h.dma_start(dst, kq_sb[es][:, :])
                        sch.add("s", f_sp, waits=[(c_rope, state["rp"] + 1)],
                                incs=[spkq.inc(skq)])
                        state["skq"] += 1
                        state["rp"] += 1
                    else:
                        st4, eq = a1, a2
                        sv = state["sv"]
                        es = sv % 2

                        def f_ev(h, es=es, pp=pp, eq=eq):
                            return h.tensor_add(v_sb[es][:, :], psr[pp][:, :],
                                                vb_sb[:, eq * CH:(eq + 1) * CH])
                        sch.add("v", f_ev,
                                waits=[(c_rnd, rr + 1), spv.wait_done(sv - 2)],
                                incs=[(c_rev, 1)])

                        row0 = (ch * TPC + st4) * 128

                        def f_sp(h, es=es, row0=row0, eq=eq):
                            return h.dma_start(
                                vs[row0:row0 + 128, eq * CH:(eq + 1) * CH],
                                v_sb[es][:, :])
                        sch.add("s", f_sp, waits=[(c_rev, rr + 1)],
                                incs=[spv.inc(sv)])
                        state["sv"] += 1
                    state["r"] += 1

                for ch in range(NCH + 1):
                    if ch >= 1:
                        while wload_q:
                            pump_wload()
                    if ch < NCH and kind == "kq":
                        tbl_load(ch)
                    if ch == 0:
                        for st in range(TPC):
                            ln_tile(st)
                        for st in range(TPC):
                            for b in range(2):
                                t_batch(st, b)
                    if ch >= 1:
                        cur = ch - 1
                        pend = ([(ch * TPC + st, b) for st in range(TPC)
                                 for b in range(2)] if ch < NCH else [])
                        if kind == "kq":
                            rlist = ([("k", cur, oc) for oc in range(16)]
                                     + ([("q", cur, oc) for oc in range(16)]
                                        if cur < NQCH else []))
                        else:
                            rlist = [("v", cur, st4, eq)
                                     for st4 in range(TPC) for eq in range(TPC)]
                        step = max(1, len(rlist) // 16)
                        for rn, r in enumerate(rlist):
                            do_round(r[0], r[1], r[2], r[3] if len(r) > 3 else 0)
                            if rn % step == step - 1 and pend:
                                i, b = pend.pop(0)
                                if b == 0:
                                    ln_tile(i)
                                t_batch(i, b)
                        while pend:
                            i, b = pend.pop(0)
                            if b == 0:
                                ln_tile(i)
                            t_batch(i, b)

                _emit(nc, sch)

        if "P0" in phases:
            emit_p0_pass("kq")
            emit_p0_pass("v")
        P0_DONE = spkq.totals() + spv.totals()

        if "A" in phases:
            with ExitStack() as ph:
                sb = lambda n, shp, dt: ph.enter_context(nc.sbuf_tensor(n, shp, dt))
                qT_sb = sb("qT_sb", [128, 16, 512], BF16)
                kT_b = [sb(f"kT_b{j}", [128, 16, 1024], BF16) for j in range(2)]
                v_b = [sb(f"v_b{j}", [128, 8, D], BF16) for j in range(2)]
                o_acc = sb("o_acc", [128, 4, D], F32)
                p_sb = [sb(f"p_sb{j}", [128, 1024], BF16) for j in range(2)]
                pT_sb = [sb(f"pT_sb{j}", [128, 8, 128], BF16) for j in range(2)]
                nm = sb("nm", [128, 4, 2], F32)
                pm = sb("pm", [128, 4, 2], F32)
                lsum = sb("lsum", [128, 4], F32)
                tmaxb = sb("tmaxb", [128, 1], F32)
                ccb = sb("ccb", [128, 2], F32)
                lpart = sb("lpart", [128, 2], F32)
                rlb = sb("rlb", [128, 2], F32)
                rlt = sb("rlt", [128, 1], F32)
                spcr = sb("spcr", [128, 8], F32)
                obf = [sb(f"obf{j}", [128, D], BF16) for j in range(2)]
                psS = ph.enter_context(nc.psum_tensor("psS", [128, 1024], F32))
                psT = ph.enter_context(nc.psum_tensor("psT", [128, 8, 128], BF16))
                psO = ph.enter_context(nc.psum_tensor("psO", [128, D], F32))

                qldr = DmaRing(ctr, "qld", 4)
                kldr = DmaRing(ctr, "kld", 2)
                vldr = DmaRing(ctr, "vld", 2)
                c_sc = ctr("c_sc")
                c_nm = ctr("c_nm")
                c_cc = ctr("c_cc")
                c_exp = ctr("c_exp")
                c_lup = ctr("c_lup")
                c_T = ctr("c_T")
                c_pTe = ctr("c_pTe")
                c_pv = ctr("c_pv")
                c_ofl = ctr("c_ofl")
                c_obf = ctr("c_obf")
                c_rm = ctr("c_rm")
                c_rl = ctr("c_rl")
                c_ms = ctr("c_ms")

                sch = new_sched()

                def f_init(h):
                    h.memset(o_acc[:, :, :], 0.0)
                    h.memset(lsum[:, :], 0.0)
                    h.memset(pm[:, :, :], -BIG)
                    return h.memset(nm[:, :, :], BIG)
                sch.add("v", f_init, incs=[(c_ms, 1)])

                def sc_round(sbk, idx):
                    gidx = sbk * IPB + idx
                    g = idx // 4
                    gg = sbk * NG + g
                    gp = g % 2
                    st = idx % 4
                    g2 = idx % 2

                    def f(h, gp=gp, st=st):
                        last = None
                        for nb in range(2):
                            for c in range(16):
                                last = h.matmul(
                                    psS[:, nb * 512:(nb + 1) * 512],
                                    qT_sb[:, c, st * 128:(st + 1) * 128],
                                    kT_b[gp][:, c, nb * 512:(nb + 1) * 512],
                                    start=(c == 0), stop=(c == 15))
                        return last
                    sch.add("t", f,
                            waits=[qldr.wait_done(sbk * 4 + st), kldr.wait_done(gg),
                                   (c_exp, gidx)],
                            incs=[(c_sc, 1)])

                    def f_rm(h):
                        return h.reduce_max(tmaxb[:, :], psS[:, :], axis=AX)
                    if not DBG_NOFLASH:
                        sch.add("v", f_rm, waits=[(c_sc, gidx + 1)],
                                incs=[(c_rm, 1)])

                    def f_cc(h, st=st, gp=gp, g2=g2):
                        return h.activation(ccb[:, g2:g2 + 1],
                                            pm[:, st, 1 - gp:2 - gp], AF.Exp,
                                            bias=nm[:, st, gp:gp + 1])
                    if not DBG_NOFLASH:
                        sch.add("a", f_cc,
                                waits=[(c_nm, gidx + 1), (c_ofl, max(0, gidx - 1))],
                                incs=[(c_cc, 1)])

                    if DBG_NOFLASH:
                        def f_exp(h, st=st, gp=gp, g2=g2):
                            return h.activation(p_sb[g2][:, :], psS[:, :], AF.Exp,
                                                scale=SCALE,
                                                accum_out=lpart[:, g2:g2 + 1])
                        sch.add("a", f_exp,
                                waits=[(c_sc, gidx + 1),
                                       (c_T, max(0, gidx - 1)),
                                       (c_lup, max(0, gidx - 1))],
                                incs=[(c_exp, 1)])
                    else:
                        def f_exp(h, st=st, gp=gp, g2=g2):
                            return h.activation(p_sb[g2][:, :], psS[:, :], AF.Exp,
                                                bias=nm[:, st, gp:gp + 1], scale=SCALE,
                                                accum_out=lpart[:, g2:g2 + 1])
                        sch.add("a", f_exp,
                                waits=[(c_T, max(0, gidx - 1)),
                                       (c_lup, max(0, gidx - 1))],
                                incs=[(c_exp, 1)])

                def nm_update(sbk, idx):
                    if DBG_NOFLASH:
                        return
                    gidx = sbk * IPB + idx
                    g = idx // 4
                    gp = g % 2
                    st = idx % 4
                    g2 = idx % 2

                    def f_nmu(h, st=st, gp=gp):
                        h.tensor_scalar(nm[:, st, gp:gp + 1], tmaxb[:, :],
                                        -SCALE, nm[:, st, 1 - gp:2 - gp],
                                        ALU.mult, ALU.min)
                        return h.tensor_scalar(pm[:, st, gp:gp + 1], tmaxb[:, :],
                                               SCALE, pm[:, st, 1 - gp:2 - gp],
                                               ALU.mult, ALU.max)
                    sch.add("v", f_nmu,
                            waits=[(c_rm, gidx + 1), (c_ms, c_ms.n),
                                   (c_cc, max(0, gidx - 1))],
                            incs=[(c_nm, 1)])

                def spacer3(h):
                    h.memset(spcr[:, :], 0.0)
                    h.memset(spcr[:, :], 1.0)
                    return h.memset(spcr[:, :], 2.0)

                def t_pv_pTe(sbk, idx):
                    gidx = sbk * IPB + idx
                    g2 = idx % 2

                    def f_T(h, g2=g2):
                        last = None
                        for tc in range(8):
                            last = h.transpose(psT[:, tc, :],
                                               p_sb[g2][:, tc * 128:(tc + 1) * 128],
                                               ident_sb[:, :])
                        return last
                    sch.add("t", f_T, waits=[(c_exp, gidx + 1), (c_pTe, gidx)],
                            incs=[(c_T, 1)])

                    def f_pTe(h, g2=g2):
                        return h.tensor_copy(pT_sb[g2][:, :, :], psT[:, :, :])
                    sch.add("v", f_pTe,
                            waits=[(c_T, gidx + 1), (c_pv, max(0, gidx - 1))],
                            incs=[(c_pTe, 1)])

                def t_pv_a(sbk, idx):
                    gidx = sbk * IPB + idx
                    g2 = idx % 2
                    st = idx % 4

                    cscal = 1.0 if DBG_NOFLASH else ccb[:, g2:g2 + 1]

                    def f_lup(h, st=st, g2=g2, cscal=cscal):
                        return h.scalar_tensor_tensor(
                            lsum[:, st:st + 1], lsum[:, st:st + 1],
                            cscal, lpart[:, g2:g2 + 1],
                            ALU.mult, ALU.add)
                    lw = [(c_exp, gidx + 1)]
                    if not DBG_NOFLASH:
                        lw.append((c_cc, gidx + 1))
                    sch.add("v", f_lup, waits=lw, incs=[(c_lup, 1)])

                def t_pv_b(sbk, idx):
                    gidx = sbk * IPB + idx
                    g = idx // 4
                    gg = sbk * NG + g
                    gp = g % 2
                    st = idx % 4
                    g2 = idx % 2

                    def f_pv(h, g2=g2, gp=gp):
                        last = None
                        for nb in range(4):
                            for tc in range(8):
                                last = h.matmul(
                                    psO[:, nb * 512:(nb + 1) * 512],
                                    pT_sb[g2][:, tc, :],
                                    v_b[gp][:, tc, nb * 512:(nb + 1) * 512],
                                    start=(tc == 0), stop=(tc == 7))
                        return last
                    sch.add("t", f_pv,
                            waits=[(c_pTe, gidx + 1), vldr.wait_done(gg),
                                   (c_ofl, gidx)],
                            incs=[(c_pv, 1)])

                    cscal2 = 1.0 if DBG_NOFLASH else ccb[:, g2:g2 + 1]

                    def f_ofl(h, st=st, g2=g2, cscal2=cscal2):
                        return h.scalar_tensor_tensor(
                            o_acc[:, st, :], o_acc[:, st, :],
                            cscal2, psO[:, :], ALU.mult, ALU.add)
                    sch.add("v", f_ofl, waits=[(c_pv, gidx + 1)],
                            incs=[(c_ofl, 1)])

                for sbk in range(NSB):
                    for g in range(2):
                        gg = sbk * NG + g
                        gp = g % 2

                        def f_k(h, g=g, gp=gp):
                            return h.dma_start(kT_b[gp][:, :, :],
                                               kTs[:, :, g * 1024:(g + 1) * 1024])
                        sch.add("s", f_k,
                                waits=P0_DONE + [(c_sc, max(0, 4 * (gg - 1)))],
                                incs=[kldr.inc(gg)])

                        def f_v(h, g=g, gp=gp):
                            return h.dma_start(
                                v_b[gp][:, :, :],
                                vs[g * 1024:(g + 1) * 1024, :]
                                .rearrange("(tc p) e -> p tc e", p=128))
                        sch.add("g", f_v,
                                waits=P0_DONE + [(c_pv, max(0, 4 * (gg - 1)))],
                                incs=[vldr.inc(gg)])

                    for qp in range(4):
                        def f_q(h, sbk=sbk, qp=qp):
                            return h.dma_start(
                                qT_sb[:, :, qp * 128:(qp + 1) * 128],
                                qTs[:, :, sbk * 512 + qp * 128:
                                    sbk * 512 + (qp + 1) * 128])
                        w = [(c_sc, max(0, (sbk - 1) * IPB + IPB - 4 + qp + 1))]
                        sch.add("a", f_q, waits=P0_DONE + w,
                                incs=[qldr.inc(sbk * 4 + qp)])

                    if sbk > 0:
                        def f_ms(h):
                            if DBG_NOFLASH:
                                h.memset(o_acc[:, :, :], 0.0)
                                h.memset(lsum[:, :], 0.0)
                            h.memset(pm[:, :, :], -BIG)
                            return h.memset(nm[:, :, :], BIG)
                        msw = [(c_exp, IPB * sbk)]
                        if DBG_NOFLASH:
                            msw.append((c_obf, 4 * sbk))
                        sch.add("v", f_ms, waits=msw, incs=[(c_ms, 1)])

                    for g in range(2, NG):
                        gg = sbk * NG + g
                        gp = g % 2

                        def f_k(h, g=g, gp=gp):
                            return h.dma_start(kT_b[gp][:, :, :],
                                               kTs[:, :, g * 1024:(g + 1) * 1024])
                        sch.add("s", f_k,
                                waits=P0_DONE + [(c_sc, max(0, 4 * (gg - 1)))],
                                incs=[kldr.inc(gg)])

                        def f_v(h, g=g, gp=gp):
                            return h.dma_start(
                                v_b[gp][:, :, :],
                                vs[g * 1024:(g + 1) * 1024, :]
                                .rearrange("(tc p) e -> p tc e", p=128))
                        sch.add("g", f_v,
                                waits=P0_DONE + [(c_pv, max(0, 4 * (gg - 1)))],
                                incs=[vldr.inc(gg)])

                    sc_round(sbk, 0)
                    if not DBG_NOFLASH:
                        sch.add("v", spacer3)
                    nm_update(sbk, 0)
                    for idx in range(1, IPB):
                        t_pv_pTe(sbk, idx - 1)
                        sc_round(sbk, idx)
                        t_pv_a(sbk, idx - 1)
                        if not DBG_NOFLASH:
                            sch.add("v", lambda h: h.memset(spcr[:, :], 0.0))
                        nm_update(sbk, idx)
                        t_pv_b(sbk, idx - 1)
                    t_pv_pTe(sbk, IPB - 1)
                    t_pv_a(sbk, IPB - 1)
                    t_pv_b(sbk, IPB - 1)

                    for st in range(4):
                        oi = sbk * 4 + st
                        os2 = oi % 2

                        def f_rl(h, st=st, os2=os2):
                            h.activation(rlt[:, :], lsum[:, st:st + 1], AF.Ln)
                            return h.activation(rlb[:, os2:os2 + 1], rlt[:, :],
                                                AF.Exp, scale=-1.0)
                        sch.add("a", f_rl,
                                waits=[(c_lup, IPB * (sbk + 1)),
                                       (c_obf, max(0, oi - 1))],
                                incs=[(c_rl, 1)])

                        def f_on(h, st=st, os2=os2):
                            if DBG_UNNORM:
                                return h.tensor_scalar(obf[os2][:, :],
                                                       o_acc[:, st, :],
                                                       1.0, None, ALU.mult)
                            return h.tensor_scalar(obf[os2][:, :], o_acc[:, st, :],
                                                   rlb[:, os2:os2 + 1], None,
                                                   ALU.mult)
                        sch.add("v", f_on,
                                waits=[(c_ofl, IPB * (sbk + 1)), (c_rl, oi + 1),
                                       spo.wait_done(oi - 2)],
                                incs=[(c_obf, 1)])

                        def f_sp(h, oi=oi, os2=os2):
                            return h.dma_start(ons[oi * 128:(oi + 1) * 128, :],
                                               obf[os2][:, :])
                        sch.add("a", f_sp, waits=[(c_obf, oi + 1)],
                                incs=[spo.inc(oi)])

                _emit(nc, sch)
        A_DONE = spo.totals()

        if "A2" in phases or "F1" in phases:
            with ExitStack() as af:
                ow_sb = af.enter_context(nc.sbuf_tensor("ow_sb", [128, 16, D], BF16))
                xn2T = af.enter_context(nc.sbuf_tensor("xn2T", [128, 16, HH2], BF16))
                sg_sb = af.enter_context(nc.sbuf_tensor("sg_sb", [128, 64], F32))
                su_sb = af.enter_context(nc.sbuf_tensor("su_sb", [128, 64], F32))

                c_w2 = ctr("c_w2")
                oldr = DmaRing(ctr, "old", 2)
                xld2r = DmaRing(ctr, "xld2", 2)
                c_T2 = ctr("c_T2")
                c_oTe = ctr("c_oTe")
                c_opr = ctr("c_opr")
                c_x2 = ctr("c_x2")
                c_sq2 = ctr("c_sq2")
                c_mn2 = ctr("c_mn2")
                c_msq2 = ctr("c_msq2")
                c_st2 = ctr("c_st2")
                c_std2 = ctr("c_std2")
                c_xn2 = ctr("c_xn2")
                c_T2b = ctr("c_T2b")
                c_xn2Te = ctr("c_xn2Te")
                wgr = DmaRing(ctr, "wgr", 2)
                wur = DmaRing(ctr, "wur", 2)
                c_gr = ctr("c_gr")
                c_ur = ctr("c_ur")
                c_gact = ctr("c_gact")
                c_us = ctr("c_us")
                c_fu = ctr("c_fu")

                XN2TE_HALF = 2 * TPH if "A2" in phases else 0

                first_af = True
                for hh in range(2):
                    if "A2" in phases:
                        with ExitStack() as ph:
                            sb = lambda n, shp, dt: ph.enter_context(
                                nc.sbuf_tensor(f"{n}_a2{hh}", shp, dt))
                            oT_sb = sb("oT_sb", [128, 32, 128], BF16)
                            ob_in = [sb(f"ob_in{j}", [128, D], BF16) for j in range(2)]
                            x2sb = [sb(f"x2sb{j}", [128, D], F32) for j in range(2)]
                            x2bf = [sb(f"x2bf{j}", [128, D], BF16) for j in range(2)]
                            xn2 = [sb(f"xn2_{j}", [128, D], BF16) for j in range(2)]
                            junk2 = sb("junk2", [128, D], BF16)
                            sxp2 = sb("sxp2", [128, 2], F32)
                            sxxp2 = sb("sxxp2", [128, 2], F32)
                            meanp2 = sb("meanp2", [128, 2], F32)
                            msqp2 = sb("msqp2", [128, 2], F32)
                            mvarp2 = sb("mvarp2", [128, 2], F32)
                            stdp2 = sb("stdp2", [128, 2], F32)
                            rsp2 = sb("rsp2", [128, 2], F32)
                            pstA = ph.enter_context(
                                nc.psum_tensor(f"pstA_a2{hh}", [128, 8, 128], BF16))
                            pstB = ph.enter_context(
                                nc.psum_tensor(f"pstB_a2{hh}", [128, 8, 128], BF16))
                            psAt = [ph.enter_context(
                                nc.psum_tensor(f"psAt{j}_a2{hh}", [128, 1024], F32))
                                for j in range(2)]

                            sch = new_sched()
                            if first_af:
                                for qq in range(4):
                                    def f_ow(h, qq=qq):
                                        return h.dma_start(
                                            ow_sb[:, 4 * qq:4 * qq + 4, :],
                                            owT.rearrange("(c p) j -> p c j", p=128)
                                            [:, 4 * qq:4 * qq + 4, :])
                                    sch.add("g", f_ow,
                                            waits=A_DONE if qq == 0 else [],
                                            incs=[(c_w2, 16)])
                                sch.add("g", lambda h: h.dma_start(sg_sb[:, :], sg2),
                                        incs=[(c_w2, 16)])
                                sch.add("g", lambda h: h.dma_start(su_sb[:, :], su2),
                                        incs=[(c_w2, 16)])
                                first_af = False
                            W2_N = c_w2.n

                            for tt in range(TPH):
                                tg = hh * TPH + tt
                                t2 = tt % 2

                                def f_ol(h, tg=tg, t2=t2):
                                    return h.dma_start(
                                        ob_in[t2][:, :],
                                        ons[tg * 128:(tg + 1) * 128, :])
                                f1gate = ([(cc, 16 * 32 * hh) for cc in spf.c]
                                          if "F1" in phases and hh else [])
                                sch.add("s", f_ol,
                                        waits=A_DONE + f1gate
                                        + [(c_T2, max(0, 2 * (tg - 1)))],
                                        incs=[oldr.inc(tg)])

                                for b, pst2 in ((0, pstA), (1, pstB)):
                                    def f_T(h, t2=t2, b=b, pst2=pst2):
                                        last = None
                                        for j in range(8):
                                            cc0 = 8 * b + j
                                            last = h.transpose(
                                                pst2[:, j, :],
                                                ob_in[t2][:, cc0 * 128:(cc0 + 1) * 128],
                                                ident_sb[:, :])
                                        return last
                                    sch.add("t", f_T,
                                            waits=[oldr.wait_done(tg),
                                                   (c_xn2Te, max(0, 2 * (tg - 1) + b + 1))],
                                            incs=[(c_T2, 1)])

                                    def f_ev(h, t2=t2, b=b, pst2=pst2):
                                        return h.tensor_copy(
                                            oT_sb[:, t2 * 16 + 8 * b:t2 * 16 + 8 * b + 8, :],
                                            pst2[:, :, :])
                                    sch.add("v", f_ev,
                                            waits=[(c_T2, 2 * tg + b + 1),
                                                   (c_opr, max(0, 2 * (tg - 1)))],
                                            incs=[(c_oTe, 1)])

                                def f_xl(h, tg=tg, t2=t2):
                                    return h.dma_start(
                                        x2sb[t2][:, :],
                                        x_all[tg * 128:(tg + 1) * 128, :])
                                sch.add("s", f_xl,
                                        waits=A_DONE + f1gate
                                        + [(c_sq2, max(0, tg - 1)),
                                           spx2.wait_done(tg - 2)],
                                        incs=[xld2r.inc(tg)])

                                for rr in range(2):
                                    for nb in range(2):
                                        for qq in range(2):
                                            def f_pr(h, t2=t2, rr=rr, nb=nb, qq=qq):
                                                last = None
                                                for c in range(8 * qq, 8 * qq + 8):
                                                    last = h.matmul(
                                                        psAt[rr][:, nb * 512:(nb + 1) * 512],
                                                        oT_sb[:, t2 * 16 + c, :],
                                                        ow_sb[:, c, rr * 1024 + nb * 512:
                                                              rr * 1024 + (nb + 1) * 512],
                                                        start=(c == 0), stop=(c == 15))
                                                return last
                                            w_pr = [(c_w2, 32 * (qq + 1))]
                                            if nb == 0 and qq == 0:
                                                w_pr += [(c_oTe, 2 * tg + 2),
                                                         (c_x2, max(0, 2 * tg + rr - 1))]
                                            sch.add("t", f_pr, waits=w_pr,
                                                    incs=[(c_opr, 1)]
                                                    if (nb == 1 and qq == 1) else [])

                                    def f_x2(h, t2=t2, rr=rr):
                                        return h.tensor_add(
                                            x2sb[t2][:, rr * 1024:(rr + 1) * 1024],
                                            psAt[rr][:, :],
                                            x2sb[t2][:, rr * 1024:(rr + 1) * 1024])
                                    sch.add("v", f_x2,
                                            waits=[(c_opr, 2 * tg + rr + 1),
                                                   xld2r.wait_done(tg)],
                                            incs=[(c_x2, 1)])

                                def f_spx(h, tg=tg, t2=t2):
                                    return h.dma_start(
                                        x2s[tg * 128:(tg + 1) * 128, :],
                                        x2sb[t2][:, :])
                                sch.add("a", f_spx, waits=[(c_x2, 2 * tg + 2)],
                                        incs=[spx2.inc(tg)])

                                def f_cp(h, t2=t2):
                                    return h.activation(
                                        x2bf[t2][:, :], x2sb[t2][:, :], AF.Copy,
                                        accum_out=sxp2[:, t2:t2 + 1])
                                sch.add("a", f_cp,
                                        waits=[(c_x2, 2 * tg + 2),
                                               (c_xn2, max(0, tg - 1)),
                                               (c_st2, max(0, tg - 1))])

                                def f_sq(h, t2=t2):
                                    return h.activation(
                                        junk2[:, :], x2sb[t2][:, :], AF.Square,
                                        accum_out=sxxp2[:, t2:t2 + 1])
                                sch.add("a", f_sq, incs=[(c_sq2, 1)])

                                def f_mean(h, t2=t2):
                                    return h.tensor_scalar(
                                        meanp2[:, t2:t2 + 1], sxp2[:, t2:t2 + 1],
                                        1.0 / D, None, ALU.mult)
                                sch.add("v", f_mean,
                                        waits=[(c_sq2, tg + 1),
                                               (c_std2, max(0, tg - 1))],
                                        incs=[(c_mn2, 1)])

                                def f_msq(h, t2=t2):
                                    return h.activation(
                                        msqp2[:, t2:t2 + 1], meanp2[:, t2:t2 + 1],
                                        AF.Square)
                                sch.add("a", f_msq, waits=[(c_mn2, tg + 1)],
                                        incs=[(c_msq2, 1)])

                                def f_var(h, t2=t2):
                                    return h.scalar_tensor_tensor(
                                        mvarp2[:, t2:t2 + 1], sxxp2[:, t2:t2 + 1],
                                        1.0 / D, msqp2[:, t2:t2 + 1],
                                        ALU.mult, ALU.subtract)
                                sch.add("v", f_var, waits=[(c_msq2, tg + 1)],
                                        incs=[(c_st2, 1)])

                                def f_sqrt(h, t2=t2):
                                    h.activation(
                                        stdp2[:, t2:t2 + 1], mvarp2[:, t2:t2 + 1],
                                        AF.Ln, bias=eps_ap)
                                    return h.activation(
                                        rsp2[:, t2:t2 + 1], stdp2[:, t2:t2 + 1],
                                        AF.Exp, scale=-0.5)
                                sch.add("a", f_sqrt,
                                        waits=[(c_st2, tg + 1),
                                               (c_xn2, max(0, tg - 1))],
                                        incs=[(c_std2, 1)])

                                def f_xn(h, t2=t2):
                                    return h.tensor_scalar(
                                        xn2[t2][:, :], x2bf[t2][:, :],
                                        meanp2[:, t2:t2 + 1], rsp2[:, t2:t2 + 1],
                                        ALU.subtract, ALU.mult)
                                sch.add("v", f_xn,
                                        waits=[(c_std2, tg + 1),
                                               (c_T2b, max(0, 2 * (tg - 1)))],
                                        incs=[(c_xn2, 1)])

                                for b, pst2 in ((0, pstA), (1, pstB)):
                                    def f_T(h, t2=t2, b=b, pst2=pst2):
                                        last = None
                                        for j in range(8):
                                            cc0 = 8 * b + j
                                            last = h.transpose(
                                                pst2[:, j, :],
                                                xn2[t2][:, cc0 * 128:(cc0 + 1) * 128],
                                                ident_sb[:, :])
                                        return last
                                    sch.add("t", f_T,
                                            waits=[(c_xn2, tg + 1),
                                                   (c_oTe, 2 * tg + b + 1)],
                                            incs=[(c_T2b, 1)])

                                    def f_ev(h, b=b, pst2=pst2, tt=tt, hh=hh):
                                        return h.tensor_copy(
                                            xn2T[:, 8 * b:8 * b + 8,
                                                 tt * 128:(tt + 1) * 128],
                                            pst2[:, :, :])
                                    sch.add("v", f_ev,
                                            waits=[(c_T2b, 2 * tg + b + 1),
                                                   (c_gr, 64 * hh),
                                                   (c_ur, 64 * hh)],
                                            incs=[(c_xn2Te, 1)])
                            _emit(nc, sch)

                    if "F1" in phases:
                        with ExitStack() as ph:
                            sb = lambda n, shp, dt: ph.enter_context(
                                nc.sbuf_tensor(f"{n}_f1{hh}", shp, dt))
                            wg_sb = [sb(f"wg_sb{j}", [128, D], BF16) for j in range(2)]
                            wu_sb = [sb(f"wu_sb{j}", [128, D], BF16) for j in range(2)]
                            gact = [sb(f"gact{j}", [128, HH2], BF16) for j in range(2)]
                            usb = [sb(f"usb{j}", [128, HH2], BF16) for j in range(2)]
                            fuse = [sb(f"fuse{j}", [128, HH2], BF16) for j in range(2)]
                            psG = ph.enter_context(
                                nc.psum_tensor(f"psG_f1{hh}", [128, HH2], F32))
                            psU = ph.enter_context(
                                nc.psum_tensor(f"psU_f1{hh}", [128, HH2], F32))

                            sch = new_sched()
                            if first_af:
                                sch.add("g", lambda h: h.dma_start(sg_sb[:, :], sg2),
                                        incs=[(c_w2, 16)])
                                sch.add("g", lambda h: h.dma_start(su_sb[:, :], su2),
                                        incs=[(c_w2, 16)])
                                first_af = False
                            WF_N = c_w2.n

                            for ic in range(64):
                                ig = hh * 64 + ic
                                i2 = ic % 2

                                phase_gate = (
                                    [(c_xn2Te, XN2TE_HALF * (hh + 1))]
                                    + [(cc, 16 * (TPH // 2) * (hh + 1))
                                       for cc in spx2.c]
                                    if "A2" in phases else [])

                                def f_lg(h, ic=ic, i2=i2):
                                    return h.dma_start(wg_sb[i2][:, :], wgt[ic, :, :])
                                sch.add("g", f_lg,
                                        waits=[(c_gr, max(0, ig - 1))] + phase_gate,
                                        incs=[wgr.inc(ig)])

                                def f_lu(h, ic=ic, i2=i2):
                                    return h.dma_start(wu_sb[i2][:, :], wut[ic, :, :])
                                sch.add("g", f_lu, waits=[(c_ur, max(0, ig - 1))],
                                        incs=[wur.inc(ig)])

                                def f_g(h, i2=i2):
                                    last = None
                                    for nb in range(HH2 // 512):
                                        for c in range(16):
                                            last = h.matmul(
                                                psG[:, nb * 512:(nb + 1) * 512],
                                                wg_sb[i2][:, c * 128:(c + 1) * 128],
                                                xn2T[:, c, nb * 512:(nb + 1) * 512],
                                                start=(c == 0), stop=(c == 15))
                                    return last
                                sch.add("t", f_g,
                                        waits=[wgr.wait_done(ig),
                                               (c_xn2Te, XN2TE_HALF * (hh + 1)),
                                               (c_gact, ig)],
                                        incs=[(c_gr, 1)])

                                def f_u(h, i2=i2):
                                    last = None
                                    for nb in range(HH2 // 512):
                                        for c in range(16):
                                            last = h.matmul(
                                                psU[:, nb * 512:(nb + 1) * 512],
                                                wu_sb[i2][:, c * 128:(c + 1) * 128],
                                                xn2T[:, c, nb * 512:(nb + 1) * 512],
                                                start=(c == 0), stop=(c == 15))
                                    return last
                                sch.add("t", f_u,
                                        waits=[wur.wait_done(ig), (c_us, ig)],
                                        incs=[(c_ur, 1)])

                                def f_ga(h, ic=ic, i2=i2):
                                    return h.activation(gact[i2][:, :], psG[:, :],
                                                        AF.Silu,
                                                        scale=sg_sb[:, ic:ic + 1])
                                sch.add("a", f_ga,
                                        waits=[(c_gr, ig + 1), (c_w2, WF_N),
                                               (c_fu, max(0, ig - 1))],
                                        incs=[(c_gact, 1)])

                                def f_us(h, ic=ic, i2=i2):
                                    return h.tensor_scalar(usb[i2][:, :], psU[:, :],
                                                           su_sb[:, ic:ic + 1],
                                                           None, ALU.mult)
                                sch.add("v", f_us,
                                        waits=[(c_ur, ig + 1), (c_w2, WF_N),
                                               (c_fu, max(0, ig - 1))],
                                        incs=[(c_us, 1)])

                                def f_fu(h, i2=i2):
                                    return h.tensor_mul(fuse[i2][:, :],
                                                        gact[i2][:, :],
                                                        usb[i2][:, :])
                                sch.add("v", f_fu,
                                        waits=[(c_gact, ig + 1),
                                               spf.wait_done(ig - 2)],
                                        incs=[(c_fu, 1)])

                                def f_sp(h, ic=ic, i2=i2, hh=hh):
                                    ntl = HH2 // 128
                                    return h.dma_start(
                                        fs[:, hh * ntl:(hh + 1) * ntl, ic, :],
                                        fuse[i2][:, :].rearrange(
                                            "p (t s) -> p t s", s=128))
                                sch.add("s", f_sp, waits=[(c_fu, ig + 1)],
                                        incs=[spf.inc(ig)])
                            _emit(nc, sch)
        AF_DONE = spx2.totals() + spf.totals()

        if "F2" in phases:
            with ExitStack() as ph:
                sb = lambda n, shp, dt: ph.enter_context(nc.sbuf_tensor(n, shp, dt))
                wd_sb = sb("wd_sb", [128, 64, 1024], BF16)
                fuse_st = [sb(f"fuse_st{j}", [128, 64, 128], BF16) for j in range(3)]
                x2t = [sb(f"x2t{j}", [128, 1024], F32) for j in range(2)]
                ty = [sb(f"ty{j}", [128, 1024], F32) for j in range(2)]
                sd_sb = sb("sd_sb", [128, D], F32)
                psD = [ph.enter_context(nc.psum_tensor(f"psD{j}", [128, 1024], F32))
                       for j in range(2)]

                c_wd = ctr("c_wd")
                fldr = DmaRing(ctr, "fld", 3)
                x2ldr = DmaRing(ctr, "x2ld", 2)
                c_dr = ctr("c_dr")
                c_dev = ctr("c_dev")

                sch = new_sched()
                sch.add("g", lambda h: h.dma_start(sd_sb[:, :], sd_bc),
                        incs=[(c_wd, 16)])

                for eh in range(2):
                    for qq in range(4):
                        def f_wd(h, eh=eh, qq=qq):
                            return h.dma_start(
                                wd_sb[:, 16 * qq:16 * qq + 16, :],
                                wd.rearrange("(a p) e -> p a e", p=128)
                                [:, 16 * qq:16 * qq + 16,
                                 eh * 1024:(eh + 1) * 1024])
                        sch.add("g", f_wd,
                                waits=[(c_dr, NST * eh)] if qq == 0 else [],
                                incs=[(c_wd, 16)])

                    for st in range(NST):
                        ii = eh * NST + st
                        s2 = st % 2
                        sf3 = ii % 3

                        def f_fl(h, st=st, sf3=sf3):
                            return h.dma_start(fuse_st[sf3][:, :, :],
                                               fs[:, st, :, :])
                        sch.add("s", f_fl,
                                waits=AF_DONE + [(c_dr, max(0, ii - 2))],
                                incs=[fldr.inc(ii)])

                        def f_xl(h, st=st, eh=eh, s2=s2):
                            return h.dma_start(
                                x2t[s2][:, :],
                                x2s[st * 128:(st + 1) * 128,
                                    eh * 1024:(eh + 1) * 1024])
                        sch.add("a", f_xl,
                                waits=AF_DONE + [(c_dev, max(0, ii - 1))],
                                incs=[x2ldr.inc(ii)])

                        for nb in range(2):
                            for qq in range(4):
                                def f_mm(h, s2=s2, sf3=sf3, nb=nb, qq=qq):
                                    last = None
                                    for ic in range(16 * qq, 16 * qq + 16):
                                        last = h.matmul(
                                            psD[s2][:, nb * 512:(nb + 1) * 512],
                                            fuse_st[sf3][:, ic, :],
                                            wd_sb[:, ic, nb * 512:(nb + 1) * 512],
                                            start=(ic == 0), stop=(ic == 63))
                                    return last
                                w_mm = [(c_wd, 16 * (2 + 4 * eh + qq))]
                                if nb == 0 and qq == 0:
                                    w_mm += [fldr.wait_done(ii),
                                             (c_dev, max(0, ii - 1))]
                                sch.add("t", f_mm, waits=w_mm,
                                        incs=[(c_dr, 1)]
                                        if (nb == 1 and qq == 3) else [])

                        def f_ev(h, eh=eh, s2=s2):
                            h.tensor_mul(ty[s2][:, :], psD[s2][:, :],
                                         sd_sb[:, eh * 1024:(eh + 1) * 1024])
                            return h.tensor_add(ty[s2][:, :], ty[s2][:, :],
                                                x2t[s2][:, :])
                        sch.add("v", f_ev,
                                waits=[(c_dr, ii + 1), x2ldr.wait_done(ii),
                                       spy.wait_done(ii - 2)],
                                incs=[(c_dev, 1)])

                        def f_sp(h, st=st, eh=eh, s2=s2):
                            return h.dma_start(
                                y[st * 128:(st + 1) * 128,
                                  eh * 1024:(eh + 1) * 1024],
                                ty[s2][:, :])
                        sch.add("a", f_sp, waits=[(c_dev, ii + 1)],
                                incs=[spy.inc(ii)])

                sch.add("a", lambda h: h.nop(), waits=spy.totals())
                _emit(nc, sch)

        tail = Sched()
        tail_waits = (spkq.totals() + spv.totals() + spo.totals()
                      + spx2.totals() + spf.totals() + spy.totals())
        tail.add("a", lambda h: h.nop(), waits=tail_waits)
        tail.add("s", lambda h: h.nop(), waits=tail_waits)
        _emit(nc, tail)

    return nc



def _rope_tables():
    f = np.arange(64, dtype=np.float64)
    inv = 10000.0 ** (-f / 64.0)
    pos = np.arange(S, dtype=np.float64)
    ang = inv[:, None] * pos[None, :]
    cos = np.cos(ang)
    sin = np.sin(ang)
    return (np.concatenate([cos, cos], axis=0),
            np.concatenate([-sin, sin], axis=0))


def host_prepare(inputs):
    x = np.asarray(inputs["L0_attn_x"], dtype=np.float32)
    qw = np.asarray(inputs["L0_attn_q_w"], dtype=np.float32)
    qb = np.asarray(inputs["L0_attn_q_b"], dtype=np.float32)
    kvw = np.asarray(inputs["L0_attn_kv_w"], dtype=np.float32)
    kvb = np.asarray(inputs["L0_attn_kv_b"], dtype=np.float32)
    ow = np.asarray(inputs["L0_attn_o_w"], dtype=np.float32)
    wg = np.asarray(inputs["L0_ffn_w_gate"], dtype=np.int8)
    wu = np.asarray(inputs["L0_ffn_w_up"], dtype=np.int8)
    wdn = np.asarray(inputs["L0_ffn_w_down"], dtype=np.int8)
    sg = np.asarray(inputs["L0_ffn_s_gate"], dtype=np.float32)
    su = np.asarray(inputs["L0_ffn_s_up"], dtype=np.float32)
    sd = np.asarray(inputs["L0_ffn_s_down"], dtype=np.float32)

    cosT64, sinT64 = _rope_tables()

    shared = dict(
        qwT=np.ascontiguousarray(qw.T),
        kvwT=np.ascontiguousarray(kvw.T),
        owT=np.ascontiguousarray(ow.T),
        wgt=np.ascontiguousarray(
            wg.reshape(16, 128, 64, 128).transpose(2, 1, 0, 3)).reshape(64, 128, D),
        wut=np.ascontiguousarray(
            wu.reshape(16, 128, 64, 128).transpose(2, 1, 0, 3)).reshape(64, 128, D),
        wd=wdn,
        qb2=np.ascontiguousarray(qb.reshape(16, 128).T),
        kvb2=np.ascontiguousarray(kvb.reshape(32, 128).T),
        sg2=np.ascontiguousarray(sg.reshape(64, 128).T),
        su2=np.ascontiguousarray(su.reshape(64, 128).T),
        vb_bc=np.ascontiguousarray(np.broadcast_to(kvb[D:], (128, D))),
        sd_bc=np.ascontiguousarray(np.broadcast_to(sd, (128, D))),
        identt=np.eye(128, dtype=ml_dtypes.bfloat16),
        constt=np.repeat(np.array([[1e-5, 0.0, 0.0, 0.0]], np.float32), 128, 0),
    )

    def _mk_tab(c64, s64):
        t = np.stack([c64.reshape(128, S // CH, CH),
                      s64.reshape(128, S // CH, CH)], axis=2)
        return np.ascontiguousarray(t).astype(ml_dtypes.bfloat16)

    in_maps = []
    for c in range(8):
        b, h = c // 2, c % 2
        m = dict(shared)
        if h == 0:
            m["x_all"] = np.ascontiguousarray(x[b])
            m["costab"] = _mk_tab(cosT64, sinT64)
        else:
            perm = np.concatenate([np.arange(HS, S), np.arange(0, HS)])
            m["x_all"] = np.ascontiguousarray(x[b][perm])
            m["costab"] = _mk_tab(np.ascontiguousarray(cosT64[:, perm]),
                                  np.ascontiguousarray(sinT64[:, perm]))
        in_maps.append(m)
    return in_maps


def kernel(**inputs):
    in_maps = host_prepare(inputs)
    nc = build_program()
    res = run_bass_kernel_spmd(nc, in_maps, core_ids=list(range(8)))
    out = np.empty((4, S, D), dtype=np.float32)
    for c in range(8):
        b, h = c // 2, c % 2
        out[b, h * HS:(h + 1) * HS, :] = np.asarray(res.results[c]["y"],
                                                    dtype=np.float32)
    return out


# version 12
# speedup vs baseline: 1.0553x; 1.0553x over previous
import numpy as np
import ml_dtypes
from contextlib import ExitStack

import concourse.bass as bass
import concourse.mybir as mybir
from concourse.bass_utils import run_bass_kernel_spmd

F32 = mybir.dt.float32
BF16 = mybir.dt.bfloat16
I8 = mybir.dt.int8
AF = mybir.ActivationFunctionType
ALU = mybir.AluOpType
AX = mybir.AxisListType.X

D = 2048
S = 8192
HS = 4096
INTER = 8192
NT = S // 128
CH = 512
NCH = S // CH
TPC = CH // 128
NQCH = HS // CH
SCALE = float(128 ** -0.5)
NG = S // 1024
NSB = HS // 512
IPB = NG * 4
TPH = HS // 256
HH2 = HS // 2
NST = HS // 128
BIG = 1.0e30
DBG_NOFLASH = False
DBG_UNNORM = False


class Ctr:
    def __init__(self, nc, ctx, name):
        self.sem = ctx.enter_context(nc.semaphore(name))
        self.n = 0


class DmaRing:

    def __init__(self, ctr, name, depth):
        self.c = [ctr(f"{name}{j}") for j in range(depth)]
        self.depth = depth

    def inc(self, i):
        return (self.c[i % self.depth], 16)

    def wait_done(self, i):
        if i < 0:
            return None
        return (self.c[i % self.depth], 16 * (i // self.depth + 1))

    def totals(self):
        return [(c, c.n) for c in self.c if c.n > 0]


class Sched:
    def __init__(self):
        self.q = {e: [] for e in "tvasg"}

    def add(self, eng, fn, waits=(), incs=()):
        w = [cv for cv in waits if cv is not None and cv[1] > 0]
        for c, k in incs:
            c.n += k
        self.q[eng].append((fn, w, list(incs)))


def _emit(nc, sched):
    with nc.Block() as blk:
        def runner(name):
            def run(h):
                for fn, waits, incs in sched.q[name]:
                    for c, v in waits:
                        h.wait_ge(c.sem, v)
                    last = fn(h)
                    for c, k in incs:
                        last.then_inc(c.sem, k)
            return run
        blk.tensor(runner("t"))
        blk.vector(runner("v"))
        blk.scalar(runner("a"))
        blk.sync(runner("s"))
        blk.gpsimd(runner("g"))


def build_program(phases=("P0", "A", "A2", "F1", "F2"), expose=None):
    expose = expose or {}
    nc = bass.Bass("TRN2", target_bir_lowering=False)

    def din(name, shape, dtype):
        return nc.dram_tensor(name, shape, dtype, kind="ExternalInput").ap()

    def dscr(name, shape, dtype):
        if name in expose:
            kind = "ExternalInput" if expose[name] == "in" else "ExternalOutput"
            return nc.dram_tensor(name, shape, dtype, kind=kind).ap()
        return nc.dram_tensor(name, shape, dtype).ap()

    x_all = din("x_all", [S, D], F32)
    qwT = din("qwT", [D, D], F32)
    kvwT = din("kvwT", [D, 2 * D], F32)
    owT = din("owT", [D, D], F32)
    wgt = din("wgt", [64, 128, D], I8)
    wut = din("wut", [64, 128, D], I8)
    wd = din("wd", [INTER, D], I8)
    qb2 = din("qb2", [128, 16], F32)
    kvb2 = din("kvb2", [128, 32], F32)
    sg2 = din("sg2", [128, 64], F32)
    su2 = din("su2", [128, 64], F32)
    vb_bc = din("vb_bc", [128, D], F32)
    sd_bc = din("sd_bc", [128, D], F32)
    costab = din("costab", [128, S // CH, 2, CH], BF16)
    identt = din("identt", [128, 128], BF16)
    constt = din("constt", [128, 4], F32)

    kTs = dscr("kTs", [128, 16, S], BF16)
    qTs = dscr("qTs", [128, 16, HS], BF16)
    vs = dscr("vs", [S, D], BF16)
    ons = dscr("ons", [HS, D], BF16)
    x2s = dscr("x2s", [HS, D], F32)
    fs = dscr("fs", [128, NST, 64, 128], BF16)
    y = nc.dram_tensor("y", [HS, D], F32, kind="ExternalOutput").ap()

    with ExitStack() as top:
        ident_sb = top.enter_context(nc.sbuf_tensor("ident_sb", [128, 128], BF16))
        const_sb = top.enter_context(nc.sbuf_tensor("const_sb", [128, 4], F32))
        eps_ap = const_sb[:, 0:1]
        ctr = lambda name: Ctr(nc, top, name)

        c_base = ctr("c_base")
        spkq = DmaRing(ctr, "spkq", 4)
        spv = DmaRing(ctr, "spv", 4)
        xldr = DmaRing(ctr, "xld", 4)
        spo = DmaRing(ctr, "spo", 2)
        spx2 = DmaRing(ctr, "spx2", 2)
        spf = DmaRing(ctr, "spf", 2)
        spy = DmaRing(ctr, "spy", 2)

        base = Sched()
        base.add("g", lambda h: h.dma_start(ident_sb[:, :], identt), incs=[(c_base, 16)])
        base.add("g", lambda h: h.dma_start(const_sb[:, :], constt), incs=[(c_base, 16)])
        BASE_N = c_base.n
        _first = [base]

        def new_sched():
            if _first:
                return _first.pop()
            return Sched()

        def emit_p0_pass(kind):
            with ExitStack() as ph:
                sb = lambda n, shp, dt: ph.enter_context(
                    nc.sbuf_tensor(n + "_" + kind, shp, dt))
                if kind == "kq":
                    qw_sb = sb("qw_sb", [128, 16, D], BF16)
                    wk_sb = sb("wk_sb", [128, 16, D], BF16)
                    qb_sb = sb("qb_sb", [128, 16], F32)
                    kvb_sb = sb("kvb_sb", [128, 32], F32)
                    tb_sl = [sb(f"tb{j}", [128, 2, CH], BF16) for j in range(2)]
                    kq_sb = [sb(f"kq{j}", [128, CH], BF16) for j in range(4)]
                    swp = sb("swp", [128, CH], BF16)
                else:
                    wv_sb = sb("wv_sb", [128, 16, D], BF16)
                    vb_sb = sb("vb_sb", [128, D], F32)
                    v_sb = [sb(f"vsb{j}", [128, CH], BF16) for j in range(4)]
                xnT = [sb(f"xnT{p}", [128, 16, CH], BF16) for p in range(2)]
                xbf = [sb(f"xbf{j}", [128, D], BF16) for j in range(4)]
                junk = sb("junk", [128, D], BF16)
                xn = [sb(f"xn{j}", [128, D], BF16) for j in range(4)]
                sxp = sb("sxp", [128, 2], F32)
                sxxp = sb("sxxp", [128, 2], F32)
                meanp = sb("meanp", [128, 2], F32)
                msqp = sb("msqp", [128, 2], F32)
                mvarp = sb("mvarp", [128, 2], F32)
                stdp = sb("stdp", [128, 2], F32)
                rsp = sb("rsp", [128, 2], F32)
                pst = [ph.enter_context(
                    nc.psum_tensor(f"pst{j}_{kind}", [128, 8, 128], BF16))
                    for j in range(2)]
                psr = [ph.enter_context(
                    nc.psum_tensor(f"psr{j}_{kind}", [128, CH], F32))
                    for j in range(4)]

                c_w = ctr("c_w_" + kind)
                xoff = 0 if kind == "kq" else NT
                c_sq = ctr("c_sq_" + kind)
                c_mn = ctr("c_mn_" + kind)
                c_msq = ctr("c_msq_" + kind)
                c_st1 = ctr("c_st1_" + kind)
                c_std = ctr("c_std_" + kind)
                c_xn = ctr("c_xn_" + kind)
                c_T0 = ctr("c_T0_" + kind)
                c_xnTe = ctr("c_xnTe_" + kind)
                c_rnd = ctr("c_rnd_" + kind)
                c_rev = ctr("c_rev_" + kind)
                if kind == "kq":
                    c_rope = ctr("c_rope_" + kind)
                    tbr = DmaRing(ctr, "tbr_" + kind, 2)

                sch = new_sched()
                gate = spkq.totals() if kind == "v" else []

                kv_view = kvwT.rearrange("(c p) j -> p c j", p=128)
                wload_q = []
                if kind == "kq":
                    qview = qwT.rearrange("(c p) j -> p c j", p=128)
                    for pc in range(4):
                        wload_q.append(lambda h, pc=pc: h.dma_start(
                            wk_sb[:, pc * 4:(pc + 1) * 4, :],
                            kv_view[:, pc * 4:(pc + 1) * 4, 0:D]))
                    wload_q.append(lambda h: h.dma_start(kvb_sb[:, :], kvb2))
                    wload_q.append(lambda h: h.dma_start(qb_sb[:, :], qb2))
                    for pc in range(4):
                        wload_q.append(lambda h, pc=pc: h.dma_start(
                            qw_sb[:, pc * 4:(pc + 1) * 4, :],
                            qview[:, pc * 4:(pc + 1) * 4, :]))
                else:
                    for pc in range(4):
                        wload_q.append(lambda h, pc=pc: h.dma_start(
                            wv_sb[:, pc * 4:(pc + 1) * 4, :],
                            kv_view[:, pc * 4:(pc + 1) * 4, D:2 * D]))
                    wload_q.append(lambda h: h.dma_start(vb_sb[:, :], vb_bc))

                def pump_wload():
                    if wload_q:
                        fn = wload_q.pop(0)
                        sch.add("g", fn, waits=gate if not wl_started else [],
                                incs=[(c_w, 16)])
                        wl_started.append(1)
                wl_started = []
                W_N = 16 * len(wload_q)
                WK_N = 16 * 6 if kind == "kq" else W_N

                if kind == "kq":
                    rounds_in = lambda ch: 16 + (16 if ch < NQCH else 0)
                    ropes_in = lambda ch: 16 + (16 if ch < NQCH else 0)
                else:
                    rounds_in = lambda ch: 16
                    ropes_in = lambda ch: 0
                rounds_before = [0]
                ropes_thru = [0]
                for ch in range(NCH):
                    rounds_before.append(rounds_before[-1] + rounds_in(ch))
                    ropes_thru.append(ropes_thru[-1] + ropes_in(ch))

                def ln_tile(i):
                    s3, s2 = i % 4, i % 2

                    def f_load(h, i=i, s3=s3):
                        return h.dma_start(xbf[s3][:, :],
                                           x_all[i * 128:(i + 1) * 128, :])
                    sch.add("g", f_load,
                            waits=[(c_xn, max(0, i - 3)), (c_base, BASE_N)] + gate,
                            incs=[xldr.inc(xoff + i)])
                    pump_wload()

                    def f_cp(h, s3=s3, s2=s2):
                        return h.activation(junk[:, :], xbf[s3][:, :], AF.Copy,
                                            accum_out=sxp[:, s2:s2 + 1])
                    sch.add("a", f_cp,
                            waits=[xldr.wait_done(xoff + i), (c_st1, max(0, i - 1))])

                    def f_sq(h, s3=s3, s2=s2):
                        return h.activation(junk[:, :], xbf[s3][:, :], AF.Square,
                                            accum_out=sxxp[:, s2:s2 + 1])
                    sch.add("a", f_sq, incs=[(c_sq, 1)])

                    def f_mean(h, s2=s2):
                        return h.tensor_scalar(meanp[:, s2:s2 + 1],
                                               sxp[:, s2:s2 + 1],
                                               1.0 / D, None, ALU.mult)
                    sch.add("v", f_mean,
                            waits=[(c_sq, i + 1), (c_std, max(0, i - 1))],
                            incs=[(c_mn, 1)])

                    def f_msq(h, s2=s2):
                        return h.activation(msqp[:, s2:s2 + 1],
                                            meanp[:, s2:s2 + 1], AF.Square)
                    sch.add("a", f_msq, waits=[(c_mn, i + 1)], incs=[(c_msq, 1)])

                    def f_var(h, s2=s2):
                        return h.scalar_tensor_tensor(
                            mvarp[:, s2:s2 + 1], sxxp[:, s2:s2 + 1], 1.0 / D,
                            msqp[:, s2:s2 + 1], ALU.mult, ALU.subtract)
                    sch.add("v", f_var, waits=[(c_msq, i + 1)], incs=[(c_st1, 1)])

                    def f_sqrt(h, s2=s2):
                        h.activation(stdp[:, s2:s2 + 1], mvarp[:, s2:s2 + 1],
                                     AF.Ln, bias=eps_ap)
                        return h.activation(rsp[:, s2:s2 + 1], stdp[:, s2:s2 + 1],
                                            AF.Exp, scale=-0.5)
                    sch.add("a", f_sqrt,
                            waits=[(c_st1, i + 1), (c_xn, max(0, i - 1))],
                            incs=[(c_std, 1)])

                    def f_xn(h, s3=s3, s2=s2):
                        return h.tensor_scalar(xn[s3][:, :], xbf[s3][:, :],
                                               meanp[:, s2:s2 + 1],
                                               rsp[:, s2:s2 + 1],
                                               ALU.subtract, ALU.mult)
                    sch.add("v", f_xn,
                            waits=[(c_std, i + 1), (c_T0, max(0, 2 * (i - 4) + 2))],
                            incs=[(c_xn, 1)])

                def t_batch(i, b):
                    ch = i // TPC
                    par = ch % 2
                    st = i % TPC

                    def f_T(h, i=i, b=b):
                        last = None
                        for j in range(8):
                            cc0 = 8 * b + j
                            last = h.transpose(pst[b][:, j, :],
                                               xn[i % 4][:, cc0 * 128:(cc0 + 1) * 128],
                                               ident_sb[:, :])
                        return last
                    w = [(c_xn, i + 1), (c_xnTe, max(0, 2 * (i - 1) + b + 1))]
                    if ch >= 2:
                        w.append((c_rnd, rounds_before[ch - 1]))
                    sch.add("t", f_T, waits=w, incs=[(c_T0, 1)])

                    def f_ev(h, par=par, b=b, st=st):
                        return h.tensor_copy(
                            xnT[par][:, 8 * b:8 * b + 8, st * 128:(st + 1) * 128],
                            pst[b][:, :, :])
                    sch.add("v", f_ev, waits=[(c_T0, 2 * i + b + 1)],
                            incs=[(c_xnTe, 1)])

                def tbl_load(ch):
                    sl = ch % 2
                    w = [(c_rope, ropes_thru[ch - 1])] if ch >= 2 else []

                    def f_c(h, ch=ch, sl=sl):
                        return h.dma_start(tb_sl[sl][:, :, :], costab[:, ch, :, :])
                    sch.add("a", f_c, waits=w, incs=[tbr.inc(ch)])

                state = {"r": 0, "skq": 0, "sv": 0, "rp": 0}

                def do_round(knd, ch, a1, a2=0):
                    par = ch % 2
                    rr = state["r"]
                    pp = rr % 4

                    if knd in ("k", "q"):
                        oc = a1
                        wsb = wk_sb if knd == "k" else qw_sb

                        def f_mm(h, wsb=wsb, oc=oc, par=par, pp=pp):
                            last = None
                            for c in range(16):
                                last = h.matmul(psr[pp][:, :],
                                                wsb[:, c, oc * 128:(oc + 1) * 128],
                                                xnT[par][:, c, :],
                                                start=(c == 0), stop=(c == 15))
                            return last
                    else:
                        st4, eq = a1, a2

                        def f_mm(h, st4=st4, eq=eq, par=par, pp=pp):
                            last = None
                            for c in range(16):
                                last = h.matmul(psr[pp][:, :],
                                                xnT[par][:, c, st4 * 128:(st4 + 1) * 128],
                                                wv_sb[:, c, eq * CH:(eq + 1) * CH],
                                                start=(c == 0), stop=(c == 15))
                            return last
                    wneed = WK_N if knd in ("k", "v") else W_N
                    sch.add("t", f_mm,
                            waits=[(c_xnTe, 8 * (ch + 1)), (c_w, wneed),
                                   (c_rev, max(0, rr - 3))],
                            incs=[(c_rnd, 1)])

                    if knd in ("k", "q"):
                        skq = state["skq"]
                        es = skq % 4
                        bias = (kvb_sb[:, a1:a1 + 1] if knd == "k"
                                else qb_sb[:, a1:a1 + 1])

                        def f_ev(h, es=es, pp=pp, bias=bias):
                            return h.activation(kq_sb[es][:, :], psr[pp][:, :],
                                                AF.Identity, bias=bias)
                        sch.add("a", f_ev,
                                waits=[(c_rnd, rr + 1), spkq.wait_done(skq - 4)],
                                incs=[(c_rev, 1)])

                        sl = ch % 2

                        def f_rope(h, es=es, sl=sl):
                            h.tensor_copy(swp[0:64, :], kq_sb[es][64:128, :])
                            h.tensor_copy(swp[64:128, :], kq_sb[es][0:64, :])
                            h.tensor_mul(swp[:, :], swp[:, :], tb_sl[sl][:, 1, :])
                            h.tensor_mul(kq_sb[es][:, :], kq_sb[es][:, :],
                                         tb_sl[sl][:, 0, :])
                            return h.tensor_add(kq_sb[es][:, :], kq_sb[es][:, :],
                                                swp[:, :])
                        sch.add("v", f_rope,
                                waits=[(c_rev, rr + 1), tbr.wait_done(ch)],
                                incs=[(c_rope, 1)])

                        dst = (kTs[:, a1, ch * CH:(ch + 1) * CH] if knd == "k"
                               else qTs[:, a1, ch * CH:(ch + 1) * CH])

                        def f_sp(h, es=es, dst=dst):
                            return h.dma_start(dst, kq_sb[es][:, :])
                        sch.add("s", f_sp, waits=[(c_rope, state["rp"] + 1)],
                                incs=[spkq.inc(skq)])
                        state["skq"] += 1
                        state["rp"] += 1
                    else:
                        st4, eq = a1, a2
                        sv = state["sv"]
                        es = sv % 4

                        def f_ev(h, es=es, pp=pp, eq=eq):
                            return h.tensor_add(v_sb[es][:, :], psr[pp][:, :],
                                                vb_sb[:, eq * CH:(eq + 1) * CH])
                        sch.add("v", f_ev,
                                waits=[(c_rnd, rr + 1), spv.wait_done(sv - 4)],
                                incs=[(c_rev, 1)])

                        row0 = (ch * TPC + st4) * 128

                        def f_sp(h, es=es, row0=row0, eq=eq):
                            return h.dma_start(
                                vs[row0:row0 + 128, eq * CH:(eq + 1) * CH],
                                v_sb[es][:, :])
                        sch.add("s", f_sp, waits=[(c_rev, rr + 1)],
                                incs=[spv.inc(sv)])
                        state["sv"] += 1
                    state["r"] += 1

                for ch in range(NCH + 1):
                    if ch >= 1:
                        while wload_q:
                            pump_wload()
                    if ch < NCH and kind == "kq":
                        tbl_load(ch)
                    if ch == 0:
                        for st in range(TPC):
                            ln_tile(st)
                        for st in range(TPC):
                            for b in range(2):
                                t_batch(st, b)
                    if ch >= 1:
                        cur = ch - 1
                        pend = ([(ch * TPC + st, b) for st in range(TPC)
                                 for b in range(2)] if ch < NCH else [])
                        if kind == "kq":
                            rlist = ([("k", cur, oc) for oc in range(16)]
                                     + ([("q", cur, oc) for oc in range(16)]
                                        if cur < NQCH else []))
                        else:
                            rlist = [("v", cur, st4, eq)
                                     for st4 in range(TPC) for eq in range(TPC)]
                        step = max(1, len(rlist) // 16)
                        for rn, r in enumerate(rlist):
                            do_round(r[0], r[1], r[2], r[3] if len(r) > 3 else 0)
                            if rn % step == step - 1 and pend:
                                i, b = pend.pop(0)
                                if b == 0:
                                    ln_tile(i)
                                t_batch(i, b)
                        while pend:
                            i, b = pend.pop(0)
                            if b == 0:
                                ln_tile(i)
                            t_batch(i, b)

                _emit(nc, sch)

        if "P0" in phases:
            emit_p0_pass("kq")
            emit_p0_pass("v")
        P0_DONE = spkq.totals() + spv.totals()

        if "A" in phases:
            with ExitStack() as ph:
                sb = lambda n, shp, dt: ph.enter_context(nc.sbuf_tensor(n, shp, dt))
                qT_sb = sb("qT_sb", [128, 16, 512], BF16)
                kT_b = [sb(f"kT_b{j}", [128, 16, 1024], BF16) for j in range(2)]
                v_b = [sb(f"v_b{j}", [128, 8, D], BF16) for j in range(2)]
                o_acc = sb("o_acc", [128, 4, D], F32)
                p_sb = [sb(f"p_sb{j}", [128, 1024], BF16) for j in range(2)]
                pT_sb = [sb(f"pT_sb{j}", [128, 8, 128], BF16) for j in range(2)]
                nm = sb("nm", [128, 4, 2], F32)
                pm = sb("pm", [128, 4, 2], F32)
                lsum = sb("lsum", [128, 4], F32)
                tmaxb = sb("tmaxb", [128, 1], F32)
                ccb = sb("ccb", [128, 2], F32)
                lpart = sb("lpart", [128, 2], F32)
                rlb = sb("rlb", [128, 2], F32)
                rlt = sb("rlt", [128, 1], F32)
                spcr = sb("spcr", [128, 8], F32)
                obf = [sb(f"obf{j}", [128, D], BF16) for j in range(2)]
                psS = ph.enter_context(nc.psum_tensor("psS", [128, 1024], F32))
                psT = ph.enter_context(nc.psum_tensor("psT", [128, 8, 128], BF16))
                psO = ph.enter_context(nc.psum_tensor("psO", [128, D], F32))

                qldr = DmaRing(ctr, "qld", 4)
                kldr = DmaRing(ctr, "kld", 2)
                vldr = DmaRing(ctr, "vld", 2)
                c_sc = ctr("c_sc")
                c_nm = ctr("c_nm")
                c_cc = ctr("c_cc")
                c_exp = ctr("c_exp")
                c_lup = ctr("c_lup")
                c_T = ctr("c_T")
                c_pTe = ctr("c_pTe")
                c_pv = ctr("c_pv")
                c_ofl = ctr("c_ofl")
                c_obf = ctr("c_obf")
                c_rm = ctr("c_rm")
                c_rl = ctr("c_rl")
                c_ms = ctr("c_ms")

                sch = new_sched()

                def f_init(h):
                    h.memset(o_acc[:, :, :], 0.0)
                    h.memset(lsum[:, :], 0.0)
                    h.memset(pm[:, :, :], -BIG)
                    return h.memset(nm[:, :, :], BIG)
                sch.add("v", f_init, incs=[(c_ms, 1)])

                def sc_round(sbk, idx):
                    gidx = sbk * IPB + idx
                    g = idx // 4
                    gg = sbk * NG + g
                    gp = g % 2
                    st = idx % 4
                    g2 = idx % 2

                    def f(h, gp=gp, st=st):
                        last = None
                        for nb in range(2):
                            for c in range(16):
                                last = h.matmul(
                                    psS[:, nb * 512:(nb + 1) * 512],
                                    qT_sb[:, c, st * 128:(st + 1) * 128],
                                    kT_b[gp][:, c, nb * 512:(nb + 1) * 512],
                                    start=(c == 0), stop=(c == 15))
                        return last
                    sch.add("t", f,
                            waits=[qldr.wait_done(sbk * 4 + st), kldr.wait_done(gg),
                                   (c_exp, gidx)],
                            incs=[(c_sc, 1)])

                    def f_rm(h):
                        return h.reduce_max(tmaxb[:, :], psS[:, :], axis=AX)
                    if not DBG_NOFLASH:
                        sch.add("v", f_rm, waits=[(c_sc, gidx + 1)],
                                incs=[(c_rm, 1)])

                    def f_cc(h, st=st, gp=gp, g2=g2):
                        return h.activation(ccb[:, g2:g2 + 1],
                                            pm[:, st, 1 - gp:2 - gp], AF.Exp,
                                            bias=nm[:, st, gp:gp + 1])
                    if not DBG_NOFLASH:
                        sch.add("a", f_cc,
                                waits=[(c_nm, gidx + 1), (c_ofl, max(0, gidx - 1))],
                                incs=[(c_cc, 1)])

                    if DBG_NOFLASH:
                        def f_exp(h, st=st, gp=gp, g2=g2):
                            return h.activation(p_sb[g2][:, :], psS[:, :], AF.Exp,
                                                scale=SCALE,
                                                accum_out=lpart[:, g2:g2 + 1])
                        sch.add("a", f_exp,
                                waits=[(c_sc, gidx + 1),
                                       (c_T, max(0, gidx - 1)),
                                       (c_lup, max(0, gidx - 1))],
                                incs=[(c_exp, 1)])
                    else:
                        def f_exp(h, st=st, gp=gp, g2=g2):
                            return h.activation(p_sb[g2][:, :], psS[:, :], AF.Exp,
                                                bias=nm[:, st, gp:gp + 1], scale=SCALE,
                                                accum_out=lpart[:, g2:g2 + 1])
                        sch.add("a", f_exp,
                                waits=[(c_T, max(0, gidx - 1)),
                                       (c_lup, max(0, gidx - 1))],
                                incs=[(c_exp, 1)])

                def nm_update(sbk, idx):
                    if DBG_NOFLASH:
                        return
                    gidx = sbk * IPB + idx
                    g = idx // 4
                    gp = g % 2
                    st = idx % 4
                    g2 = idx % 2

                    def f_nmu(h, st=st, gp=gp):
                        h.tensor_scalar(nm[:, st, gp:gp + 1], tmaxb[:, :],
                                        -SCALE, nm[:, st, 1 - gp:2 - gp],
                                        ALU.mult, ALU.min)
                        return h.tensor_scalar(pm[:, st, gp:gp + 1], tmaxb[:, :],
                                               SCALE, pm[:, st, 1 - gp:2 - gp],
                                               ALU.mult, ALU.max)
                    sch.add("v", f_nmu,
                            waits=[(c_rm, gidx + 1), (c_ms, c_ms.n),
                                   (c_cc, max(0, gidx - 1))],
                            incs=[(c_nm, 1)])

                def spacer3(h):
                    h.memset(spcr[:, :], 0.0)
                    h.memset(spcr[:, :], 1.0)
                    return h.memset(spcr[:, :], 2.0)

                def t_pv_pTe(sbk, idx):
                    gidx = sbk * IPB + idx
                    g2 = idx % 2

                    def f_T(h, g2=g2):
                        last = None
                        for tc in range(8):
                            last = h.transpose(psT[:, tc, :],
                                               p_sb[g2][:, tc * 128:(tc + 1) * 128],
                                               ident_sb[:, :])
                        return last
                    sch.add("t", f_T, waits=[(c_exp, gidx + 1), (c_pTe, gidx)],
                            incs=[(c_T, 1)])

                    def f_pTe(h, g2=g2):
                        return h.tensor_copy(pT_sb[g2][:, :, :], psT[:, :, :])
                    sch.add("v", f_pTe,
                            waits=[(c_T, gidx + 1), (c_pv, max(0, gidx - 1))],
                            incs=[(c_pTe, 1)])

                def t_pv_a(sbk, idx):
                    gidx = sbk * IPB + idx
                    g2 = idx % 2
                    st = idx % 4

                    cscal = 1.0 if DBG_NOFLASH else ccb[:, g2:g2 + 1]

                    def f_lup(h, st=st, g2=g2, cscal=cscal):
                        return h.scalar_tensor_tensor(
                            lsum[:, st:st + 1], lsum[:, st:st + 1],
                            cscal, lpart[:, g2:g2 + 1],
                            ALU.mult, ALU.add)
                    lw = [(c_exp, gidx + 1)]
                    if not DBG_NOFLASH:
                        lw.append((c_cc, gidx + 1))
                    sch.add("v", f_lup, waits=lw, incs=[(c_lup, 1)])

                def t_pv_b(sbk, idx):
                    gidx = sbk * IPB + idx
                    g = idx // 4
                    gg = sbk * NG + g
                    gp = g % 2
                    st = idx % 4
                    g2 = idx % 2

                    def f_pv(h, g2=g2, gp=gp):
                        last = None
                        for nb in range(4):
                            for tc in range(8):
                                last = h.matmul(
                                    psO[:, nb * 512:(nb + 1) * 512],
                                    pT_sb[g2][:, tc, :],
                                    v_b[gp][:, tc, nb * 512:(nb + 1) * 512],
                                    start=(tc == 0), stop=(tc == 7))
                        return last
                    sch.add("t", f_pv,
                            waits=[(c_pTe, gidx + 1), vldr.wait_done(gg),
                                   (c_ofl, gidx)],
                            incs=[(c_pv, 1)])

                    cscal2 = 1.0 if DBG_NOFLASH else ccb[:, g2:g2 + 1]

                    def f_ofl(h, st=st, g2=g2, cscal2=cscal2):
                        return h.scalar_tensor_tensor(
                            o_acc[:, st, :], o_acc[:, st, :],
                            cscal2, psO[:, :], ALU.mult, ALU.add)
                    sch.add("v", f_ofl, waits=[(c_pv, gidx + 1)],
                            incs=[(c_ofl, 1)])

                for sbk in range(NSB):
                    for g in range(2):
                        gg = sbk * NG + g
                        gp = g % 2

                        def f_k(h, g=g, gp=gp):
                            return h.dma_start(kT_b[gp][:, :, :],
                                               kTs[:, :, g * 1024:(g + 1) * 1024])
                        sch.add("s", f_k,
                                waits=P0_DONE + [(c_sc, max(0, 4 * (gg - 1)))],
                                incs=[kldr.inc(gg)])

                        def f_v(h, g=g, gp=gp):
                            return h.dma_start(
                                v_b[gp][:, :, :],
                                vs[g * 1024:(g + 1) * 1024, :]
                                .rearrange("(tc p) e -> p tc e", p=128))
                        sch.add("g", f_v,
                                waits=P0_DONE + [(c_pv, max(0, 4 * (gg - 1)))],
                                incs=[vldr.inc(gg)])

                    for qp in range(4):
                        def f_q(h, sbk=sbk, qp=qp):
                            return h.dma_start(
                                qT_sb[:, :, qp * 128:(qp + 1) * 128],
                                qTs[:, :, sbk * 512 + qp * 128:
                                    sbk * 512 + (qp + 1) * 128])
                        w = [(c_sc, max(0, (sbk - 1) * IPB + IPB - 4 + qp + 1))]
                        sch.add("a", f_q, waits=P0_DONE + w,
                                incs=[qldr.inc(sbk * 4 + qp)])

                    if sbk > 0:
                        def f_ms(h):
                            if DBG_NOFLASH:
                                h.memset(o_acc[:, :, :], 0.0)
                                h.memset(lsum[:, :], 0.0)
                            h.memset(pm[:, :, :], -BIG)
                            return h.memset(nm[:, :, :], BIG)
                        msw = [(c_exp, IPB * sbk)]
                        if DBG_NOFLASH:
                            msw.append((c_obf, 4 * sbk))
                        sch.add("v", f_ms, waits=msw, incs=[(c_ms, 1)])

                    for g in range(2, NG):
                        gg = sbk * NG + g
                        gp = g % 2

                        def f_k(h, g=g, gp=gp):
                            return h.dma_start(kT_b[gp][:, :, :],
                                               kTs[:, :, g * 1024:(g + 1) * 1024])
                        sch.add("s", f_k,
                                waits=P0_DONE + [(c_sc, max(0, 4 * (gg - 1)))],
                                incs=[kldr.inc(gg)])

                        def f_v(h, g=g, gp=gp):
                            return h.dma_start(
                                v_b[gp][:, :, :],
                                vs[g * 1024:(g + 1) * 1024, :]
                                .rearrange("(tc p) e -> p tc e", p=128))
                        sch.add("g", f_v,
                                waits=P0_DONE + [(c_pv, max(0, 4 * (gg - 1)))],
                                incs=[vldr.inc(gg)])

                    sc_round(sbk, 0)
                    if not DBG_NOFLASH:
                        sch.add("v", spacer3)
                    nm_update(sbk, 0)
                    for idx in range(1, IPB):
                        t_pv_pTe(sbk, idx - 1)
                        sc_round(sbk, idx)
                        t_pv_a(sbk, idx - 1)
                        if not DBG_NOFLASH:
                            sch.add("v", lambda h: h.memset(spcr[:, :], 0.0))
                        nm_update(sbk, idx)
                        t_pv_b(sbk, idx - 1)
                    t_pv_pTe(sbk, IPB - 1)
                    t_pv_a(sbk, IPB - 1)
                    t_pv_b(sbk, IPB - 1)

                    for st in range(4):
                        oi = sbk * 4 + st
                        os2 = oi % 2

                        def f_rl(h, st=st, os2=os2):
                            h.activation(rlt[:, :], lsum[:, st:st + 1], AF.Ln)
                            return h.activation(rlb[:, os2:os2 + 1], rlt[:, :],
                                                AF.Exp, scale=-1.0)
                        sch.add("a", f_rl,
                                waits=[(c_lup, IPB * (sbk + 1)),
                                       (c_obf, max(0, oi - 1))],
                                incs=[(c_rl, 1)])

                        def f_on(h, st=st, os2=os2):
                            if DBG_UNNORM:
                                return h.tensor_scalar(obf[os2][:, :],
                                                       o_acc[:, st, :],
                                                       1.0, None, ALU.mult)
                            return h.tensor_scalar(obf[os2][:, :], o_acc[:, st, :],
                                                   rlb[:, os2:os2 + 1], None,
                                                   ALU.mult)
                        sch.add("v", f_on,
                                waits=[(c_ofl, IPB * (sbk + 1)), (c_rl, oi + 1),
                                       spo.wait_done(oi - 2)],
                                incs=[(c_obf, 1)])

                        def f_sp(h, oi=oi, os2=os2):
                            return h.dma_start(ons[oi * 128:(oi + 1) * 128, :],
                                               obf[os2][:, :])
                        sch.add("a", f_sp, waits=[(c_obf, oi + 1)],
                                incs=[spo.inc(oi)])

                _emit(nc, sch)
        A_DONE = spo.totals()

        if "A2" in phases or "F1" in phases:
            with ExitStack() as af:
                ow_sb = af.enter_context(nc.sbuf_tensor("ow_sb", [128, 16, D], BF16))
                xn2T = af.enter_context(nc.sbuf_tensor("xn2T", [128, 16, HH2], BF16))
                sg_sb = af.enter_context(nc.sbuf_tensor("sg_sb", [128, 64], F32))
                su_sb = af.enter_context(nc.sbuf_tensor("su_sb", [128, 64], F32))

                c_w2 = ctr("c_w2")
                oldr = DmaRing(ctr, "old", 2)
                xld2r = DmaRing(ctr, "xld2", 2)
                c_T2 = ctr("c_T2")
                c_oTe = ctr("c_oTe")
                c_opr = ctr("c_opr")
                c_x2 = ctr("c_x2")
                c_sq2 = ctr("c_sq2")
                c_mn2 = ctr("c_mn2")
                c_msq2 = ctr("c_msq2")
                c_st2 = ctr("c_st2")
                c_std2 = ctr("c_std2")
                c_xn2 = ctr("c_xn2")
                c_T2b = ctr("c_T2b")
                c_xn2Te = ctr("c_xn2Te")
                wgr = DmaRing(ctr, "wgr", 2)
                wur = DmaRing(ctr, "wur", 2)
                c_gr = ctr("c_gr")
                c_ur = ctr("c_ur")
                c_gact = ctr("c_gact")
                c_us = ctr("c_us")
                c_fu = ctr("c_fu")

                XN2TE_HALF = 2 * TPH if "A2" in phases else 0

                first_af = True
                for hh in range(2):
                    if "A2" in phases:
                        with ExitStack() as ph:
                            sb = lambda n, shp, dt: ph.enter_context(
                                nc.sbuf_tensor(f"{n}_a2{hh}", shp, dt))
                            oT_sb = sb("oT_sb", [128, 32, 128], BF16)
                            ob_in = [sb(f"ob_in{j}", [128, D], BF16) for j in range(2)]
                            x2sb = [sb(f"x2sb{j}", [128, D], F32) for j in range(2)]
                            x2bf = [sb(f"x2bf{j}", [128, D], BF16) for j in range(2)]
                            xn2 = [sb(f"xn2_{j}", [128, D], BF16) for j in range(2)]
                            junk2 = sb("junk2", [128, D], BF16)
                            sxp2 = sb("sxp2", [128, 2], F32)
                            sxxp2 = sb("sxxp2", [128, 2], F32)
                            meanp2 = sb("meanp2", [128, 2], F32)
                            msqp2 = sb("msqp2", [128, 2], F32)
                            mvarp2 = sb("mvarp2", [128, 2], F32)
                            stdp2 = sb("stdp2", [128, 2], F32)
                            rsp2 = sb("rsp2", [128, 2], F32)
                            pstA = ph.enter_context(
                                nc.psum_tensor(f"pstA_a2{hh}", [128, 8, 128], BF16))
                            pstB = ph.enter_context(
                                nc.psum_tensor(f"pstB_a2{hh}", [128, 8, 128], BF16))
                            psAt = [ph.enter_context(
                                nc.psum_tensor(f"psAt{j}_a2{hh}", [128, 1024], F32))
                                for j in range(2)]

                            sch = new_sched()
                            if first_af:
                                for qq in range(4):
                                    def f_ow(h, qq=qq):
                                        return h.dma_start(
                                            ow_sb[:, 4 * qq:4 * qq + 4, :],
                                            owT.rearrange("(c p) j -> p c j", p=128)
                                            [:, 4 * qq:4 * qq + 4, :])
                                    sch.add("g", f_ow,
                                            waits=A_DONE if qq == 0 else [],
                                            incs=[(c_w2, 16)])
                                sch.add("g", lambda h: h.dma_start(sg_sb[:, :], sg2),
                                        incs=[(c_w2, 16)])
                                sch.add("g", lambda h: h.dma_start(su_sb[:, :], su2),
                                        incs=[(c_w2, 16)])
                                first_af = False
                            W2_N = c_w2.n

                            for tt in range(TPH):
                                tg = hh * TPH + tt
                                t2 = tt % 2

                                def f_ol(h, tg=tg, t2=t2):
                                    return h.dma_start(
                                        ob_in[t2][:, :],
                                        ons[tg * 128:(tg + 1) * 128, :])
                                f1gate = ([(cc, 16 * 32 * hh) for cc in spf.c]
                                          if "F1" in phases and hh else [])
                                sch.add("s", f_ol,
                                        waits=A_DONE + f1gate
                                        + [(c_T2, max(0, 2 * (tg - 1)))],
                                        incs=[oldr.inc(tg)])

                                for b, pst2 in ((0, pstA), (1, pstB)):
                                    def f_T(h, t2=t2, b=b, pst2=pst2):
                                        last = None
                                        for j in range(8):
                                            cc0 = 8 * b + j
                                            last = h.transpose(
                                                pst2[:, j, :],
                                                ob_in[t2][:, cc0 * 128:(cc0 + 1) * 128],
                                                ident_sb[:, :])
                                        return last
                                    sch.add("t", f_T,
                                            waits=[oldr.wait_done(tg),
                                                   (c_xn2Te, max(0, 2 * (tg - 1) + b + 1))],
                                            incs=[(c_T2, 1)])

                                    def f_ev(h, t2=t2, b=b, pst2=pst2):
                                        return h.tensor_copy(
                                            oT_sb[:, t2 * 16 + 8 * b:t2 * 16 + 8 * b + 8, :],
                                            pst2[:, :, :])
                                    sch.add("v", f_ev,
                                            waits=[(c_T2, 2 * tg + b + 1),
                                                   (c_opr, max(0, 2 * (tg - 1)))],
                                            incs=[(c_oTe, 1)])

                                def f_xl(h, tg=tg, t2=t2):
                                    return h.dma_start(
                                        x2sb[t2][:, :],
                                        x_all[tg * 128:(tg + 1) * 128, :])
                                sch.add("s", f_xl,
                                        waits=A_DONE + f1gate
                                        + [(c_sq2, max(0, tg - 1)),
                                           spx2.wait_done(tg - 2)],
                                        incs=[xld2r.inc(tg)])

                                for rr in range(2):
                                    for nb in range(2):
                                        for qq in range(2):
                                            def f_pr(h, t2=t2, rr=rr, nb=nb, qq=qq):
                                                last = None
                                                for c in range(8 * qq, 8 * qq + 8):
                                                    last = h.matmul(
                                                        psAt[rr][:, nb * 512:(nb + 1) * 512],
                                                        oT_sb[:, t2 * 16 + c, :],
                                                        ow_sb[:, c, rr * 1024 + nb * 512:
                                                              rr * 1024 + (nb + 1) * 512],
                                                        start=(c == 0), stop=(c == 15))
                                                return last
                                            w_pr = [(c_w2, 32 * (qq + 1))]
                                            if nb == 0 and qq == 0:
                                                w_pr += [(c_oTe, 2 * tg + 2),
                                                         (c_x2, max(0, 2 * tg + rr - 1))]
                                            sch.add("t", f_pr, waits=w_pr,
                                                    incs=[(c_opr, 1)]
                                                    if (nb == 1 and qq == 1) else [])

                                    def f_x2(h, t2=t2, rr=rr):
                                        return h.tensor_add(
                                            x2sb[t2][:, rr * 1024:(rr + 1) * 1024],
                                            psAt[rr][:, :],
                                            x2sb[t2][:, rr * 1024:(rr + 1) * 1024])
                                    sch.add("v", f_x2,
                                            waits=[(c_opr, 2 * tg + rr + 1),
                                                   xld2r.wait_done(tg)],
                                            incs=[(c_x2, 1)])

                                def f_spx(h, tg=tg, t2=t2):
                                    return h.dma_start(
                                        x2s[tg * 128:(tg + 1) * 128, :],
                                        x2sb[t2][:, :])
                                sch.add("a", f_spx, waits=[(c_x2, 2 * tg + 2)],
                                        incs=[spx2.inc(tg)])

                                def f_cp(h, t2=t2):
                                    return h.activation(
                                        x2bf[t2][:, :], x2sb[t2][:, :], AF.Copy,
                                        accum_out=sxp2[:, t2:t2 + 1])
                                sch.add("a", f_cp,
                                        waits=[(c_x2, 2 * tg + 2),
                                               (c_xn2, max(0, tg - 1)),
                                               (c_st2, max(0, tg - 1))])

                                def f_sq(h, t2=t2):
                                    return h.activation(
                                        junk2[:, :], x2sb[t2][:, :], AF.Square,
                                        accum_out=sxxp2[:, t2:t2 + 1])
                                sch.add("a", f_sq, incs=[(c_sq2, 1)])

                                def f_mean(h, t2=t2):
                                    return h.tensor_scalar(
                                        meanp2[:, t2:t2 + 1], sxp2[:, t2:t2 + 1],
                                        1.0 / D, None, ALU.mult)
                                sch.add("v", f_mean,
                                        waits=[(c_sq2, tg + 1),
                                               (c_std2, max(0, tg - 1))],
                                        incs=[(c_mn2, 1)])

                                def f_msq(h, t2=t2):
                                    return h.activation(
                                        msqp2[:, t2:t2 + 1], meanp2[:, t2:t2 + 1],
                                        AF.Square)
                                sch.add("a", f_msq, waits=[(c_mn2, tg + 1)],
                                        incs=[(c_msq2, 1)])

                                def f_var(h, t2=t2):
                                    return h.scalar_tensor_tensor(
                                        mvarp2[:, t2:t2 + 1], sxxp2[:, t2:t2 + 1],
                                        1.0 / D, msqp2[:, t2:t2 + 1],
                                        ALU.mult, ALU.subtract)
                                sch.add("v", f_var, waits=[(c_msq2, tg + 1)],
                                        incs=[(c_st2, 1)])

                                def f_sqrt(h, t2=t2):
                                    h.activation(
                                        stdp2[:, t2:t2 + 1], mvarp2[:, t2:t2 + 1],
                                        AF.Ln, bias=eps_ap)
                                    return h.activation(
                                        rsp2[:, t2:t2 + 1], stdp2[:, t2:t2 + 1],
                                        AF.Exp, scale=-0.5)
                                sch.add("a", f_sqrt,
                                        waits=[(c_st2, tg + 1),
                                               (c_xn2, max(0, tg - 1))],
                                        incs=[(c_std2, 1)])

                                def f_xn(h, t2=t2):
                                    return h.tensor_scalar(
                                        xn2[t2][:, :], x2bf[t2][:, :],
                                        meanp2[:, t2:t2 + 1], rsp2[:, t2:t2 + 1],
                                        ALU.subtract, ALU.mult)
                                sch.add("v", f_xn,
                                        waits=[(c_std2, tg + 1),
                                               (c_T2b, max(0, 2 * (tg - 1)))],
                                        incs=[(c_xn2, 1)])

                                for b, pst2 in ((0, pstA), (1, pstB)):
                                    def f_T(h, t2=t2, b=b, pst2=pst2):
                                        last = None
                                        for j in range(8):
                                            cc0 = 8 * b + j
                                            last = h.transpose(
                                                pst2[:, j, :],
                                                xn2[t2][:, cc0 * 128:(cc0 + 1) * 128],
                                                ident_sb[:, :])
                                        return last
                                    sch.add("t", f_T,
                                            waits=[(c_xn2, tg + 1),
                                                   (c_oTe, 2 * tg + b + 1)],
                                            incs=[(c_T2b, 1)])

                                    def f_ev(h, b=b, pst2=pst2, tt=tt, hh=hh):
                                        return h.tensor_copy(
                                            xn2T[:, 8 * b:8 * b + 8,
                                                 tt * 128:(tt + 1) * 128],
                                            pst2[:, :, :])
                                    sch.add("v", f_ev,
                                            waits=[(c_T2b, 2 * tg + b + 1),
                                                   (c_gr, 64 * hh),
                                                   (c_ur, 64 * hh)],
                                            incs=[(c_xn2Te, 1)])
                            _emit(nc, sch)

                    if "F1" in phases:
                        with ExitStack() as ph:
                            sb = lambda n, shp, dt: ph.enter_context(
                                nc.sbuf_tensor(f"{n}_f1{hh}", shp, dt))
                            wg_sb = [sb(f"wg_sb{j}", [128, D], BF16) for j in range(2)]
                            wu_sb = [sb(f"wu_sb{j}", [128, D], BF16) for j in range(2)]
                            gact = [sb(f"gact{j}", [128, HH2], BF16) for j in range(2)]
                            usb = [sb(f"usb{j}", [128, HH2], BF16) for j in range(2)]
                            fuse = [sb(f"fuse{j}", [128, HH2], BF16) for j in range(2)]
                            psG = ph.enter_context(
                                nc.psum_tensor(f"psG_f1{hh}", [128, HH2], F32))
                            psU = ph.enter_context(
                                nc.psum_tensor(f"psU_f1{hh}", [128, HH2], F32))

                            sch = new_sched()
                            if first_af:
                                sch.add("g", lambda h: h.dma_start(sg_sb[:, :], sg2),
                                        incs=[(c_w2, 16)])
                                sch.add("g", lambda h: h.dma_start(su_sb[:, :], su2),
                                        incs=[(c_w2, 16)])
                                first_af = False
                            WF_N = c_w2.n

                            for ic in range(64):
                                ig = hh * 64 + ic
                                i2 = ic % 2

                                phase_gate = (
                                    [(c_xn2Te, XN2TE_HALF * (hh + 1))]
                                    + [(cc, 16 * (TPH // 2) * (hh + 1))
                                       for cc in spx2.c]
                                    if "A2" in phases else [])

                                def f_lg(h, ic=ic, i2=i2):
                                    return h.dma_start(wg_sb[i2][:, :], wgt[ic, :, :])
                                sch.add("g", f_lg,
                                        waits=[(c_gr, max(0, ig - 1))] + phase_gate,
                                        incs=[wgr.inc(ig)])

                                def f_lu(h, ic=ic, i2=i2):
                                    return h.dma_start(wu_sb[i2][:, :], wut[ic, :, :])
                                sch.add("g", f_lu, waits=[(c_ur, max(0, ig - 1))],
                                        incs=[wur.inc(ig)])

                                def f_g(h, i2=i2):
                                    last = None
                                    for nb in range(HH2 // 512):
                                        for c in range(16):
                                            last = h.matmul(
                                                psG[:, nb * 512:(nb + 1) * 512],
                                                wg_sb[i2][:, c * 128:(c + 1) * 128],
                                                xn2T[:, c, nb * 512:(nb + 1) * 512],
                                                start=(c == 0), stop=(c == 15))
                                    return last
                                sch.add("t", f_g,
                                        waits=[wgr.wait_done(ig),
                                               (c_xn2Te, XN2TE_HALF * (hh + 1)),
                                               (c_gact, ig)],
                                        incs=[(c_gr, 1)])

                                def f_u(h, i2=i2):
                                    last = None
                                    for nb in range(HH2 // 512):
                                        for c in range(16):
                                            last = h.matmul(
                                                psU[:, nb * 512:(nb + 1) * 512],
                                                wu_sb[i2][:, c * 128:(c + 1) * 128],
                                                xn2T[:, c, nb * 512:(nb + 1) * 512],
                                                start=(c == 0), stop=(c == 15))
                                    return last
                                sch.add("t", f_u,
                                        waits=[wur.wait_done(ig), (c_us, ig)],
                                        incs=[(c_ur, 1)])

                                def f_ga(h, ic=ic, i2=i2):
                                    return h.activation(gact[i2][:, :], psG[:, :],
                                                        AF.Silu,
                                                        scale=sg_sb[:, ic:ic + 1])
                                sch.add("a", f_ga,
                                        waits=[(c_gr, ig + 1), (c_w2, WF_N),
                                               (c_fu, max(0, ig - 1))],
                                        incs=[(c_gact, 1)])

                                def f_us(h, ic=ic, i2=i2):
                                    return h.tensor_scalar(usb[i2][:, :], psU[:, :],
                                                           su_sb[:, ic:ic + 1],
                                                           None, ALU.mult)
                                sch.add("v", f_us,
                                        waits=[(c_ur, ig + 1), (c_w2, WF_N),
                                               (c_fu, max(0, ig - 1))],
                                        incs=[(c_us, 1)])

                                def f_fu(h, i2=i2):
                                    return h.tensor_mul(fuse[i2][:, :],
                                                        gact[i2][:, :],
                                                        usb[i2][:, :])
                                sch.add("v", f_fu,
                                        waits=[(c_gact, ig + 1),
                                               spf.wait_done(ig - 2)],
                                        incs=[(c_fu, 1)])

                                def f_sp(h, ic=ic, i2=i2, hh=hh):
                                    ntl = HH2 // 128
                                    return h.dma_start(
                                        fs[:, hh * ntl:(hh + 1) * ntl, ic, :],
                                        fuse[i2][:, :].rearrange(
                                            "p (t s) -> p t s", s=128))
                                sch.add("s", f_sp, waits=[(c_fu, ig + 1)],
                                        incs=[spf.inc(ig)])
                            _emit(nc, sch)
        AF_DONE = spx2.totals() + spf.totals()

        if "F2" in phases:
            with ExitStack() as ph:
                sb = lambda n, shp, dt: ph.enter_context(nc.sbuf_tensor(n, shp, dt))
                wd_sb = sb("wd_sb", [128, 64, 1024], BF16)
                fuse_st = [sb(f"fuse_st{j}", [128, 64, 128], BF16) for j in range(3)]
                x2t = [sb(f"x2t{j}", [128, 1024], F32) for j in range(2)]
                ty = [sb(f"ty{j}", [128, 1024], F32) for j in range(2)]
                sd_sb = sb("sd_sb", [128, D], F32)
                psD = [ph.enter_context(nc.psum_tensor(f"psD{j}", [128, 1024], F32))
                       for j in range(2)]

                c_wd = ctr("c_wd")
                fldr = DmaRing(ctr, "fld", 3)
                x2ldr = DmaRing(ctr, "x2ld", 2)
                c_dr = ctr("c_dr")
                c_dev = ctr("c_dev")

                sch = new_sched()
                sch.add("g", lambda h: h.dma_start(sd_sb[:, :], sd_bc),
                        incs=[(c_wd, 16)])

                for eh in range(2):
                    for qq in range(4):
                        def f_wd(h, eh=eh, qq=qq):
                            return h.dma_start(
                                wd_sb[:, 16 * qq:16 * qq + 16, :],
                                wd.rearrange("(a p) e -> p a e", p=128)
                                [:, 16 * qq:16 * qq + 16,
                                 eh * 1024:(eh + 1) * 1024])
                        sch.add("g", f_wd,
                                waits=[(c_dr, NST * eh)] if qq == 0 else [],
                                incs=[(c_wd, 16)])

                    for st in range(NST):
                        ii = eh * NST + st
                        s2 = st % 2
                        sf3 = ii % 3

                        def f_fl(h, st=st, sf3=sf3):
                            return h.dma_start(fuse_st[sf3][:, :, :],
                                               fs[:, st, :, :])
                        sch.add("s", f_fl,
                                waits=AF_DONE + [(c_dr, max(0, ii - 2))],
                                incs=[fldr.inc(ii)])

                        def f_xl(h, st=st, eh=eh, s2=s2):
                            return h.dma_start(
                                x2t[s2][:, :],
                                x2s[st * 128:(st + 1) * 128,
                                    eh * 1024:(eh + 1) * 1024])
                        sch.add("a", f_xl,
                                waits=AF_DONE + [(c_dev, max(0, ii - 1))],
                                incs=[x2ldr.inc(ii)])

                        for nb in range(2):
                            for qq in range(4):
                                def f_mm(h, s2=s2, sf3=sf3, nb=nb, qq=qq):
                                    last = None
                                    for ic in range(16 * qq, 16 * qq + 16):
                                        last = h.matmul(
                                            psD[s2][:, nb * 512:(nb + 1) * 512],
                                            fuse_st[sf3][:, ic, :],
                                            wd_sb[:, ic, nb * 512:(nb + 1) * 512],
                                            start=(ic == 0), stop=(ic == 63))
                                    return last
                                w_mm = [(c_wd, 16 * (2 + 4 * eh + qq))]
                                if nb == 0 and qq == 0:
                                    w_mm += [fldr.wait_done(ii),
                                             (c_dev, max(0, ii - 1))]
                                sch.add("t", f_mm, waits=w_mm,
                                        incs=[(c_dr, 1)]
                                        if (nb == 1 and qq == 3) else [])

                        def f_ev(h, eh=eh, s2=s2):
                            h.tensor_mul(ty[s2][:, :], psD[s2][:, :],
                                         sd_sb[:, eh * 1024:(eh + 1) * 1024])
                            return h.tensor_add(ty[s2][:, :], ty[s2][:, :],
                                                x2t[s2][:, :])
                        sch.add("v", f_ev,
                                waits=[(c_dr, ii + 1), x2ldr.wait_done(ii),
                                       spy.wait_done(ii - 2)],
                                incs=[(c_dev, 1)])

                        def f_sp(h, st=st, eh=eh, s2=s2):
                            return h.dma_start(
                                y[st * 128:(st + 1) * 128,
                                  eh * 1024:(eh + 1) * 1024],
                                ty[s2][:, :])
                        sch.add("a", f_sp, waits=[(c_dev, ii + 1)],
                                incs=[spy.inc(ii)])

                sch.add("a", lambda h: h.nop(), waits=spy.totals())
                _emit(nc, sch)

        tail = Sched()
        tail_waits = (spkq.totals() + spv.totals() + spo.totals()
                      + spx2.totals() + spf.totals() + spy.totals())
        tail.add("a", lambda h: h.nop(), waits=tail_waits)
        tail.add("s", lambda h: h.nop(), waits=tail_waits)
        _emit(nc, tail)

    return nc



def _rope_tables():
    f = np.arange(64, dtype=np.float64)
    inv = 10000.0 ** (-f / 64.0)
    pos = np.arange(S, dtype=np.float64)
    ang = inv[:, None] * pos[None, :]
    cos = np.cos(ang)
    sin = np.sin(ang)
    return (np.concatenate([cos, cos], axis=0),
            np.concatenate([-sin, sin], axis=0))


def host_prepare(inputs):
    x = np.asarray(inputs["L0_attn_x"], dtype=np.float32)
    qw = np.asarray(inputs["L0_attn_q_w"], dtype=np.float32)
    qb = np.asarray(inputs["L0_attn_q_b"], dtype=np.float32)
    kvw = np.asarray(inputs["L0_attn_kv_w"], dtype=np.float32)
    kvb = np.asarray(inputs["L0_attn_kv_b"], dtype=np.float32)
    ow = np.asarray(inputs["L0_attn_o_w"], dtype=np.float32)
    wg = np.asarray(inputs["L0_ffn_w_gate"], dtype=np.int8)
    wu = np.asarray(inputs["L0_ffn_w_up"], dtype=np.int8)
    wdn = np.asarray(inputs["L0_ffn_w_down"], dtype=np.int8)
    sg = np.asarray(inputs["L0_ffn_s_gate"], dtype=np.float32)
    su = np.asarray(inputs["L0_ffn_s_up"], dtype=np.float32)
    sd = np.asarray(inputs["L0_ffn_s_down"], dtype=np.float32)

    cosT64, sinT64 = _rope_tables()

    shared = dict(
        qwT=np.ascontiguousarray(qw.T),
        kvwT=np.ascontiguousarray(kvw.T),
        owT=np.ascontiguousarray(ow.T),
        wgt=np.ascontiguousarray(
            wg.reshape(16, 128, 64, 128).transpose(2, 1, 0, 3)).reshape(64, 128, D),
        wut=np.ascontiguousarray(
            wu.reshape(16, 128, 64, 128).transpose(2, 1, 0, 3)).reshape(64, 128, D),
        wd=wdn,
        qb2=np.ascontiguousarray(qb.reshape(16, 128).T),
        kvb2=np.ascontiguousarray(kvb.reshape(32, 128).T),
        sg2=np.ascontiguousarray(sg.reshape(64, 128).T),
        su2=np.ascontiguousarray(su.reshape(64, 128).T),
        vb_bc=np.ascontiguousarray(np.broadcast_to(kvb[D:], (128, D))),
        sd_bc=np.ascontiguousarray(np.broadcast_to(sd, (128, D))),
        identt=np.eye(128, dtype=ml_dtypes.bfloat16),
        constt=np.repeat(np.array([[1e-5, 0.0, 0.0, 0.0]], np.float32), 128, 0),
    )

    def _mk_tab(c64, s64):
        t = np.stack([c64.reshape(128, S // CH, CH),
                      s64.reshape(128, S // CH, CH)], axis=2)
        return np.ascontiguousarray(t).astype(ml_dtypes.bfloat16)

    in_maps = []
    for c in range(8):
        b, h = c // 2, c % 2
        m = dict(shared)
        if h == 0:
            m["x_all"] = np.ascontiguousarray(x[b])
            m["costab"] = _mk_tab(cosT64, sinT64)
        else:
            perm = np.concatenate([np.arange(HS, S), np.arange(0, HS)])
            m["x_all"] = np.ascontiguousarray(x[b][perm])
            m["costab"] = _mk_tab(np.ascontiguousarray(cosT64[:, perm]),
                                  np.ascontiguousarray(sinT64[:, perm]))
        in_maps.append(m)
    return in_maps


def kernel(**inputs):
    in_maps = host_prepare(inputs)
    nc = build_program()
    res = run_bass_kernel_spmd(nc, in_maps, core_ids=list(range(8)))
    out = np.empty((4, S, D), dtype=np.float32)
    for c in range(8):
        b, h = c // 2, c % 2
        out[b, h * HS:(h + 1) * HS, :] = np.asarray(res.results[c]["y"],
                                                    dtype=np.float32)
    return out


# version 14
# speedup vs baseline: 1.1356x; 1.0760x over previous
import numpy as np
import ml_dtypes
from contextlib import ExitStack

import concourse.bass as bass
import concourse.mybir as mybir
from concourse.bass_utils import run_bass_kernel_spmd

F32 = mybir.dt.float32
BF16 = mybir.dt.bfloat16
I8 = mybir.dt.int8
AF = mybir.ActivationFunctionType
ALU = mybir.AluOpType
AX = mybir.AxisListType.X

D = 2048
S = 8192
HS = 4096
INTER = 8192
NT = S // 128
CH = 512
NCH = S // CH
TPC = CH // 128
NQCH = HS // CH
SCALE = float(128 ** -0.5)
NG = S // 1024
NSB = HS // 512
IPB = NG * 4
TPH = HS // 256
HH2 = HS // 2
NST = HS // 128
BIG = 1.0e30
DBG_NOFLASH = False
DBG_UNNORM = False


class Ctr:
    def __init__(self, nc, ctx, name):
        self.sem = ctx.enter_context(nc.semaphore(name))
        self.n = 0


class DmaRing:

    def __init__(self, ctr, name, depth):
        self.c = [ctr(f"{name}{j}") for j in range(depth)]
        self.depth = depth

    def inc(self, i):
        return (self.c[i % self.depth], 16)

    def wait_done(self, i):
        if i < 0:
            return None
        return (self.c[i % self.depth], 16 * (i // self.depth + 1))

    def totals(self):
        return [(c, c.n) for c in self.c if c.n > 0]


class Sched:
    def __init__(self):
        self.q = {e: [] for e in "tvasg"}

    def add(self, eng, fn, waits=(), incs=()):
        w = [cv for cv in waits if cv is not None and cv[1] > 0]
        for c, k in incs:
            c.n += k
        self.q[eng].append((fn, w, list(incs)))


def _emit(nc, sched):
    with nc.Block() as blk:
        def runner(name):
            def run(h):
                for fn, waits, incs in sched.q[name]:
                    for c, v in waits:
                        h.wait_ge(c.sem, v)
                    last = fn(h)
                    for c, k in incs:
                        last.then_inc(c.sem, k)
            return run
        blk.tensor(runner("t"))
        blk.vector(runner("v"))
        blk.scalar(runner("a"))
        blk.sync(runner("s"))
        blk.gpsimd(runner("g"))


def build_program(phases=("P0", "A", "A2", "F1", "F2"), expose=None):
    expose = expose or {}
    nc = bass.Bass("TRN2", target_bir_lowering=False)

    def din(name, shape, dtype):
        return nc.dram_tensor(name, shape, dtype, kind="ExternalInput").ap()

    def dscr(name, shape, dtype):
        if name in expose:
            kind = "ExternalInput" if expose[name] == "in" else "ExternalOutput"
            return nc.dram_tensor(name, shape, dtype, kind=kind).ap()
        return nc.dram_tensor(name, shape, dtype).ap()

    x_all = din("x_all", [S, D], F32)
    qwT = din("qwT", [D, D], F32)
    kvwT = din("kvwT", [D, 2 * D], F32)
    owT = din("owT", [D, D], F32)
    wgt = din("wgt", [64, 128, D], I8)
    wut = din("wut", [64, 128, D], I8)
    wd = din("wd", [INTER, D], I8)
    qb2 = din("qb2", [128, 16], F32)
    kvb2 = din("kvb2", [128, 32], F32)
    sg2 = din("sg2", [128, 64], F32)
    su2 = din("su2", [128, 64], F32)
    vb_bc = din("vb_bc", [128, D], F32)
    sd_bc = din("sd_bc", [128, D], F32)
    costab = din("costab", [128, S // CH, 2, CH], BF16)
    identt = din("identt", [128, 128], BF16)
    constt = din("constt", [128, 4], F32)

    kTs = dscr("kTs", [128, 16, S], BF16)
    qTs = dscr("qTs", [128, 16, HS], BF16)
    vs = dscr("vs", [S, D], BF16)
    ons = dscr("ons", [HS, D], BF16)
    x2s = dscr("x2s", [HS, D], F32)
    fs = dscr("fs", [128, NST, 64, 128], BF16)
    y = nc.dram_tensor("y", [HS, D], F32, kind="ExternalOutput").ap()

    with ExitStack() as top:
        ident_sb = top.enter_context(nc.sbuf_tensor("ident_sb", [128, 128], BF16))
        const_sb = top.enter_context(nc.sbuf_tensor("const_sb", [128, 4], F32))
        eps_ap = const_sb[:, 0:1]
        ctr = lambda name: Ctr(nc, top, name)

        c_base = ctr("c_base")
        spkq = DmaRing(ctr, "spkq", 4)
        spv = DmaRing(ctr, "spv", 4)
        xldr = DmaRing(ctr, "xld", 4)
        spo = DmaRing(ctr, "spo", 2)
        spx2 = DmaRing(ctr, "spx2", 2)
        spf = DmaRing(ctr, "spf", 2)
        spy = DmaRing(ctr, "spy", 2)

        base = Sched()
        base.add("g", lambda h: h.dma_start(ident_sb[:, :], identt), incs=[(c_base, 16)])
        base.add("g", lambda h: h.dma_start(const_sb[:, :], constt), incs=[(c_base, 16)])
        BASE_N = c_base.n
        _first = [base]

        def new_sched():
            if _first:
                return _first.pop()
            return Sched()

        def emit_p0_pass(kind):
            with ExitStack() as ph:
                sb = lambda n, shp, dt: ph.enter_context(
                    nc.sbuf_tensor(n + "_" + kind, shp, dt))
                if kind == "kq":
                    qw_sb = sb("qw_sb", [128, 16, D], BF16)
                    wk_sb = sb("wk_sb", [128, 16, D], BF16)
                    qb_sb = sb("qb_sb", [128, 16], F32)
                    kvb_sb = sb("kvb_sb", [128, 32], F32)
                    tb_sl = [sb(f"tb{j}", [128, 2, CH], BF16) for j in range(2)]
                    kq_sb = [sb(f"kq{j}", [128, CH], BF16) for j in range(4)]
                    swp = sb("swp", [128, CH], BF16)
                else:
                    wv_sb = sb("wv_sb", [128, 16, D], BF16)
                    vb_sb = sb("vb_sb", [128, D], F32)
                    v_sb = [sb(f"vsb{j}", [128, CH], BF16) for j in range(4)]
                xnT = [sb(f"xnT{p}", [128, 16, CH], BF16) for p in range(2)]
                xbf = [sb(f"xbf{j}", [128, D], BF16) for j in range(4)]
                junk = sb("junk", [128, D], BF16)
                xn = [sb(f"xn{j}", [128, D], BF16) for j in range(4)]
                sxp = sb("sxp", [128, 4], F32)
                sxxp = sb("sxxp", [128, 4], F32)
                meanp = sb("meanp", [128, 4], F32)
                msqp = sb("msqp", [128, 4], F32)
                mvarp = sb("mvarp", [128, 4], F32)
                stdp = sb("stdp", [128, 4], F32)
                rsp = sb("rsp", [128, 4], F32)
                pst = [ph.enter_context(
                    nc.psum_tensor(f"pst{j}_{kind}", [128, 8, 128], BF16))
                    for j in range(2)]
                psr = [ph.enter_context(
                    nc.psum_tensor(f"psr{j}_{kind}", [128, CH], F32))
                    for j in range(4)]

                c_w = ctr("c_w_" + kind)
                xoff = 0 if kind == "kq" else NT
                c_sq = ctr("c_sq_" + kind)
                c_mn = ctr("c_mn_" + kind)
                c_msq = ctr("c_msq_" + kind)
                c_st1 = ctr("c_st1_" + kind)
                c_std = ctr("c_std_" + kind)
                c_xn = ctr("c_xn_" + kind)
                c_T0 = ctr("c_T0_" + kind)
                c_xnTe = ctr("c_xnTe_" + kind)
                c_rnd = ctr("c_rnd_" + kind)
                c_rev = ctr("c_rev_" + kind)
                if kind == "kq":
                    c_rope = ctr("c_rope_" + kind)
                    tbr = DmaRing(ctr, "tbr_" + kind, 2)

                sch = new_sched()
                gate = spkq.totals() if kind == "v" else []

                kv_view = kvwT.rearrange("(c p) j -> p c j", p=128)
                wload_q = []
                if kind == "kq":
                    qview = qwT.rearrange("(c p) j -> p c j", p=128)
                    for pc in range(4):
                        wload_q.append(lambda h, pc=pc: h.dma_start(
                            wk_sb[:, pc * 4:(pc + 1) * 4, :],
                            kv_view[:, pc * 4:(pc + 1) * 4, 0:D]))
                    wload_q.append(lambda h: h.dma_start(kvb_sb[:, :], kvb2))
                    wload_q.append(lambda h: h.dma_start(qb_sb[:, :], qb2))
                    for pc in range(4):
                        wload_q.append(lambda h, pc=pc: h.dma_start(
                            qw_sb[:, pc * 4:(pc + 1) * 4, :],
                            qview[:, pc * 4:(pc + 1) * 4, :]))
                else:
                    for pc in range(4):
                        wload_q.append(lambda h, pc=pc: h.dma_start(
                            wv_sb[:, pc * 4:(pc + 1) * 4, :],
                            kv_view[:, pc * 4:(pc + 1) * 4, D:2 * D]))
                    wload_q.append(lambda h: h.dma_start(vb_sb[:, :], vb_bc))

                def pump_wload():
                    if wload_q:
                        fn = wload_q.pop(0)
                        sch.add("g", fn, waits=gate if not wl_started else [],
                                incs=[(c_w, 16)])
                        wl_started.append(1)
                wl_started = []
                W_N = 16 * len(wload_q)
                WK_N = 16 * 6 if kind == "kq" else W_N

                if kind == "kq":
                    rounds_in = lambda ch: 16 + (16 if ch < NQCH else 0)
                    ropes_in = lambda ch: 16 + (16 if ch < NQCH else 0)
                else:
                    rounds_in = lambda ch: 16
                    ropes_in = lambda ch: 0
                rounds_before = [0]
                ropes_thru = [0]
                for ch in range(NCH):
                    rounds_before.append(rounds_before[-1] + rounds_in(ch))
                    ropes_thru.append(ropes_thru[-1] + ropes_in(ch))

                def ln_tile(i):
                    s3, s2 = i % 4, i % 4

                    def f_load(h, i=i, s3=s3):
                        return h.dma_start(xbf[s3][:, :],
                                           x_all[i * 128:(i + 1) * 128, :])
                    sch.add("g", f_load,
                            waits=[(c_xn, max(0, i - 3)), (c_base, BASE_N)] + gate,
                            incs=[xldr.inc(xoff + i)])
                    pump_wload()

                    def f_cp(h, s3=s3, s2=s2):
                        return h.activation(junk[:, :], xbf[s3][:, :], AF.Copy,
                                            accum_out=sxp[:, s2:s2 + 1])
                    sch.add("a", f_cp,
                            waits=[xldr.wait_done(xoff + i), (c_st1, max(0, i - 3))])

                    def f_sq(h, s3=s3, s2=s2):
                        return h.activation(junk[:, :], xbf[s3][:, :], AF.Square,
                                            accum_out=sxxp[:, s2:s2 + 1])
                    sch.add("a", f_sq, incs=[(c_sq, 1)])

                    def f_mean(h, s2=s2):
                        return h.tensor_scalar(meanp[:, s2:s2 + 1],
                                               sxp[:, s2:s2 + 1],
                                               1.0 / D, None, ALU.mult)
                    sch.add("v", f_mean,
                            waits=[(c_sq, i + 1), (c_msq, max(0, i - 3))],
                            incs=[(c_mn, 1)])

                    def f_msq(h, s2=s2):
                        return h.activation(msqp[:, s2:s2 + 1],
                                            meanp[:, s2:s2 + 1], AF.Square)
                    sch.add("a", f_msq, waits=[(c_mn, i + 1)], incs=[(c_msq, 1)])

                    def f_var(h, s2=s2):
                        return h.scalar_tensor_tensor(
                            mvarp[:, s2:s2 + 1], sxxp[:, s2:s2 + 1], 1.0 / D,
                            msqp[:, s2:s2 + 1], ALU.mult, ALU.subtract)
                    sch.add("v", f_var, waits=[(c_msq, i + 1)], incs=[(c_st1, 1)])

                    def f_sqrt(h, s2=s2):
                        h.activation(stdp[:, s2:s2 + 1], mvarp[:, s2:s2 + 1],
                                     AF.Ln, bias=eps_ap)
                        return h.activation(rsp[:, s2:s2 + 1], stdp[:, s2:s2 + 1],
                                            AF.Exp, scale=-0.5)
                    sch.add("a", f_sqrt,
                            waits=[(c_st1, i + 1), (c_xn, max(0, i - 3))],
                            incs=[(c_std, 1)])

                    def f_xn(h, s3=s3, s2=s2):
                        return h.tensor_scalar(xn[s3][:, :], xbf[s3][:, :],
                                               meanp[:, s2:s2 + 1],
                                               rsp[:, s2:s2 + 1],
                                               ALU.subtract, ALU.mult)
                    sch.add("v", f_xn,
                            waits=[(c_std, i + 1), (c_T0, max(0, 2 * (i - 4) + 2))],
                            incs=[(c_xn, 1)])

                def t_batch(i, b):
                    ch = i // TPC
                    par = ch % 2
                    st = i % TPC

                    def f_T(h, i=i, b=b):
                        last = None
                        for j in range(8):
                            cc0 = 8 * b + j
                            last = h.transpose(pst[b][:, j, :],
                                               xn[i % 4][:, cc0 * 128:(cc0 + 1) * 128],
                                               ident_sb[:, :])
                        return last
                    w = [(c_xn, i + 1), (c_xnTe, max(0, 2 * (i - 1) + b + 1))]
                    if ch >= 2:
                        w.append((c_rnd, rounds_before[ch - 1]))
                    sch.add("t", f_T, waits=w, incs=[(c_T0, 1)])

                    def f_ev(h, par=par, b=b, st=st):
                        return h.tensor_copy(
                            xnT[par][:, 8 * b:8 * b + 8, st * 128:(st + 1) * 128],
                            pst[b][:, :, :])
                    sch.add("v", f_ev, waits=[(c_T0, 2 * i + b + 1)],
                            incs=[(c_xnTe, 1)])

                def tbl_load(ch):
                    sl = ch % 2
                    w = [(c_rope, ropes_thru[ch - 1])] if ch >= 2 else []

                    def f_c(h, ch=ch, sl=sl):
                        return h.dma_start(tb_sl[sl][:, :, :], costab[:, ch, :, :])
                    sch.add("a", f_c, waits=w, incs=[tbr.inc(ch)])

                state = {"r": 0, "skq": 0, "sv": 0, "rp": 0}

                def do_round(knd, ch, a1, a2=0):
                    par = ch % 2
                    rr = state["r"]
                    pp = rr % 4

                    if knd in ("k", "q"):
                        oc = a1
                        wsb = wk_sb if knd == "k" else qw_sb

                        def f_mm(h, wsb=wsb, oc=oc, par=par, pp=pp):
                            last = None
                            for c in range(16):
                                last = h.matmul(psr[pp][:, :],
                                                wsb[:, c, oc * 128:(oc + 1) * 128],
                                                xnT[par][:, c, :],
                                                start=(c == 0), stop=(c == 15))
                            return last
                    else:
                        st4, eq = a1, a2

                        def f_mm(h, st4=st4, eq=eq, par=par, pp=pp):
                            last = None
                            for c in range(16):
                                last = h.matmul(psr[pp][:, :],
                                                xnT[par][:, c, st4 * 128:(st4 + 1) * 128],
                                                wv_sb[:, c, eq * CH:(eq + 1) * CH],
                                                start=(c == 0), stop=(c == 15))
                            return last
                    wneed = WK_N if knd in ("k", "v") else W_N
                    sch.add("t", f_mm,
                            waits=[(c_xnTe, 8 * (ch + 1)), (c_w, wneed),
                                   (c_rev, max(0, rr - 3))],
                            incs=[(c_rnd, 1)])

                    if knd in ("k", "q"):
                        skq = state["skq"]
                        es = skq % 4
                        bias = (kvb_sb[:, a1:a1 + 1] if knd == "k"
                                else qb_sb[:, a1:a1 + 1])

                        def f_ev(h, es=es, pp=pp, bias=bias):
                            return h.activation(kq_sb[es][:, :], psr[pp][:, :],
                                                AF.Identity, bias=bias)
                        sch.add("a", f_ev,
                                waits=[(c_rnd, rr + 1), spkq.wait_done(skq - 4)],
                                incs=[(c_rev, 1)])

                        sl = ch % 2

                        def f_rope(h, es=es, sl=sl):
                            h.tensor_copy(swp[0:64, :], kq_sb[es][64:128, :])
                            h.tensor_copy(swp[64:128, :], kq_sb[es][0:64, :])
                            h.tensor_mul(swp[:, :], swp[:, :], tb_sl[sl][:, 1, :])
                            h.tensor_mul(kq_sb[es][:, :], kq_sb[es][:, :],
                                         tb_sl[sl][:, 0, :])
                            return h.tensor_add(kq_sb[es][:, :], kq_sb[es][:, :],
                                                swp[:, :])
                        sch.add("v", f_rope,
                                waits=[(c_rev, rr + 1), tbr.wait_done(ch)],
                                incs=[(c_rope, 1)])

                        dst = (kTs[:, a1, ch * CH:(ch + 1) * CH] if knd == "k"
                               else qTs[:, a1, ch * CH:(ch + 1) * CH])

                        def f_sp(h, es=es, dst=dst):
                            return h.dma_start(dst, kq_sb[es][:, :])
                        sch.add("s", f_sp, waits=[(c_rope, state["rp"] + 1)],
                                incs=[spkq.inc(skq)])
                        state["skq"] += 1
                        state["rp"] += 1
                    else:
                        st4, eq = a1, a2
                        sv = state["sv"]
                        es = sv % 4

                        def f_ev(h, es=es, pp=pp, eq=eq):
                            return h.tensor_add(v_sb[es][:, :], psr[pp][:, :],
                                                vb_sb[:, eq * CH:(eq + 1) * CH])
                        sch.add("v", f_ev,
                                waits=[(c_rnd, rr + 1), spv.wait_done(sv - 4)],
                                incs=[(c_rev, 1)])

                        row0 = (ch * TPC + st4) * 128

                        def f_sp(h, es=es, row0=row0, eq=eq):
                            return h.dma_start(
                                vs[row0:row0 + 128, eq * CH:(eq + 1) * CH],
                                v_sb[es][:, :])
                        sch.add("s", f_sp, waits=[(c_rev, rr + 1)],
                                incs=[spv.inc(sv)])
                        state["sv"] += 1
                    state["r"] += 1

                for ch in range(NCH + 1):
                    if ch >= 1:
                        while wload_q:
                            pump_wload()
                    if ch < NCH and kind == "kq":
                        tbl_load(ch)
                    if ch == 0:
                        for st in range(TPC):
                            ln_tile(st)
                        for st in range(TPC):
                            for b in range(2):
                                t_batch(st, b)
                    if ch >= 1:
                        cur = ch - 1
                        pend_ln = ([ch * TPC + st for st in range(TPC)]
                                   if ch < NCH else [])
                        pend_tb = ([(ch * TPC + st, b) for st in range(TPC)
                                    for b in range(2)] if ch < NCH else [])
                        if kind == "kq":
                            rlist = ([("k", cur, oc) for oc in range(16)]
                                     + ([("q", cur, oc) for oc in range(16)]
                                        if cur < NQCH else []))
                        else:
                            rlist = [("v", cur, st4, eq)
                                     for st4 in range(TPC) for eq in range(TPC)]
                        step = max(1, len(rlist) // 16)
                        LEAD = 4
                        slot = 0
                        for rn, r in enumerate(rlist):
                            do_round(r[0], r[1], r[2], r[3] if len(r) > 3 else 0)
                            if rn % step == step - 1:
                                slot += 1
                                if pend_ln:
                                    ln_tile(pend_ln.pop(0))
                                if pend_tb and slot > LEAD:
                                    t_batch(*pend_tb.pop(0))
                        while pend_ln:
                            ln_tile(pend_ln.pop(0))
                        while pend_tb:
                            t_batch(*pend_tb.pop(0))

                _emit(nc, sch)

        if "P0" in phases:
            emit_p0_pass("kq")
            emit_p0_pass("v")
        P0_DONE = spkq.totals() + spv.totals()

        if "A" in phases:
            with ExitStack() as ph:
                sb = lambda n, shp, dt: ph.enter_context(nc.sbuf_tensor(n, shp, dt))
                qT_sb = sb("qT_sb", [128, 16, 512], BF16)
                kT_b = [sb(f"kT_b{j}", [128, 16, 1024], BF16) for j in range(2)]
                v_b = [sb(f"v_b{j}", [128, 8, D], BF16) for j in range(2)]
                o_acc = sb("o_acc", [128, 4, D], F32)
                p_sb = [sb(f"p_sb{j}", [128, 1024], BF16) for j in range(2)]
                pT_sb = [sb(f"pT_sb{j}", [128, 8, 128], BF16) for j in range(2)]
                nm = sb("nm", [128, 4, 2], F32)
                pm = sb("pm", [128, 4, 2], F32)
                lsum = sb("lsum", [128, 4], F32)
                tmaxb = sb("tmaxb", [128, 1], F32)
                ccb = sb("ccb", [128, 2], F32)
                lpart = sb("lpart", [128, 2], F32)
                rlb = sb("rlb", [128, 2], F32)
                rlt = sb("rlt", [128, 1], F32)
                spcr = sb("spcr", [128, 8], F32)
                obf = [sb(f"obf{j}", [128, D], BF16) for j in range(2)]
                psS = ph.enter_context(nc.psum_tensor("psS", [128, 1024], F32))
                psT = ph.enter_context(nc.psum_tensor("psT", [128, 8, 128], BF16))
                psO = ph.enter_context(nc.psum_tensor("psO", [128, D], F32))

                qldr = DmaRing(ctr, "qld", 4)
                kldr = DmaRing(ctr, "kld", 2)
                vldr = DmaRing(ctr, "vld", 2)
                c_sc = ctr("c_sc")
                c_nm = ctr("c_nm")
                c_cc = ctr("c_cc")
                c_exp = ctr("c_exp")
                c_lup = ctr("c_lup")
                c_T = ctr("c_T")
                c_pTe = ctr("c_pTe")
                c_pv = ctr("c_pv")
                c_ofl = ctr("c_ofl")
                c_obf = ctr("c_obf")
                c_rm = ctr("c_rm")
                c_rl = ctr("c_rl")
                c_ms = ctr("c_ms")

                sch = new_sched()

                def f_init(h):
                    h.memset(o_acc[:, :, :], 0.0)
                    h.memset(lsum[:, :], 0.0)
                    h.memset(pm[:, :, :], -BIG)
                    return h.memset(nm[:, :, :], BIG)
                sch.add("v", f_init, incs=[(c_ms, 1)])

                def sc_round(sbk, idx):
                    gidx = sbk * IPB + idx
                    g = idx // 4
                    gg = sbk * NG + g
                    gp = g % 2
                    st = idx % 4
                    g2 = idx % 2

                    def f(h, gp=gp, st=st):
                        last = None
                        for nb in range(2):
                            for c in range(16):
                                last = h.matmul(
                                    psS[:, nb * 512:(nb + 1) * 512],
                                    qT_sb[:, c, st * 128:(st + 1) * 128],
                                    kT_b[gp][:, c, nb * 512:(nb + 1) * 512],
                                    start=(c == 0), stop=(c == 15))
                        return last
                    sch.add("t", f,
                            waits=[qldr.wait_done(sbk * 4 + st), kldr.wait_done(gg),
                                   (c_exp, gidx)],
                            incs=[(c_sc, 1)])

                    def f_rm(h):
                        return h.reduce_max(tmaxb[:, :], psS[:, :], axis=AX)
                    if not DBG_NOFLASH:
                        sch.add("v", f_rm, waits=[(c_sc, gidx + 1)],
                                incs=[(c_rm, 1)])

                    def f_cc(h, st=st, gp=gp, g2=g2):
                        return h.activation(ccb[:, g2:g2 + 1],
                                            pm[:, st, 1 - gp:2 - gp], AF.Exp,
                                            bias=nm[:, st, gp:gp + 1])
                    if not DBG_NOFLASH:
                        sch.add("a", f_cc,
                                waits=[(c_nm, gidx + 1), (c_ofl, max(0, gidx - 1))],
                                incs=[(c_cc, 1)])

                    if DBG_NOFLASH:
                        def f_exp(h, st=st, gp=gp, g2=g2):
                            return h.activation(p_sb[g2][:, :], psS[:, :], AF.Exp,
                                                scale=SCALE,
                                                accum_out=lpart[:, g2:g2 + 1])
                        sch.add("a", f_exp,
                                waits=[(c_sc, gidx + 1),
                                       (c_T, max(0, gidx - 1)),
                                       (c_lup, max(0, gidx - 1))],
                                incs=[(c_exp, 1)])
                    else:
                        def f_exp(h, st=st, gp=gp, g2=g2):
                            return h.activation(p_sb[g2][:, :], psS[:, :], AF.Exp,
                                                bias=nm[:, st, gp:gp + 1], scale=SCALE,
                                                accum_out=lpart[:, g2:g2 + 1])
                        sch.add("a", f_exp,
                                waits=[(c_T, max(0, gidx - 1)),
                                       (c_lup, max(0, gidx - 1))],
                                incs=[(c_exp, 1)])

                def nm_update(sbk, idx):
                    if DBG_NOFLASH:
                        return
                    gidx = sbk * IPB + idx
                    g = idx // 4
                    gp = g % 2
                    st = idx % 4
                    g2 = idx % 2

                    def f_nmu(h, st=st, gp=gp):
                        h.tensor_scalar(nm[:, st, gp:gp + 1], tmaxb[:, :],
                                        -SCALE, nm[:, st, 1 - gp:2 - gp],
                                        ALU.mult, ALU.min)
                        return h.tensor_scalar(pm[:, st, gp:gp + 1], tmaxb[:, :],
                                               SCALE, pm[:, st, 1 - gp:2 - gp],
                                               ALU.mult, ALU.max)
                    sch.add("v", f_nmu,
                            waits=[(c_rm, gidx + 1), (c_ms, c_ms.n),
                                   (c_cc, max(0, gidx - 1))],
                            incs=[(c_nm, 1)])

                def spacer3(h):
                    h.memset(spcr[:, :], 0.0)
                    h.memset(spcr[:, :], 1.0)
                    return h.memset(spcr[:, :], 2.0)

                def t_pv_pTe(sbk, idx):
                    gidx = sbk * IPB + idx
                    g2 = idx % 2

                    def f_T(h, g2=g2):
                        last = None
                        for tc in range(8):
                            last = h.transpose(psT[:, tc, :],
                                               p_sb[g2][:, tc * 128:(tc + 1) * 128],
                                               ident_sb[:, :])
                        return last
                    sch.add("t", f_T, waits=[(c_exp, gidx + 1), (c_pTe, gidx)],
                            incs=[(c_T, 1)])

                    def f_pTe(h, g2=g2):
                        return h.tensor_copy(pT_sb[g2][:, :, :], psT[:, :, :])
                    sch.add("v", f_pTe,
                            waits=[(c_T, gidx + 1), (c_pv, max(0, gidx - 1))],
                            incs=[(c_pTe, 1)])

                def t_pv_a(sbk, idx):
                    gidx = sbk * IPB + idx
                    g2 = idx % 2
                    st = idx % 4

                    cscal = 1.0 if DBG_NOFLASH else ccb[:, g2:g2 + 1]

                    def f_lup(h, st=st, g2=g2, cscal=cscal):
                        return h.scalar_tensor_tensor(
                            lsum[:, st:st + 1], lsum[:, st:st + 1],
                            cscal, lpart[:, g2:g2 + 1],
                            ALU.mult, ALU.add)
                    lw = [(c_exp, gidx + 1)]
                    if not DBG_NOFLASH:
                        lw.append((c_cc, gidx + 1))
                    sch.add("v", f_lup, waits=lw, incs=[(c_lup, 1)])

                def t_pv_b(sbk, idx):
                    gidx = sbk * IPB + idx
                    g = idx // 4
                    gg = sbk * NG + g
                    gp = g % 2
                    st = idx % 4
                    g2 = idx % 2

                    def f_pv(h, g2=g2, gp=gp):
                        last = None
                        for nb in range(4):
                            for tc in range(8):
                                last = h.matmul(
                                    psO[:, nb * 512:(nb + 1) * 512],
                                    pT_sb[g2][:, tc, :],
                                    v_b[gp][:, tc, nb * 512:(nb + 1) * 512],
                                    start=(tc == 0), stop=(tc == 7))
                        return last
                    sch.add("t", f_pv,
                            waits=[(c_pTe, gidx + 1), vldr.wait_done(gg),
                                   (c_ofl, gidx)],
                            incs=[(c_pv, 1)])

                    cscal2 = 1.0 if DBG_NOFLASH else ccb[:, g2:g2 + 1]

                    def f_ofl(h, st=st, g2=g2, cscal2=cscal2):
                        return h.scalar_tensor_tensor(
                            o_acc[:, st, :], o_acc[:, st, :],
                            cscal2, psO[:, :], ALU.mult, ALU.add)
                    sch.add("v", f_ofl, waits=[(c_pv, gidx + 1)],
                            incs=[(c_ofl, 1)])

                for sbk in range(NSB):
                    for g in range(2):
                        gg = sbk * NG + g
                        gp = g % 2

                        def f_k(h, g=g, gp=gp):
                            return h.dma_start(kT_b[gp][:, :, :],
                                               kTs[:, :, g * 1024:(g + 1) * 1024])
                        sch.add("s", f_k,
                                waits=P0_DONE + [(c_sc, max(0, 4 * (gg - 1)))],
                                incs=[kldr.inc(gg)])

                        def f_v(h, g=g, gp=gp):
                            return h.dma_start(
                                v_b[gp][:, :, :],
                                vs[g * 1024:(g + 1) * 1024, :]
                                .rearrange("(tc p) e -> p tc e", p=128))
                        sch.add("g", f_v,
                                waits=P0_DONE + [(c_pv, max(0, 4 * (gg - 1)))],
                                incs=[vldr.inc(gg)])

                    for qp in range(4):
                        def f_q(h, sbk=sbk, qp=qp):
                            return h.dma_start(
                                qT_sb[:, :, qp * 128:(qp + 1) * 128],
                                qTs[:, :, sbk * 512 + qp * 128:
                                    sbk * 512 + (qp + 1) * 128])
                        w = [(c_sc, max(0, (sbk - 1) * IPB + IPB - 4 + qp + 1))]
                        sch.add("a", f_q, waits=P0_DONE + w,
                                incs=[qldr.inc(sbk * 4 + qp)])

                    if sbk > 0:
                        def f_ms(h):
                            if DBG_NOFLASH:
                                h.memset(o_acc[:, :, :], 0.0)
                                h.memset(lsum[:, :], 0.0)
                            h.memset(pm[:, :, :], -BIG)
                            return h.memset(nm[:, :, :], BIG)
                        msw = [(c_exp, IPB * sbk)]
                        if DBG_NOFLASH:
                            msw.append((c_obf, 4 * sbk))
                        sch.add("v", f_ms, waits=msw, incs=[(c_ms, 1)])

                    for g in range(2, NG):
                        gg = sbk * NG + g
                        gp = g % 2

                        def f_k(h, g=g, gp=gp):
                            return h.dma_start(kT_b[gp][:, :, :],
                                               kTs[:, :, g * 1024:(g + 1) * 1024])
                        sch.add("s", f_k,
                                waits=P0_DONE + [(c_sc, max(0, 4 * (gg - 1)))],
                                incs=[kldr.inc(gg)])

                        def f_v(h, g=g, gp=gp):
                            return h.dma_start(
                                v_b[gp][:, :, :],
                                vs[g * 1024:(g + 1) * 1024, :]
                                .rearrange("(tc p) e -> p tc e", p=128))
                        sch.add("g", f_v,
                                waits=P0_DONE + [(c_pv, max(0, 4 * (gg - 1)))],
                                incs=[vldr.inc(gg)])

                    sc_round(sbk, 0)
                    if not DBG_NOFLASH:
                        sch.add("v", spacer3)
                    nm_update(sbk, 0)
                    for idx in range(1, IPB):
                        t_pv_pTe(sbk, idx - 1)
                        sc_round(sbk, idx)
                        t_pv_a(sbk, idx - 1)
                        if not DBG_NOFLASH:
                            sch.add("v", lambda h: h.memset(spcr[:, :], 0.0))
                        nm_update(sbk, idx)
                        t_pv_b(sbk, idx - 1)
                    t_pv_pTe(sbk, IPB - 1)
                    t_pv_a(sbk, IPB - 1)
                    t_pv_b(sbk, IPB - 1)

                    for st in range(4):
                        oi = sbk * 4 + st
                        os2 = oi % 2

                        def f_rl(h, st=st, os2=os2):
                            h.activation(rlt[:, :], lsum[:, st:st + 1], AF.Ln)
                            return h.activation(rlb[:, os2:os2 + 1], rlt[:, :],
                                                AF.Exp, scale=-1.0)
                        sch.add("a", f_rl,
                                waits=[(c_lup, IPB * (sbk + 1)),
                                       (c_obf, max(0, oi - 1))],
                                incs=[(c_rl, 1)])

                        def f_on(h, st=st, os2=os2):
                            if DBG_UNNORM:
                                return h.tensor_scalar(obf[os2][:, :],
                                                       o_acc[:, st, :],
                                                       1.0, None, ALU.mult)
                            return h.tensor_scalar(obf[os2][:, :], o_acc[:, st, :],
                                                   rlb[:, os2:os2 + 1], None,
                                                   ALU.mult)
                        sch.add("v", f_on,
                                waits=[(c_ofl, IPB * (sbk + 1)), (c_rl, oi + 1),
                                       spo.wait_done(oi - 2)],
                                incs=[(c_obf, 1)])

                        def f_sp(h, oi=oi, os2=os2):
                            return h.dma_start(ons[oi * 128:(oi + 1) * 128, :],
                                               obf[os2][:, :])
                        sch.add("a", f_sp, waits=[(c_obf, oi + 1)],
                                incs=[spo.inc(oi)])

                _emit(nc, sch)
        A_DONE = spo.totals()

        if "A2" in phases or "F1" in phases:
            with ExitStack() as af:
                ow_sb = af.enter_context(nc.sbuf_tensor("ow_sb", [128, 16, D], BF16))
                xn2T = af.enter_context(nc.sbuf_tensor("xn2T", [128, 16, HH2], BF16))
                sg_sb = af.enter_context(nc.sbuf_tensor("sg_sb", [128, 64], F32))
                su_sb = af.enter_context(nc.sbuf_tensor("su_sb", [128, 64], F32))

                c_w2 = ctr("c_w2")
                oldr = DmaRing(ctr, "old", 2)
                xld2r = DmaRing(ctr, "xld2", 2)
                c_T2 = ctr("c_T2")
                c_oTe = ctr("c_oTe")
                c_opr = ctr("c_opr")
                c_x2 = ctr("c_x2")
                c_sq2 = ctr("c_sq2")
                c_mn2 = ctr("c_mn2")
                c_msq2 = ctr("c_msq2")
                c_st2 = ctr("c_st2")
                c_std2 = ctr("c_std2")
                c_xn2 = ctr("c_xn2")
                c_T2b = ctr("c_T2b")
                c_xn2Te = ctr("c_xn2Te")
                wgr = DmaRing(ctr, "wgr", 2)
                wur = DmaRing(ctr, "wur", 2)
                c_gr = ctr("c_gr")
                c_ur = ctr("c_ur")
                c_gact = ctr("c_gact")
                c_us = ctr("c_us")
                c_fu = ctr("c_fu")

                XN2TE_HALF = 2 * TPH if "A2" in phases else 0

                first_af = True
                for hh in range(2):
                    if "A2" in phases:
                        with ExitStack() as ph:
                            sb = lambda n, shp, dt: ph.enter_context(
                                nc.sbuf_tensor(f"{n}_a2{hh}", shp, dt))
                            oT_sb = sb("oT_sb", [128, 32, 128], BF16)
                            ob_in = [sb(f"ob_in{j}", [128, D], BF16) for j in range(2)]
                            x2sb = [sb(f"x2sb{j}", [128, D], F32) for j in range(2)]
                            x2bf = [sb(f"x2bf{j}", [128, D], BF16) for j in range(2)]
                            xn2 = [sb(f"xn2_{j}", [128, D], BF16) for j in range(2)]
                            junk2 = sb("junk2", [128, D], BF16)
                            sxp2 = sb("sxp2", [128, 2], F32)
                            sxxp2 = sb("sxxp2", [128, 2], F32)
                            meanp2 = sb("meanp2", [128, 2], F32)
                            msqp2 = sb("msqp2", [128, 2], F32)
                            mvarp2 = sb("mvarp2", [128, 2], F32)
                            stdp2 = sb("stdp2", [128, 2], F32)
                            rsp2 = sb("rsp2", [128, 2], F32)
                            pstA = ph.enter_context(
                                nc.psum_tensor(f"pstA_a2{hh}", [128, 8, 128], BF16))
                            pstB = ph.enter_context(
                                nc.psum_tensor(f"pstB_a2{hh}", [128, 8, 128], BF16))
                            psAt = [ph.enter_context(
                                nc.psum_tensor(f"psAt{j}_a2{hh}", [128, 1024], F32))
                                for j in range(2)]

                            sch = new_sched()
                            if first_af:
                                for qq in range(4):
                                    def f_ow(h, qq=qq):
                                        return h.dma_start(
                                            ow_sb[:, 4 * qq:4 * qq + 4, :],
                                            owT.rearrange("(c p) j -> p c j", p=128)
                                            [:, 4 * qq:4 * qq + 4, :])
                                    sch.add("g", f_ow,
                                            waits=A_DONE if qq == 0 else [],
                                            incs=[(c_w2, 16)])
                                sch.add("g", lambda h: h.dma_start(sg_sb[:, :], sg2),
                                        incs=[(c_w2, 16)])
                                sch.add("g", lambda h: h.dma_start(su_sb[:, :], su2),
                                        incs=[(c_w2, 16)])
                                first_af = False
                            W2_N = c_w2.n

                            for tt in range(TPH):
                                tg = hh * TPH + tt
                                t2 = tt % 2

                                def f_ol(h, tg=tg, t2=t2):
                                    return h.dma_start(
                                        ob_in[t2][:, :],
                                        ons[tg * 128:(tg + 1) * 128, :])
                                f1gate = ([(cc, 16 * 32 * hh) for cc in spf.c]
                                          if "F1" in phases and hh else [])
                                sch.add("s", f_ol,
                                        waits=A_DONE + f1gate
                                        + [(c_T2, max(0, 2 * (tg - 1)))],
                                        incs=[oldr.inc(tg)])

                                for b, pst2 in ((0, pstA), (1, pstB)):
                                    def f_T(h, t2=t2, b=b, pst2=pst2):
                                        last = None
                                        for j in range(8):
                                            cc0 = 8 * b + j
                                            last = h.transpose(
                                                pst2[:, j, :],
                                                ob_in[t2][:, cc0 * 128:(cc0 + 1) * 128],
                                                ident_sb[:, :])
                                        return last
                                    sch.add("t", f_T,
                                            waits=[oldr.wait_done(tg),
                                                   (c_xn2Te, max(0, 2 * (tg - 1) + b + 1))],
                                            incs=[(c_T2, 1)])

                                    def f_ev(h, t2=t2, b=b, pst2=pst2):
                                        return h.tensor_copy(
                                            oT_sb[:, t2 * 16 + 8 * b:t2 * 16 + 8 * b + 8, :],
                                            pst2[:, :, :])
                                    sch.add("v", f_ev,
                                            waits=[(c_T2, 2 * tg + b + 1),
                                                   (c_opr, max(0, 2 * (tg - 1)))],
                                            incs=[(c_oTe, 1)])

                                def f_xl(h, tg=tg, t2=t2):
                                    return h.dma_start(
                                        x2sb[t2][:, :],
                                        x_all[tg * 128:(tg + 1) * 128, :])
                                sch.add("s", f_xl,
                                        waits=A_DONE + f1gate
                                        + [(c_sq2, max(0, tg - 1)),
                                           spx2.wait_done(tg - 2)],
                                        incs=[xld2r.inc(tg)])

                                for rr in range(2):
                                    for nb in range(2):
                                        for qq in range(2):
                                            def f_pr(h, t2=t2, rr=rr, nb=nb, qq=qq):
                                                last = None
                                                for c in range(8 * qq, 8 * qq + 8):
                                                    last = h.matmul(
                                                        psAt[rr][:, nb * 512:(nb + 1) * 512],
                                                        oT_sb[:, t2 * 16 + c, :],
                                                        ow_sb[:, c, rr * 1024 + nb * 512:
                                                              rr * 1024 + (nb + 1) * 512],
                                                        start=(c == 0), stop=(c == 15))
                                                return last
                                            w_pr = [(c_w2, 32 * (qq + 1))]
                                            if nb == 0 and qq == 0:
                                                w_pr += [(c_oTe, 2 * tg + 2),
                                                         (c_x2, max(0, 2 * tg + rr - 1))]
                                            sch.add("t", f_pr, waits=w_pr,
                                                    incs=[(c_opr, 1)]
                                                    if (nb == 1 and qq == 1) else [])

                                    def f_x2(h, t2=t2, rr=rr):
                                        return h.tensor_add(
                                            x2sb[t2][:, rr * 1024:(rr + 1) * 1024],
                                            psAt[rr][:, :],
                                            x2sb[t2][:, rr * 1024:(rr + 1) * 1024])
                                    sch.add("v", f_x2,
                                            waits=[(c_opr, 2 * tg + rr + 1),
                                                   xld2r.wait_done(tg)],
                                            incs=[(c_x2, 1)])

                                def f_spx(h, tg=tg, t2=t2):
                                    return h.dma_start(
                                        x2s[tg * 128:(tg + 1) * 128, :],
                                        x2sb[t2][:, :])
                                sch.add("a", f_spx, waits=[(c_x2, 2 * tg + 2)],
                                        incs=[spx2.inc(tg)])

                                def f_cp(h, t2=t2):
                                    return h.activation(
                                        x2bf[t2][:, :], x2sb[t2][:, :], AF.Copy,
                                        accum_out=sxp2[:, t2:t2 + 1])
                                sch.add("a", f_cp,
                                        waits=[(c_x2, 2 * tg + 2),
                                               (c_xn2, max(0, tg - 1)),
                                               (c_st2, max(0, tg - 1))])

                                def f_sq(h, t2=t2):
                                    return h.activation(
                                        junk2[:, :], x2sb[t2][:, :], AF.Square,
                                        accum_out=sxxp2[:, t2:t2 + 1])
                                sch.add("a", f_sq, incs=[(c_sq2, 1)])

                                def f_mean(h, t2=t2):
                                    return h.tensor_scalar(
                                        meanp2[:, t2:t2 + 1], sxp2[:, t2:t2 + 1],
                                        1.0 / D, None, ALU.mult)
                                sch.add("v", f_mean,
                                        waits=[(c_sq2, tg + 1),
                                               (c_std2, max(0, tg - 1))],
                                        incs=[(c_mn2, 1)])

                                def f_msq(h, t2=t2):
                                    return h.activation(
                                        msqp2[:, t2:t2 + 1], meanp2[:, t2:t2 + 1],
                                        AF.Square)
                                sch.add("a", f_msq, waits=[(c_mn2, tg + 1)],
                                        incs=[(c_msq2, 1)])

                                def f_var(h, t2=t2):
                                    return h.scalar_tensor_tensor(
                                        mvarp2[:, t2:t2 + 1], sxxp2[:, t2:t2 + 1],
                                        1.0 / D, msqp2[:, t2:t2 + 1],
                                        ALU.mult, ALU.subtract)
                                sch.add("v", f_var, waits=[(c_msq2, tg + 1)],
                                        incs=[(c_st2, 1)])

                                def f_sqrt(h, t2=t2):
                                    h.activation(
                                        stdp2[:, t2:t2 + 1], mvarp2[:, t2:t2 + 1],
                                        AF.Ln, bias=eps_ap)
                                    return h.activation(
                                        rsp2[:, t2:t2 + 1], stdp2[:, t2:t2 + 1],
                                        AF.Exp, scale=-0.5)
                                sch.add("a", f_sqrt,
                                        waits=[(c_st2, tg + 1),
                                               (c_xn2, max(0, tg - 1))],
                                        incs=[(c_std2, 1)])

                                def f_xn(h, t2=t2):
                                    return h.tensor_scalar(
                                        xn2[t2][:, :], x2bf[t2][:, :],
                                        meanp2[:, t2:t2 + 1], rsp2[:, t2:t2 + 1],
                                        ALU.subtract, ALU.mult)
                                sch.add("v", f_xn,
                                        waits=[(c_std2, tg + 1),
                                               (c_T2b, max(0, 2 * (tg - 1)))],
                                        incs=[(c_xn2, 1)])

                                for b, pst2 in ((0, pstA), (1, pstB)):
                                    def f_T(h, t2=t2, b=b, pst2=pst2):
                                        last = None
                                        for j in range(8):
                                            cc0 = 8 * b + j
                                            last = h.transpose(
                                                pst2[:, j, :],
                                                xn2[t2][:, cc0 * 128:(cc0 + 1) * 128],
                                                ident_sb[:, :])
                                        return last
                                    sch.add("t", f_T,
                                            waits=[(c_xn2, tg + 1),
                                                   (c_oTe, 2 * tg + b + 1)],
                                            incs=[(c_T2b, 1)])

                                    def f_ev(h, b=b, pst2=pst2, tt=tt, hh=hh):
                                        return h.tensor_copy(
                                            xn2T[:, 8 * b:8 * b + 8,
                                                 tt * 128:(tt + 1) * 128],
                                            pst2[:, :, :])
                                    sch.add("v", f_ev,
                                            waits=[(c_T2b, 2 * tg + b + 1),
                                                   (c_gr, 64 * hh),
                                                   (c_ur, 64 * hh)],
                                            incs=[(c_xn2Te, 1)])
                            _emit(nc, sch)

                    if "F1" in phases:
                        with ExitStack() as ph:
                            sb = lambda n, shp, dt: ph.enter_context(
                                nc.sbuf_tensor(f"{n}_f1{hh}", shp, dt))
                            wg_sb = [sb(f"wg_sb{j}", [128, D], BF16) for j in range(2)]
                            wu_sb = [sb(f"wu_sb{j}", [128, D], BF16) for j in range(2)]
                            gact = [sb(f"gact{j}", [128, HH2], BF16) for j in range(2)]
                            usb = [sb(f"usb{j}", [128, HH2], BF16) for j in range(2)]
                            fuse = [sb(f"fuse{j}", [128, HH2], BF16) for j in range(2)]
                            psG = ph.enter_context(
                                nc.psum_tensor(f"psG_f1{hh}", [128, HH2], F32))
                            psU = ph.enter_context(
                                nc.psum_tensor(f"psU_f1{hh}", [128, HH2], F32))

                            sch = new_sched()
                            if first_af:
                                sch.add("g", lambda h: h.dma_start(sg_sb[:, :], sg2),
                                        incs=[(c_w2, 16)])
                                sch.add("g", lambda h: h.dma_start(su_sb[:, :], su2),
                                        incs=[(c_w2, 16)])
                                first_af = False
                            WF_N = c_w2.n

                            for ic in range(64):
                                ig = hh * 64 + ic
                                i2 = ic % 2

                                phase_gate = (
                                    [(c_xn2Te, XN2TE_HALF * (hh + 1))]
                                    + [(cc, 16 * (TPH // 2) * (hh + 1))
                                       for cc in spx2.c]
                                    if "A2" in phases else [])

                                def f_lg(h, ic=ic, i2=i2):
                                    return h.dma_start(wg_sb[i2][:, :], wgt[ic, :, :])
                                sch.add("g", f_lg,
                                        waits=[(c_gr, max(0, ig - 1))] + phase_gate,
                                        incs=[wgr.inc(ig)])

                                def f_lu(h, ic=ic, i2=i2):
                                    return h.dma_start(wu_sb[i2][:, :], wut[ic, :, :])
                                sch.add("g", f_lu, waits=[(c_ur, max(0, ig - 1))],
                                        incs=[wur.inc(ig)])

                                def f_g(h, i2=i2):
                                    last = None
                                    for nb in range(HH2 // 512):
                                        for c in range(16):
                                            last = h.matmul(
                                                psG[:, nb * 512:(nb + 1) * 512],
                                                wg_sb[i2][:, c * 128:(c + 1) * 128],
                                                xn2T[:, c, nb * 512:(nb + 1) * 512],
                                                start=(c == 0), stop=(c == 15))
                                    return last
                                sch.add("t", f_g,
                                        waits=[wgr.wait_done(ig),
                                               (c_xn2Te, XN2TE_HALF * (hh + 1)),
                                               (c_gact, ig)],
                                        incs=[(c_gr, 1)])

                                def f_u(h, i2=i2):
                                    last = None
                                    for nb in range(HH2 // 512):
                                        for c in range(16):
                                            last = h.matmul(
                                                psU[:, nb * 512:(nb + 1) * 512],
                                                wu_sb[i2][:, c * 128:(c + 1) * 128],
                                                xn2T[:, c, nb * 512:(nb + 1) * 512],
                                                start=(c == 0), stop=(c == 15))
                                    return last
                                sch.add("t", f_u,
                                        waits=[wur.wait_done(ig), (c_us, ig)],
                                        incs=[(c_ur, 1)])

                                def f_ga(h, ic=ic, i2=i2):
                                    return h.activation(gact[i2][:, :], psG[:, :],
                                                        AF.Silu,
                                                        scale=sg_sb[:, ic:ic + 1])
                                sch.add("a", f_ga,
                                        waits=[(c_gr, ig + 1), (c_w2, WF_N),
                                               (c_fu, max(0, ig - 1))],
                                        incs=[(c_gact, 1)])

                                def f_us(h, ic=ic, i2=i2):
                                    return h.tensor_scalar(usb[i2][:, :], psU[:, :],
                                                           su_sb[:, ic:ic + 1],
                                                           None, ALU.mult)
                                sch.add("v", f_us,
                                        waits=[(c_ur, ig + 1), (c_w2, WF_N),
                                               (c_fu, max(0, ig - 1))],
                                        incs=[(c_us, 1)])

                                def f_fu(h, i2=i2):
                                    return h.tensor_mul(fuse[i2][:, :],
                                                        gact[i2][:, :],
                                                        usb[i2][:, :])
                                sch.add("v", f_fu,
                                        waits=[(c_gact, ig + 1),
                                               spf.wait_done(ig - 2)],
                                        incs=[(c_fu, 1)])

                                def f_sp(h, ic=ic, i2=i2, hh=hh):
                                    ntl = HH2 // 128
                                    return h.dma_start(
                                        fs[:, hh * ntl:(hh + 1) * ntl, ic, :],
                                        fuse[i2][:, :].rearrange(
                                            "p (t s) -> p t s", s=128))
                                sch.add("s", f_sp, waits=[(c_fu, ig + 1)],
                                        incs=[spf.inc(ig)])
                            _emit(nc, sch)
        AF_DONE = spx2.totals() + spf.totals()

        if "F2" in phases:
            with ExitStack() as ph:
                sb = lambda n, shp, dt: ph.enter_context(nc.sbuf_tensor(n, shp, dt))
                wd_sb = sb("wd_sb", [128, 64, 1024], BF16)
                fuse_st = [sb(f"fuse_st{j}", [128, 64, 128], BF16) for j in range(3)]
                x2t = [sb(f"x2t{j}", [128, 1024], F32) for j in range(2)]
                ty = [sb(f"ty{j}", [128, 1024], F32) for j in range(2)]
                sd_sb = sb("sd_sb", [128, D], F32)
                psD = [ph.enter_context(nc.psum_tensor(f"psD{j}", [128, 1024], F32))
                       for j in range(2)]

                c_wd = ctr("c_wd")
                fldr = DmaRing(ctr, "fld", 3)
                x2ldr = DmaRing(ctr, "x2ld", 2)
                c_dr = ctr("c_dr")
                c_dev = ctr("c_dev")

                sch = new_sched()
                sch.add("g", lambda h: h.dma_start(sd_sb[:, :], sd_bc),
                        incs=[(c_wd, 16)])

                for eh in range(2):
                    for qq in range(4):
                        def f_wd(h, eh=eh, qq=qq):
                            return h.dma_start(
                                wd_sb[:, 16 * qq:16 * qq + 16, :],
                                wd.rearrange("(a p) e -> p a e", p=128)
                                [:, 16 * qq:16 * qq + 16,
                                 eh * 1024:(eh + 1) * 1024])
                        sch.add("g", f_wd,
                                waits=[(c_dr, NST * eh)] if qq == 0 else [],
                                incs=[(c_wd, 16)])

                    for st in range(NST):
                        ii = eh * NST + st
                        s2 = st % 2
                        sf3 = ii % 3

                        def f_fl(h, st=st, sf3=sf3):
                            return h.dma_start(fuse_st[sf3][:, :, :],
                                               fs[:, st, :, :])
                        sch.add("s", f_fl,
                                waits=AF_DONE + [(c_dr, max(0, ii - 2))],
                                incs=[fldr.inc(ii)])

                        def f_xl(h, st=st, eh=eh, s2=s2):
                            return h.dma_start(
                                x2t[s2][:, :],
                                x2s[st * 128:(st + 1) * 128,
                                    eh * 1024:(eh + 1) * 1024])
                        sch.add("a", f_xl,
                                waits=AF_DONE + [(c_dev, max(0, ii - 1))],
                                incs=[x2ldr.inc(ii)])

                        for nb in range(2):
                            for qq in range(4):
                                def f_mm(h, s2=s2, sf3=sf3, nb=nb, qq=qq):
                                    last = None
                                    for ic in range(16 * qq, 16 * qq + 16):
                                        last = h.matmul(
                                            psD[s2][:, nb * 512:(nb + 1) * 512],
                                            fuse_st[sf3][:, ic, :],
                                            wd_sb[:, ic, nb * 512:(nb + 1) * 512],
                                            start=(ic == 0), stop=(ic == 63))
                                    return last
                                w_mm = [(c_wd, 16 * (2 + 4 * eh + qq))]
                                if nb == 0 and qq == 0:
                                    w_mm += [fldr.wait_done(ii),
                                             (c_dev, max(0, ii - 1))]
                                sch.add("t", f_mm, waits=w_mm,
                                        incs=[(c_dr, 1)]
                                        if (nb == 1 and qq == 3) else [])

                        def f_ev(h, eh=eh, s2=s2):
                            h.tensor_mul(ty[s2][:, :], psD[s2][:, :],
                                         sd_sb[:, eh * 1024:(eh + 1) * 1024])
                            return h.tensor_add(ty[s2][:, :], ty[s2][:, :],
                                                x2t[s2][:, :])
                        sch.add("v", f_ev,
                                waits=[(c_dr, ii + 1), x2ldr.wait_done(ii),
                                       spy.wait_done(ii - 2)],
                                incs=[(c_dev, 1)])

                        def f_sp(h, st=st, eh=eh, s2=s2):
                            return h.dma_start(
                                y[st * 128:(st + 1) * 128,
                                  eh * 1024:(eh + 1) * 1024],
                                ty[s2][:, :])
                        sch.add("a", f_sp, waits=[(c_dev, ii + 1)],
                                incs=[spy.inc(ii)])

                sch.add("a", lambda h: h.nop(), waits=spy.totals())
                _emit(nc, sch)

        tail = Sched()
        tail_waits = (spkq.totals() + spv.totals() + spo.totals()
                      + spx2.totals() + spf.totals() + spy.totals())
        tail.add("a", lambda h: h.nop(), waits=tail_waits)
        tail.add("s", lambda h: h.nop(), waits=tail_waits)
        _emit(nc, tail)

    return nc



def _rope_tables():
    f = np.arange(64, dtype=np.float64)
    inv = 10000.0 ** (-f / 64.0)
    pos = np.arange(S, dtype=np.float64)
    ang = inv[:, None] * pos[None, :]
    cos = np.cos(ang)
    sin = np.sin(ang)
    return (np.concatenate([cos, cos], axis=0),
            np.concatenate([-sin, sin], axis=0))


def host_prepare(inputs):
    x = np.asarray(inputs["L0_attn_x"], dtype=np.float32)
    qw = np.asarray(inputs["L0_attn_q_w"], dtype=np.float32)
    qb = np.asarray(inputs["L0_attn_q_b"], dtype=np.float32)
    kvw = np.asarray(inputs["L0_attn_kv_w"], dtype=np.float32)
    kvb = np.asarray(inputs["L0_attn_kv_b"], dtype=np.float32)
    ow = np.asarray(inputs["L0_attn_o_w"], dtype=np.float32)
    wg = np.asarray(inputs["L0_ffn_w_gate"], dtype=np.int8)
    wu = np.asarray(inputs["L0_ffn_w_up"], dtype=np.int8)
    wdn = np.asarray(inputs["L0_ffn_w_down"], dtype=np.int8)
    sg = np.asarray(inputs["L0_ffn_s_gate"], dtype=np.float32)
    su = np.asarray(inputs["L0_ffn_s_up"], dtype=np.float32)
    sd = np.asarray(inputs["L0_ffn_s_down"], dtype=np.float32)

    cosT64, sinT64 = _rope_tables()

    shared = dict(
        qwT=np.ascontiguousarray(qw.T),
        kvwT=np.ascontiguousarray(kvw.T),
        owT=np.ascontiguousarray(ow.T),
        wgt=np.ascontiguousarray(
            wg.reshape(16, 128, 64, 128).transpose(2, 1, 0, 3)).reshape(64, 128, D),
        wut=np.ascontiguousarray(
            wu.reshape(16, 128, 64, 128).transpose(2, 1, 0, 3)).reshape(64, 128, D),
        wd=wdn,
        qb2=np.ascontiguousarray(qb.reshape(16, 128).T),
        kvb2=np.ascontiguousarray(kvb.reshape(32, 128).T),
        sg2=np.ascontiguousarray(sg.reshape(64, 128).T),
        su2=np.ascontiguousarray(su.reshape(64, 128).T),
        vb_bc=np.ascontiguousarray(np.broadcast_to(kvb[D:], (128, D))),
        sd_bc=np.ascontiguousarray(np.broadcast_to(sd, (128, D))),
        identt=np.eye(128, dtype=ml_dtypes.bfloat16),
        constt=np.repeat(np.array([[1e-5, 0.0, 0.0, 0.0]], np.float32), 128, 0),
    )

    def _mk_tab(c64, s64):
        t = np.stack([c64.reshape(128, S // CH, CH),
                      s64.reshape(128, S // CH, CH)], axis=2)
        return np.ascontiguousarray(t).astype(ml_dtypes.bfloat16)

    in_maps = []
    for c in range(8):
        b, h = c // 2, c % 2
        m = dict(shared)
        if h == 0:
            m["x_all"] = np.ascontiguousarray(x[b])
            m["costab"] = _mk_tab(cosT64, sinT64)
        else:
            perm = np.concatenate([np.arange(HS, S), np.arange(0, HS)])
            m["x_all"] = np.ascontiguousarray(x[b][perm])
            m["costab"] = _mk_tab(np.ascontiguousarray(cosT64[:, perm]),
                                  np.ascontiguousarray(sinT64[:, perm]))
        in_maps.append(m)
    return in_maps


def kernel(**inputs):
    in_maps = host_prepare(inputs)
    nc = build_program()
    res = run_bass_kernel_spmd(nc, in_maps, core_ids=list(range(8)))
    out = np.empty((4, S, D), dtype=np.float32)
    for c in range(8):
        b, h = c // 2, c % 2
        out[b, h * HS:(h + 1) * HS, :] = np.asarray(res.results[c]["y"],
                                                    dtype=np.float32)
    return out


# version 15
# speedup vs baseline: 1.1574x; 1.0192x over previous
import numpy as np
import ml_dtypes
from contextlib import ExitStack

import concourse.bass as bass
import concourse.mybir as mybir
from concourse.bass_utils import run_bass_kernel_spmd

F32 = mybir.dt.float32
BF16 = mybir.dt.bfloat16
I8 = mybir.dt.int8
AF = mybir.ActivationFunctionType
ALU = mybir.AluOpType
AX = mybir.AxisListType.X

D = 2048
S = 8192
HS = 4096
INTER = 8192
NT = S // 128
CH = 512
NCH = S // CH
TPC = CH // 128
NQCH = HS // CH
SCALE = float(128 ** -0.5)
NG = S // 1024
NSB = HS // 512
IPB = NG * 4
TPH = HS // 256
HH2 = HS // 2
NST = HS // 128
BIG = 1.0e30
DBG_NOFLASH = False
DBG_UNNORM = False


class Ctr:
    def __init__(self, nc, ctx, name):
        self.sem = ctx.enter_context(nc.semaphore(name))
        self.n = 0


class DmaRing:

    def __init__(self, ctr, name, depth):
        self.c = [ctr(f"{name}{j}") for j in range(depth)]
        self.depth = depth

    def inc(self, i):
        return (self.c[i % self.depth], 16)

    def wait_done(self, i):
        if i < 0:
            return None
        return (self.c[i % self.depth], 16 * (i // self.depth + 1))

    def totals(self):
        return [(c, c.n) for c in self.c if c.n > 0]


class Sched:
    def __init__(self):
        self.q = {e: [] for e in "tvasg"}

    def add(self, eng, fn, waits=(), incs=()):
        w = [cv for cv in waits if cv is not None and cv[1] > 0]
        for c, k in incs:
            c.n += k
        self.q[eng].append((fn, w, list(incs)))


def _emit(nc, sched):
    with nc.Block() as blk:
        def runner(name):
            def run(h):
                for fn, waits, incs in sched.q[name]:
                    for c, v in waits:
                        h.wait_ge(c.sem, v)
                    last = fn(h)
                    for c, k in incs:
                        last.then_inc(c.sem, k)
            return run
        blk.tensor(runner("t"))
        blk.vector(runner("v"))
        blk.scalar(runner("a"))
        blk.sync(runner("s"))
        blk.gpsimd(runner("g"))


def build_program(phases=("P0", "A", "A2", "F1", "F2"), expose=None):
    expose = expose or {}
    nc = bass.Bass("TRN2", target_bir_lowering=False)

    def din(name, shape, dtype):
        return nc.dram_tensor(name, shape, dtype, kind="ExternalInput").ap()

    def dscr(name, shape, dtype):
        if name in expose:
            kind = "ExternalInput" if expose[name] == "in" else "ExternalOutput"
            return nc.dram_tensor(name, shape, dtype, kind=kind).ap()
        return nc.dram_tensor(name, shape, dtype).ap()

    x_all = din("x_all", [S, D], F32)
    qwT = din("qwT", [D, D], F32)
    kvwT = din("kvwT", [D, 2 * D], F32)
    owT = din("owT", [D, D], F32)
    wgt = din("wgt", [64, 128, D], I8)
    wut = din("wut", [64, 128, D], I8)
    wd = din("wd", [INTER, D], I8)
    qb2 = din("qb2", [128, 16], F32)
    kvb2 = din("kvb2", [128, 32], F32)
    sg2 = din("sg2", [128, 64], F32)
    su2 = din("su2", [128, 64], F32)
    vb_bc = din("vb_bc", [128, D], F32)
    sd_bc = din("sd_bc", [128, D], F32)
    costab = din("costab", [128, S // CH, 2, CH], BF16)
    identt = din("identt", [128, 128], BF16)
    constt = din("constt", [128, 4], F32)

    kTs = dscr("kTs", [128, 16, S], BF16)
    qTs = dscr("qTs", [128, 16, HS], BF16)
    vs = dscr("vs", [S, D], BF16)
    ons = dscr("ons", [HS, D], BF16)
    x2s = dscr("x2s", [HS, D], F32)
    fs = dscr("fs", [128, NST, 64, 128], BF16)
    y = nc.dram_tensor("y", [HS, D], F32, kind="ExternalOutput").ap()

    with ExitStack() as top:
        ident_sb = top.enter_context(nc.sbuf_tensor("ident_sb", [128, 128], BF16))
        const_sb = top.enter_context(nc.sbuf_tensor("const_sb", [128, 4], F32))
        eps_ap = const_sb[:, 0:1]
        ctr = lambda name: Ctr(nc, top, name)

        c_base = ctr("c_base")
        spkq = DmaRing(ctr, "spkq", 4)
        spv = DmaRing(ctr, "spv", 4)
        xldr = DmaRing(ctr, "xld", 4)
        spo = DmaRing(ctr, "spo", 2)
        spx2 = DmaRing(ctr, "spx2", 2)
        spf = DmaRing(ctr, "spf", 2)
        spy = DmaRing(ctr, "spy", 2)

        base = Sched()
        base.add("g", lambda h: h.dma_start(ident_sb[:, :], identt), incs=[(c_base, 16)])
        base.add("g", lambda h: h.dma_start(const_sb[:, :], constt), incs=[(c_base, 16)])
        BASE_N = c_base.n
        _first = [base]

        def new_sched():
            if _first:
                return _first.pop()
            return Sched()

        def emit_p0_pass(kind):
            with ExitStack() as ph:
                sb = lambda n, shp, dt: ph.enter_context(
                    nc.sbuf_tensor(n + "_" + kind, shp, dt))
                if kind == "kq":
                    qw_sb = sb("qw_sb", [128, 16, D], BF16)
                    wk_sb = sb("wk_sb", [128, 16, D], BF16)
                    qb_sb = sb("qb_sb", [128, 16], F32)
                    kvb_sb = sb("kvb_sb", [128, 32], F32)
                    tb_sl = [sb(f"tb{j}", [128, 2, CH], BF16) for j in range(2)]
                    kq_sb = [sb(f"kq{j}", [128, CH], BF16) for j in range(4)]
                    swp = sb("swp", [128, CH], BF16)
                else:
                    wv_sb = sb("wv_sb", [128, 16, D], BF16)
                    vb_sb = sb("vb_sb", [128, D], F32)
                    v_sb = [sb(f"vsb{j}", [128, CH], BF16) for j in range(4)]
                xnT = [sb(f"xnT{p}", [128, 16, CH], BF16) for p in range(2)]
                xbf = [sb(f"xbf{j}", [128, D], BF16) for j in range(4)]
                junk = sb("junk", [128, D], BF16)
                xn = [sb(f"xn{j}", [128, D], BF16) for j in range(4)]
                sxp = sb("sxp", [128, 4], F32)
                sxxp = sb("sxxp", [128, 4], F32)
                meanp = sb("meanp", [128, 4], F32)
                msqp = sb("msqp", [128, 4], F32)
                mvarp = sb("mvarp", [128, 4], F32)
                stdp = sb("stdp", [128, 4], F32)
                rsp = sb("rsp", [128, 4], F32)
                pst = [ph.enter_context(
                    nc.psum_tensor(f"pst{j}_{kind}", [128, 8, 128], BF16))
                    for j in range(2)]
                psr = [ph.enter_context(
                    nc.psum_tensor(f"psr{j}_{kind}", [128, CH], F32))
                    for j in range(4)]

                c_w = ctr("c_w_" + kind)
                xoff = 0 if kind == "kq" else NT
                c_sq = ctr("c_sq_" + kind)
                c_mn = ctr("c_mn_" + kind)
                c_msq = ctr("c_msq_" + kind)
                c_st1 = ctr("c_st1_" + kind)
                c_std = ctr("c_std_" + kind)
                c_xn = ctr("c_xn_" + kind)
                c_T0 = ctr("c_T0_" + kind)
                c_xnTe = ctr("c_xnTe_" + kind)
                c_rnd = ctr("c_rnd_" + kind)
                c_rev = ctr("c_rev_" + kind)
                if kind == "kq":
                    c_rope = ctr("c_rope_" + kind)
                    tbr = DmaRing(ctr, "tbr_" + kind, 2)

                sch = new_sched()
                gate = spkq.totals() if kind == "v" else []

                kv_view = kvwT.rearrange("(c p) j -> p c j", p=128)
                wload_q = []
                if kind == "kq":
                    qview = qwT.rearrange("(c p) j -> p c j", p=128)
                    for pc in range(4):
                        wload_q.append(lambda h, pc=pc: h.dma_start(
                            wk_sb[:, pc * 4:(pc + 1) * 4, :],
                            kv_view[:, pc * 4:(pc + 1) * 4, 0:D]))
                    wload_q.append(lambda h: h.dma_start(kvb_sb[:, :], kvb2))
                    wload_q.append(lambda h: h.dma_start(qb_sb[:, :], qb2))
                    for pc in range(4):
                        wload_q.append(lambda h, pc=pc: h.dma_start(
                            qw_sb[:, pc * 4:(pc + 1) * 4, :],
                            qview[:, pc * 4:(pc + 1) * 4, :]))
                else:
                    for pc in range(4):
                        wload_q.append(lambda h, pc=pc: h.dma_start(
                            wv_sb[:, pc * 4:(pc + 1) * 4, :],
                            kv_view[:, pc * 4:(pc + 1) * 4, D:2 * D]))
                    wload_q.append(lambda h: h.dma_start(vb_sb[:, :], vb_bc))

                def pump_wload():
                    if wload_q:
                        fn = wload_q.pop(0)
                        sch.add("g", fn, waits=gate if not wl_started else [],
                                incs=[(c_w, 16)])
                        wl_started.append(1)
                wl_started = []
                W_N = 16 * len(wload_q)
                WK_N = 16 * 6 if kind == "kq" else W_N

                if kind == "kq":
                    rounds_in = lambda ch: 16 + (16 if ch < NQCH else 0)
                    ropes_in = lambda ch: 16 + (16 if ch < NQCH else 0)
                else:
                    rounds_in = lambda ch: 16
                    ropes_in = lambda ch: 0
                rounds_before = [0]
                ropes_thru = [0]
                for ch in range(NCH):
                    rounds_before.append(rounds_before[-1] + rounds_in(ch))
                    ropes_thru.append(ropes_thru[-1] + ropes_in(ch))

                def ln_tile(i):
                    s3, s2 = i % 4, i % 4

                    def f_load(h, i=i, s3=s3):
                        return h.dma_start(xbf[s3][:, :],
                                           x_all[i * 128:(i + 1) * 128, :])
                    sch.add("g", f_load,
                            waits=[(c_xn, max(0, i - 3)), (c_base, BASE_N)] + gate,
                            incs=[xldr.inc(xoff + i)])
                    pump_wload()

                    def f_cp(h, s3=s3, s2=s2):
                        return h.activation(junk[:, :], xbf[s3][:, :], AF.Copy,
                                            accum_out=sxp[:, s2:s2 + 1])
                    sch.add("a", f_cp,
                            waits=[xldr.wait_done(xoff + i), (c_st1, max(0, i - 3))])

                    def f_sq(h, s3=s3, s2=s2):
                        return h.activation(junk[:, :], xbf[s3][:, :], AF.Square,
                                            accum_out=sxxp[:, s2:s2 + 1])
                    sch.add("a", f_sq, incs=[(c_sq, 1)])

                    def f_mean(h, s2=s2):
                        return h.tensor_scalar(meanp[:, s2:s2 + 1],
                                               sxp[:, s2:s2 + 1],
                                               1.0 / D, None, ALU.mult)
                    sch.add("v", f_mean,
                            waits=[(c_sq, i + 1), (c_msq, max(0, i - 3))],
                            incs=[(c_mn, 1)])

                    def f_msq(h, s2=s2):
                        return h.activation(msqp[:, s2:s2 + 1],
                                            meanp[:, s2:s2 + 1], AF.Square)
                    sch.add("a", f_msq, waits=[(c_mn, i + 1)], incs=[(c_msq, 1)])

                    def f_var(h, s2=s2):
                        return h.scalar_tensor_tensor(
                            mvarp[:, s2:s2 + 1], sxxp[:, s2:s2 + 1], 1.0 / D,
                            msqp[:, s2:s2 + 1], ALU.mult, ALU.subtract)
                    sch.add("v", f_var, waits=[(c_msq, i + 1)], incs=[(c_st1, 1)])

                    def f_sqrt(h, s2=s2):
                        h.activation(stdp[:, s2:s2 + 1], mvarp[:, s2:s2 + 1],
                                     AF.Ln, bias=eps_ap)
                        return h.activation(rsp[:, s2:s2 + 1], stdp[:, s2:s2 + 1],
                                            AF.Exp, scale=-0.5)
                    sch.add("a", f_sqrt,
                            waits=[(c_st1, i + 1), (c_xn, max(0, i - 3))],
                            incs=[(c_std, 1)])

                    def f_xn(h, s3=s3, s2=s2):
                        return h.tensor_scalar(xn[s3][:, :], xbf[s3][:, :],
                                               meanp[:, s2:s2 + 1],
                                               rsp[:, s2:s2 + 1],
                                               ALU.subtract, ALU.mult)
                    sch.add("v", f_xn,
                            waits=[(c_std, i + 1), (c_T0, max(0, 2 * (i - 4) + 2))],
                            incs=[(c_xn, 1)])

                def t_batch(i, b):
                    ch = i // TPC
                    par = ch % 2
                    st = i % TPC

                    def f_T(h, i=i, b=b):
                        last = None
                        for j in range(8):
                            cc0 = 8 * b + j
                            last = h.transpose(pst[b][:, j, :],
                                               xn[i % 4][:, cc0 * 128:(cc0 + 1) * 128],
                                               ident_sb[:, :])
                        return last
                    w = [(c_xn, i + 1), (c_xnTe, max(0, 2 * (i - 1) + b + 1))]
                    if ch >= 2:
                        w.append((c_rnd, rounds_before[ch - 1]))
                    sch.add("t", f_T, waits=w, incs=[(c_T0, 1)])

                    def f_ev(h, par=par, b=b, st=st):
                        return h.tensor_copy(
                            xnT[par][:, 8 * b:8 * b + 8, st * 128:(st + 1) * 128],
                            pst[b][:, :, :])
                    sch.add("v", f_ev, waits=[(c_T0, 2 * i + b + 1)],
                            incs=[(c_xnTe, 1)])

                def tbl_load(ch):
                    sl = ch % 2
                    w = [(c_rope, ropes_thru[ch - 1])] if ch >= 2 else []

                    def f_c(h, ch=ch, sl=sl):
                        return h.dma_start(tb_sl[sl][:, :, :], costab[:, ch, :, :])
                    sch.add("a", f_c, waits=w, incs=[tbr.inc(ch)])

                state = {"r": 0, "skq": 0, "sv": 0, "rp": 0}

                def do_round(knd, ch, a1, a2=0):
                    par = ch % 2
                    rr = state["r"]
                    pp = rr % 4

                    if knd in ("k", "q"):
                        oc = a1
                        wsb = wk_sb if knd == "k" else qw_sb

                        def f_mm(h, wsb=wsb, oc=oc, par=par, pp=pp):
                            last = None
                            for c in range(16):
                                last = h.matmul(psr[pp][:, :],
                                                wsb[:, c, oc * 128:(oc + 1) * 128],
                                                xnT[par][:, c, :],
                                                start=(c == 0), stop=(c == 15))
                            return last
                    else:
                        st4, eq = a1, a2

                        def f_mm(h, st4=st4, eq=eq, par=par, pp=pp):
                            last = None
                            for c in range(16):
                                last = h.matmul(psr[pp][:, :],
                                                xnT[par][:, c, st4 * 128:(st4 + 1) * 128],
                                                wv_sb[:, c, eq * CH:(eq + 1) * CH],
                                                start=(c == 0), stop=(c == 15))
                            return last
                    wneed = WK_N if knd in ("k", "v") else W_N
                    sch.add("t", f_mm,
                            waits=[(c_xnTe, 8 * (ch + 1)), (c_w, wneed),
                                   (c_rev, max(0, rr - 3))],
                            incs=[(c_rnd, 1)])

                    if knd in ("k", "q"):
                        skq = state["skq"]
                        es = skq % 4
                        bias = (kvb_sb[:, a1:a1 + 1] if knd == "k"
                                else qb_sb[:, a1:a1 + 1])

                        def f_ev(h, es=es, pp=pp, bias=bias):
                            return h.activation(kq_sb[es][:, :], psr[pp][:, :],
                                                AF.Identity, bias=bias)
                        sch.add("a", f_ev,
                                waits=[(c_rnd, rr + 1), spkq.wait_done(skq - 4)],
                                incs=[(c_rev, 1)])

                        sl = ch % 2

                        def f_rope(h, es=es, sl=sl):
                            h.tensor_copy(swp[0:64, :], kq_sb[es][64:128, :])
                            h.tensor_copy(swp[64:128, :], kq_sb[es][0:64, :])
                            h.tensor_mul(swp[:, :], swp[:, :], tb_sl[sl][:, 1, :])
                            h.tensor_mul(kq_sb[es][:, :], kq_sb[es][:, :],
                                         tb_sl[sl][:, 0, :])
                            return h.tensor_add(kq_sb[es][:, :], kq_sb[es][:, :],
                                                swp[:, :])
                        sch.add("v", f_rope,
                                waits=[(c_rev, rr + 1), tbr.wait_done(ch)],
                                incs=[(c_rope, 1)])

                        dst = (kTs[:, a1, ch * CH:(ch + 1) * CH] if knd == "k"
                               else qTs[:, a1, ch * CH:(ch + 1) * CH])

                        def f_sp(h, es=es, dst=dst):
                            return h.dma_start(dst, kq_sb[es][:, :])
                        sch.add("s", f_sp, waits=[(c_rope, state["rp"] + 1)],
                                incs=[spkq.inc(skq)])
                        state["skq"] += 1
                        state["rp"] += 1
                    else:
                        st4, eq = a1, a2
                        sv = state["sv"]
                        es = sv % 4

                        def f_ev(h, es=es, pp=pp, eq=eq):
                            return h.tensor_add(v_sb[es][:, :], psr[pp][:, :],
                                                vb_sb[:, eq * CH:(eq + 1) * CH])
                        sch.add("v", f_ev,
                                waits=[(c_rnd, rr + 1), spv.wait_done(sv - 4)],
                                incs=[(c_rev, 1)])

                        row0 = (ch * TPC + st4) * 128

                        def f_sp(h, es=es, row0=row0, eq=eq):
                            return h.dma_start(
                                vs[row0:row0 + 128, eq * CH:(eq + 1) * CH],
                                v_sb[es][:, :])
                        sch.add("s", f_sp, waits=[(c_rev, rr + 1)],
                                incs=[spv.inc(sv)])
                        state["sv"] += 1
                    state["r"] += 1

                for ch in range(NCH + 1):
                    if ch >= 1:
                        while wload_q:
                            pump_wload()
                    if ch < NCH and kind == "kq":
                        tbl_load(ch)
                    if ch == 0:
                        for st in range(TPC):
                            ln_tile(st)
                        for st in range(TPC):
                            for b in range(2):
                                t_batch(st, b)
                    if ch >= 1:
                        cur = ch - 1
                        pend_ln = ([ch * TPC + st for st in range(TPC)]
                                   if ch < NCH else [])
                        pend_tb = ([(ch * TPC + st, b) for st in range(TPC)
                                    for b in range(2)] if ch < NCH else [])
                        if kind == "kq":
                            rlist = ([("k", cur, oc) for oc in range(16)]
                                     + ([("q", cur, oc) for oc in range(16)]
                                        if cur < NQCH else []))
                        else:
                            rlist = [("v", cur, st4, eq)
                                     for st4 in range(TPC) for eq in range(TPC)]
                        step = max(1, len(rlist) // 16)
                        LEAD = 4
                        slot = 0
                        for rn, r in enumerate(rlist):
                            do_round(r[0], r[1], r[2], r[3] if len(r) > 3 else 0)
                            if rn % step == step - 1:
                                slot += 1
                                if pend_ln:
                                    ln_tile(pend_ln.pop(0))
                                if pend_tb and slot > LEAD:
                                    t_batch(*pend_tb.pop(0))
                        while pend_ln:
                            ln_tile(pend_ln.pop(0))
                        while pend_tb:
                            t_batch(*pend_tb.pop(0))

                _emit(nc, sch)

        if "P0" in phases:
            emit_p0_pass("kq")
            emit_p0_pass("v")
        P0_DONE = spkq.totals() + spv.totals()

        if "A" in phases:
            with ExitStack() as ph:
                sb = lambda n, shp, dt: ph.enter_context(nc.sbuf_tensor(n, shp, dt))
                qT_sb = sb("qT_sb", [128, 16, 512], BF16)
                kT_b = [sb(f"kT_b{j}", [128, 16, 1024], BF16) for j in range(2)]
                v_b = [sb(f"v_b{j}", [128, 8, D], BF16) for j in range(2)]
                o_acc = sb("o_acc", [128, 4, D], F32)
                p_sb = [sb(f"p_sb{j}", [128, 1024], BF16) for j in range(2)]
                pT_sb = [sb(f"pT_sb{j}", [128, 8, 128], BF16) for j in range(2)]
                nm = sb("nm", [128, 4, 2], F32)
                pm = sb("pm", [128, 4, 2], F32)
                lsum = sb("lsum", [128, 4], F32)
                tmaxb = sb("tmaxb", [128, 1], F32)
                ccb = sb("ccb", [128, 2], F32)
                lpart = sb("lpart", [128, 2], F32)
                rlb = sb("rlb", [128, 2], F32)
                rlt = sb("rlt", [128, 1], F32)
                spcr = sb("spcr", [128, 8], F32)
                obf = [sb(f"obf{j}", [128, D], BF16) for j in range(2)]
                psS = ph.enter_context(nc.psum_tensor("psS", [128, 1024], F32))
                psT = ph.enter_context(nc.psum_tensor("psT", [128, 8, 128], BF16))
                psO = ph.enter_context(nc.psum_tensor("psO", [128, D], F32))

                qldr = DmaRing(ctr, "qld", 4)
                kldr = DmaRing(ctr, "kld", 2)
                vldr = DmaRing(ctr, "vld", 2)
                c_sc = ctr("c_sc")
                c_nm = ctr("c_nm")
                c_cc = ctr("c_cc")
                c_exp = ctr("c_exp")
                c_lup = ctr("c_lup")
                c_T = ctr("c_T")
                c_pTe = ctr("c_pTe")
                c_pv = ctr("c_pv")
                c_ofl = ctr("c_ofl")
                c_obf = ctr("c_obf")
                c_rm = ctr("c_rm")
                c_rl = ctr("c_rl")
                c_ms = ctr("c_ms")

                sch = new_sched()

                def f_init(h):
                    h.memset(o_acc[:, :, :], 0.0)
                    h.memset(lsum[:, :], 0.0)
                    h.memset(pm[:, :, :], -BIG)
                    return h.memset(nm[:, :, :], BIG)
                sch.add("v", f_init, incs=[(c_ms, 1)])

                def sc_round(sbk, idx):
                    gidx = sbk * IPB + idx
                    g = idx // 4
                    gg = sbk * NG + g
                    gp = g % 2
                    st = idx % 4
                    g2 = idx % 2

                    def f(h, gp=gp, st=st):
                        last = None
                        for nb in range(2):
                            for c in range(16):
                                last = h.matmul(
                                    psS[:, nb * 512:(nb + 1) * 512],
                                    qT_sb[:, c, st * 128:(st + 1) * 128],
                                    kT_b[gp][:, c, nb * 512:(nb + 1) * 512],
                                    start=(c == 0), stop=(c == 15))
                        return last
                    sch.add("t", f,
                            waits=[qldr.wait_done(sbk * 4 + st), kldr.wait_done(gg),
                                   (c_exp, gidx)],
                            incs=[(c_sc, 1)])

                    def f_rm(h):
                        return h.reduce_max(tmaxb[:, :], psS[:, :], axis=AX)
                    if not DBG_NOFLASH:
                        sch.add("v", f_rm, waits=[(c_sc, gidx + 1)],
                                incs=[(c_rm, 1)])

                    def f_cc(h, st=st, gp=gp, g2=g2):
                        return h.activation(ccb[:, g2:g2 + 1],
                                            pm[:, st, 1 - gp:2 - gp], AF.Exp,
                                            bias=nm[:, st, gp:gp + 1])
                    if not DBG_NOFLASH:
                        sch.add("a", f_cc,
                                waits=[(c_nm, gidx + 1), (c_ofl, max(0, gidx - 1))],
                                incs=[(c_cc, 1)])

                    if DBG_NOFLASH:
                        def f_exp(h, st=st, gp=gp, g2=g2):
                            return h.activation(p_sb[g2][:, :], psS[:, :], AF.Exp,
                                                scale=SCALE,
                                                accum_out=lpart[:, g2:g2 + 1])
                        sch.add("a", f_exp,
                                waits=[(c_sc, gidx + 1),
                                       (c_T, max(0, gidx - 1)),
                                       (c_lup, max(0, gidx - 1))],
                                incs=[(c_exp, 1)])
                    else:
                        def f_exp(h, st=st, gp=gp, g2=g2):
                            return h.activation(p_sb[g2][:, :], psS[:, :], AF.Exp,
                                                bias=nm[:, st, gp:gp + 1], scale=SCALE,
                                                accum_out=lpart[:, g2:g2 + 1])
                        sch.add("a", f_exp,
                                waits=[(c_T, max(0, gidx - 1)),
                                       (c_lup, max(0, gidx - 1))],
                                incs=[(c_exp, 1)])

                def nm_update(sbk, idx):
                    if DBG_NOFLASH:
                        return
                    gidx = sbk * IPB + idx
                    g = idx // 4
                    gp = g % 2
                    st = idx % 4
                    g2 = idx % 2

                    def f_nmu(h, st=st, gp=gp):
                        h.tensor_scalar(nm[:, st, gp:gp + 1], tmaxb[:, :],
                                        -SCALE, nm[:, st, 1 - gp:2 - gp],
                                        ALU.mult, ALU.min)
                        return h.tensor_scalar(pm[:, st, gp:gp + 1], tmaxb[:, :],
                                               SCALE, pm[:, st, 1 - gp:2 - gp],
                                               ALU.mult, ALU.max)
                    sch.add("v", f_nmu,
                            waits=[(c_rm, gidx + 1), (c_ms, c_ms.n),
                                   (c_cc, max(0, gidx - 1))],
                            incs=[(c_nm, 1)])

                def spacer3(h):
                    h.memset(spcr[:, :], 0.0)
                    h.memset(spcr[:, :], 1.0)
                    return h.memset(spcr[:, :], 2.0)

                def t_pv_pTe(sbk, idx):
                    gidx = sbk * IPB + idx
                    g2 = idx % 2

                    def f_T(h, g2=g2):
                        last = None
                        for tc in range(8):
                            last = h.transpose(psT[:, tc, :],
                                               p_sb[g2][:, tc * 128:(tc + 1) * 128],
                                               ident_sb[:, :])
                        return last
                    sch.add("t", f_T, waits=[(c_exp, gidx + 1), (c_pTe, gidx)],
                            incs=[(c_T, 1)])

                    def f_pTe(h, g2=g2):
                        return h.tensor_copy(pT_sb[g2][:, :, :], psT[:, :, :])
                    sch.add("v", f_pTe,
                            waits=[(c_T, gidx + 1), (c_pv, max(0, gidx - 1))],
                            incs=[(c_pTe, 1)])

                def t_pv_a(sbk, idx):
                    gidx = sbk * IPB + idx
                    g2 = idx % 2
                    st = idx % 4

                    cscal = 1.0 if DBG_NOFLASH else ccb[:, g2:g2 + 1]

                    def f_lup(h, st=st, g2=g2, cscal=cscal):
                        return h.scalar_tensor_tensor(
                            lsum[:, st:st + 1], lsum[:, st:st + 1],
                            cscal, lpart[:, g2:g2 + 1],
                            ALU.mult, ALU.add)
                    lw = [(c_exp, gidx + 1)]
                    if not DBG_NOFLASH:
                        lw.append((c_cc, gidx + 1))
                    sch.add("v", f_lup, waits=lw, incs=[(c_lup, 1)])

                def t_pv_b(sbk, idx):
                    gidx = sbk * IPB + idx
                    g = idx // 4
                    gg = sbk * NG + g
                    gp = g % 2
                    st = idx % 4
                    g2 = idx % 2

                    def f_pv(h, g2=g2, gp=gp):
                        last = None
                        for nb in range(4):
                            for tc in range(8):
                                last = h.matmul(
                                    psO[:, nb * 512:(nb + 1) * 512],
                                    pT_sb[g2][:, tc, :],
                                    v_b[gp][:, tc, nb * 512:(nb + 1) * 512],
                                    start=(tc == 0), stop=(tc == 7))
                        return last
                    sch.add("t", f_pv,
                            waits=[(c_pTe, gidx + 1), vldr.wait_done(gg),
                                   (c_ofl, gidx)],
                            incs=[(c_pv, 1)])

                    cscal2 = 1.0 if DBG_NOFLASH else ccb[:, g2:g2 + 1]

                    def f_ofl(h, st=st, g2=g2, cscal2=cscal2):
                        return h.scalar_tensor_tensor(
                            o_acc[:, st, :], o_acc[:, st, :],
                            cscal2, psO[:, :], ALU.mult, ALU.add)
                    sch.add("v", f_ofl, waits=[(c_pv, gidx + 1)],
                            incs=[(c_ofl, 1)])

                for sbk in range(NSB):
                    for g in range(2):
                        gg = sbk * NG + g
                        gp = g % 2

                        def f_k(h, g=g, gp=gp):
                            return h.dma_start(kT_b[gp][:, :, :],
                                               kTs[:, :, g * 1024:(g + 1) * 1024])
                        sch.add("s", f_k,
                                waits=P0_DONE + [(c_sc, max(0, 4 * (gg - 1)))],
                                incs=[kldr.inc(gg)])

                        def f_v(h, g=g, gp=gp):
                            return h.dma_start(
                                v_b[gp][:, :, :],
                                vs[g * 1024:(g + 1) * 1024, :]
                                .rearrange("(tc p) e -> p tc e", p=128))
                        sch.add("g", f_v,
                                waits=P0_DONE + [(c_pv, max(0, 4 * (gg - 1)))],
                                incs=[vldr.inc(gg)])

                    for qp in range(4):
                        def f_q(h, sbk=sbk, qp=qp):
                            return h.dma_start(
                                qT_sb[:, :, qp * 128:(qp + 1) * 128],
                                qTs[:, :, sbk * 512 + qp * 128:
                                    sbk * 512 + (qp + 1) * 128])
                        w = [(c_sc, max(0, (sbk - 1) * IPB + IPB - 4 + qp + 1))]
                        sch.add("a", f_q, waits=P0_DONE + w,
                                incs=[qldr.inc(sbk * 4 + qp)])

                    if sbk > 0:
                        def f_ms(h):
                            if DBG_NOFLASH:
                                h.memset(o_acc[:, :, :], 0.0)
                                h.memset(lsum[:, :], 0.0)
                            h.memset(pm[:, :, :], -BIG)
                            return h.memset(nm[:, :, :], BIG)
                        msw = [(c_exp, IPB * sbk)]
                        if DBG_NOFLASH:
                            msw.append((c_obf, 4 * sbk))
                        sch.add("v", f_ms, waits=msw, incs=[(c_ms, 1)])

                    for g in range(2, NG):
                        gg = sbk * NG + g
                        gp = g % 2

                        def f_k(h, g=g, gp=gp):
                            return h.dma_start(kT_b[gp][:, :, :],
                                               kTs[:, :, g * 1024:(g + 1) * 1024])
                        sch.add("s", f_k,
                                waits=P0_DONE + [(c_sc, max(0, 4 * (gg - 1)))],
                                incs=[kldr.inc(gg)])

                        def f_v(h, g=g, gp=gp):
                            return h.dma_start(
                                v_b[gp][:, :, :],
                                vs[g * 1024:(g + 1) * 1024, :]
                                .rearrange("(tc p) e -> p tc e", p=128))
                        sch.add("g", f_v,
                                waits=P0_DONE + [(c_pv, max(0, 4 * (gg - 1)))],
                                incs=[vldr.inc(gg)])

                    sc_round(sbk, 0)
                    if not DBG_NOFLASH:
                        sch.add("v", spacer3)
                    nm_update(sbk, 0)
                    for idx in range(1, IPB):
                        t_pv_pTe(sbk, idx - 1)
                        sc_round(sbk, idx)
                        t_pv_a(sbk, idx - 1)
                        if not DBG_NOFLASH:
                            sch.add("v", lambda h: h.memset(spcr[:, :], 0.0))
                        nm_update(sbk, idx)
                        t_pv_b(sbk, idx - 1)
                    t_pv_pTe(sbk, IPB - 1)
                    t_pv_a(sbk, IPB - 1)
                    t_pv_b(sbk, IPB - 1)

                    for st in range(4):
                        oi = sbk * 4 + st
                        os2 = oi % 2

                        def f_rl(h, st=st, os2=os2):
                            h.activation(rlt[:, :], lsum[:, st:st + 1], AF.Ln)
                            return h.activation(rlb[:, os2:os2 + 1], rlt[:, :],
                                                AF.Exp, scale=-1.0)
                        sch.add("a", f_rl,
                                waits=[(c_lup, IPB * (sbk + 1)),
                                       (c_obf, max(0, oi - 1))],
                                incs=[(c_rl, 1)])

                        def f_on(h, st=st, os2=os2):
                            if DBG_UNNORM:
                                return h.tensor_scalar(obf[os2][:, :],
                                                       o_acc[:, st, :],
                                                       1.0, None, ALU.mult)
                            return h.tensor_scalar(obf[os2][:, :], o_acc[:, st, :],
                                                   rlb[:, os2:os2 + 1], None,
                                                   ALU.mult)
                        sch.add("v", f_on,
                                waits=[(c_ofl, IPB * (sbk + 1)), (c_rl, oi + 1),
                                       spo.wait_done(oi - 2)],
                                incs=[(c_obf, 1)])

                        def f_sp(h, oi=oi, os2=os2):
                            return h.dma_start(ons[oi * 128:(oi + 1) * 128, :],
                                               obf[os2][:, :])
                        sch.add("a", f_sp, waits=[(c_obf, oi + 1)],
                                incs=[spo.inc(oi)])

                _emit(nc, sch)
        A_DONE = spo.totals()

        if "A2" in phases or "F1" in phases:
            with ExitStack() as af:
                ow_sb = af.enter_context(nc.sbuf_tensor("ow_sb", [128, 16, D], BF16))
                xn2T = af.enter_context(nc.sbuf_tensor("xn2T", [128, 16, HH2], BF16))
                sg_sb = af.enter_context(nc.sbuf_tensor("sg_sb", [128, 64], F32))
                su_sb = af.enter_context(nc.sbuf_tensor("su_sb", [128, 64], F32))

                c_w2 = ctr("c_w2")
                oldr = DmaRing(ctr, "old", 2)
                xld2r = DmaRing(ctr, "xld2", 2)
                c_T2 = ctr("c_T2")
                c_oTe = ctr("c_oTe")
                c_opr = ctr("c_opr")
                c_x2 = ctr("c_x2")
                c_sq2 = ctr("c_sq2")
                c_mn2 = ctr("c_mn2")
                c_msq2 = ctr("c_msq2")
                c_st2 = ctr("c_st2")
                c_std2 = ctr("c_std2")
                c_xn2 = ctr("c_xn2")
                c_T2b = ctr("c_T2b")
                c_xn2Te = ctr("c_xn2Te")
                wgr = DmaRing(ctr, "wgr", 2)
                wur = DmaRing(ctr, "wur", 2)
                c_gr = ctr("c_gr")
                c_ur = ctr("c_ur")
                c_gact = ctr("c_gact")
                c_us = ctr("c_us")
                c_fu = ctr("c_fu")

                XN2TE_HALF = 2 * TPH if "A2" in phases else 0

                first_af = True
                for hh in range(2):
                    if "A2" in phases:
                        with ExitStack() as ph:
                            sb = lambda n, shp, dt: ph.enter_context(
                                nc.sbuf_tensor(f"{n}_a2{hh}", shp, dt))
                            oT_sb = sb("oT_sb", [128, 32, 128], BF16)
                            ob_in = [sb(f"ob_in{j}", [128, D], BF16) for j in range(2)]
                            x2sb = [sb(f"x2sb{j}", [128, D], F32) for j in range(2)]
                            x2bf = [sb(f"x2bf{j}", [128, D], BF16) for j in range(2)]
                            xn2 = [sb(f"xn2_{j}", [128, D], BF16) for j in range(2)]
                            junk2 = sb("junk2", [128, D], BF16)
                            sxp2 = sb("sxp2", [128, 2], F32)
                            sxxp2 = sb("sxxp2", [128, 2], F32)
                            meanp2 = sb("meanp2", [128, 2], F32)
                            msqp2 = sb("msqp2", [128, 2], F32)
                            mvarp2 = sb("mvarp2", [128, 2], F32)
                            stdp2 = sb("stdp2", [128, 2], F32)
                            rsp2 = sb("rsp2", [128, 2], F32)
                            pstA = [ph.enter_context(
                                nc.psum_tensor(f"pstA{j}_a2{hh}", [128, 8, 128],
                                               BF16)) for j in range(2)]
                            pstB = [ph.enter_context(
                                nc.psum_tensor(f"pstB{j}_a2{hh}", [128, 8, 128],
                                               BF16)) for j in range(2)]
                            psAt = [ph.enter_context(
                                nc.psum_tensor(f"psAt{j}_a2{hh}", [128, 1024], F32))
                                for j in range(2)]

                            sch = new_sched()
                            if first_af:
                                for qq in range(4):
                                    def f_ow(h, qq=qq):
                                        return h.dma_start(
                                            ow_sb[:, 4 * qq:4 * qq + 4, :],
                                            owT.rearrange("(c p) j -> p c j", p=128)
                                            [:, 4 * qq:4 * qq + 4, :])
                                    sch.add("g", f_ow,
                                            waits=A_DONE if qq == 0 else [],
                                            incs=[(c_w2, 16)])
                                sch.add("g", lambda h: h.dma_start(sg_sb[:, :], sg2),
                                        incs=[(c_w2, 16)])
                                sch.add("g", lambda h: h.dma_start(su_sb[:, :], su2),
                                        incs=[(c_w2, 16)])
                                first_af = False
                            W2_N = c_w2.n

                            def emit_xn2T(tt2):
                                tg2 = hh * TPH + tt2
                                t22 = tt2 % 2
                                for b, pst2 in ((0, pstA[tt2 % 2]),
                                                (1, pstB[tt2 % 2])):
                                    def f_T(h, t22=t22, b=b, pst2=pst2):
                                        last = None
                                        for j in range(8):
                                            cc0 = 8 * b + j
                                            last = h.transpose(
                                                pst2[:, j, :],
                                                xn2[t22][:, cc0 * 128:(cc0 + 1) * 128],
                                                ident_sb[:, :])
                                        return last
                                    sch.add("t", f_T,
                                            waits=[(c_xn2, tg2 + 1),
                                                   (c_oTe, 2 * tg2 + b + 1)],
                                            incs=[(c_T2b, 1)])

                                    def f_ev(h, b=b, pst2=pst2, tt2=tt2, hh=hh):
                                        return h.tensor_copy(
                                            xn2T[:, 8 * b:8 * b + 8,
                                                 tt2 * 128:(tt2 + 1) * 128],
                                            pst2[:, :, :])
                                    sch.add("v", f_ev,
                                            waits=[(c_T2b, 2 * tg2 + b + 1),
                                                   (c_gr, 64 * hh),
                                                   (c_ur, 64 * hh)],
                                            incs=[(c_xn2Te, 1)])

                            for tt in range(TPH):
                                tg = hh * TPH + tt
                                t2 = tt % 2

                                def f_ol(h, tg=tg, t2=t2):
                                    return h.dma_start(
                                        ob_in[t2][:, :],
                                        ons[tg * 128:(tg + 1) * 128, :])
                                f1gate = ([(cc, 16 * 32 * hh) for cc in spf.c]
                                          if "F1" in phases and hh else [])
                                sch.add("s", f_ol,
                                        waits=A_DONE + f1gate
                                        + [(c_T2, max(0, 2 * (tg - 1)))],
                                        incs=[oldr.inc(tg)])

                                for b, pst2 in ((0, pstA[tt % 2]), (1, pstB[tt % 2])):
                                    def f_T(h, t2=t2, b=b, pst2=pst2):
                                        last = None
                                        for j in range(8):
                                            cc0 = 8 * b + j
                                            last = h.transpose(
                                                pst2[:, j, :],
                                                ob_in[t2][:, cc0 * 128:(cc0 + 1) * 128],
                                                ident_sb[:, :])
                                        return last
                                    sch.add("t", f_T,
                                            waits=[oldr.wait_done(tg),
                                                   (c_xn2Te, max(0, 2 * (tg - 2) + b + 1))],
                                            incs=[(c_T2, 1)])

                                    def f_ev(h, t2=t2, b=b, pst2=pst2):
                                        return h.tensor_copy(
                                            oT_sb[:, t2 * 16 + 8 * b:t2 * 16 + 8 * b + 8, :],
                                            pst2[:, :, :])
                                    sch.add("v", f_ev,
                                            waits=[(c_T2, 2 * tg + b + 1),
                                                   (c_opr, max(0, 2 * (tg - 1)))],
                                            incs=[(c_oTe, 1)])

                                def f_xl(h, tg=tg, t2=t2):
                                    return h.dma_start(
                                        x2sb[t2][:, :],
                                        x_all[tg * 128:(tg + 1) * 128, :])
                                sch.add("s", f_xl,
                                        waits=A_DONE + f1gate
                                        + [(c_sq2, max(0, tg - 1)),
                                           spx2.wait_done(tg - 2)],
                                        incs=[xld2r.inc(tg)])

                                for rr in range(2):
                                    for nb in range(2):
                                        for qq in range(2):
                                            def f_pr(h, t2=t2, rr=rr, nb=nb, qq=qq):
                                                last = None
                                                for c in range(8 * qq, 8 * qq + 8):
                                                    last = h.matmul(
                                                        psAt[rr][:, nb * 512:(nb + 1) * 512],
                                                        oT_sb[:, t2 * 16 + c, :],
                                                        ow_sb[:, c, rr * 1024 + nb * 512:
                                                              rr * 1024 + (nb + 1) * 512],
                                                        start=(c == 0), stop=(c == 15))
                                                return last
                                            w_pr = [(c_w2, 32 * (qq + 1))]
                                            if nb == 0 and qq == 0:
                                                w_pr += [(c_oTe, 2 * tg + 2),
                                                         (c_x2, max(0, 2 * tg + rr - 1))]
                                            sch.add("t", f_pr, waits=w_pr,
                                                    incs=[(c_opr, 1)]
                                                    if (nb == 1 and qq == 1) else [])

                                    def f_x2(h, t2=t2, rr=rr):
                                        return h.tensor_add(
                                            x2sb[t2][:, rr * 1024:(rr + 1) * 1024],
                                            psAt[rr][:, :],
                                            x2sb[t2][:, rr * 1024:(rr + 1) * 1024])
                                    sch.add("v", f_x2,
                                            waits=[(c_opr, 2 * tg + rr + 1),
                                                   xld2r.wait_done(tg)],
                                            incs=[(c_x2, 1)])

                                if tt >= 1:
                                    emit_xn2T(tt - 1)

                                def f_spx(h, tg=tg, t2=t2):
                                    return h.dma_start(
                                        x2s[tg * 128:(tg + 1) * 128, :],
                                        x2sb[t2][:, :])
                                sch.add("a", f_spx, waits=[(c_x2, 2 * tg + 2)],
                                        incs=[spx2.inc(tg)])

                                def f_cp(h, t2=t2):
                                    return h.activation(
                                        x2bf[t2][:, :], x2sb[t2][:, :], AF.Copy,
                                        accum_out=sxp2[:, t2:t2 + 1])
                                sch.add("a", f_cp,
                                        waits=[(c_x2, 2 * tg + 2),
                                               (c_xn2, max(0, tg - 1)),
                                               (c_st2, max(0, tg - 1))])

                                def f_sq(h, t2=t2):
                                    return h.activation(
                                        junk2[:, :], x2sb[t2][:, :], AF.Square,
                                        accum_out=sxxp2[:, t2:t2 + 1])
                                sch.add("a", f_sq, incs=[(c_sq2, 1)])

                                def f_mean(h, t2=t2):
                                    return h.tensor_scalar(
                                        meanp2[:, t2:t2 + 1], sxp2[:, t2:t2 + 1],
                                        1.0 / D, None, ALU.mult)
                                sch.add("v", f_mean,
                                        waits=[(c_sq2, tg + 1),
                                               (c_std2, max(0, tg - 1))],
                                        incs=[(c_mn2, 1)])

                                def f_msq(h, t2=t2):
                                    return h.activation(
                                        msqp2[:, t2:t2 + 1], meanp2[:, t2:t2 + 1],
                                        AF.Square)
                                sch.add("a", f_msq, waits=[(c_mn2, tg + 1)],
                                        incs=[(c_msq2, 1)])

                                def f_var(h, t2=t2):
                                    return h.scalar_tensor_tensor(
                                        mvarp2[:, t2:t2 + 1], sxxp2[:, t2:t2 + 1],
                                        1.0 / D, msqp2[:, t2:t2 + 1],
                                        ALU.mult, ALU.subtract)
                                sch.add("v", f_var, waits=[(c_msq2, tg + 1)],
                                        incs=[(c_st2, 1)])

                                def f_sqrt(h, t2=t2):
                                    h.activation(
                                        stdp2[:, t2:t2 + 1], mvarp2[:, t2:t2 + 1],
                                        AF.Ln, bias=eps_ap)
                                    return h.activation(
                                        rsp2[:, t2:t2 + 1], stdp2[:, t2:t2 + 1],
                                        AF.Exp, scale=-0.5)
                                sch.add("a", f_sqrt,
                                        waits=[(c_st2, tg + 1),
                                               (c_xn2, max(0, tg - 1))],
                                        incs=[(c_std2, 1)])

                                def f_xn(h, t2=t2):
                                    return h.tensor_scalar(
                                        xn2[t2][:, :], x2bf[t2][:, :],
                                        meanp2[:, t2:t2 + 1], rsp2[:, t2:t2 + 1],
                                        ALU.subtract, ALU.mult)
                                sch.add("v", f_xn,
                                        waits=[(c_std2, tg + 1),
                                               (c_T2b, max(0, 2 * (tg - 1)))],
                                        incs=[(c_xn2, 1)])
                            emit_xn2T(TPH - 1)
                            _emit(nc, sch)

                    if "F1" in phases:
                        with ExitStack() as ph:
                            sb = lambda n, shp, dt: ph.enter_context(
                                nc.sbuf_tensor(f"{n}_f1{hh}", shp, dt))
                            wg_sb = [sb(f"wg_sb{j}", [128, D], BF16) for j in range(2)]
                            wu_sb = [sb(f"wu_sb{j}", [128, D], BF16) for j in range(2)]
                            gact = [sb(f"gact{j}", [128, HH2], BF16) for j in range(2)]
                            usb = [sb(f"usb{j}", [128, HH2], BF16) for j in range(2)]
                            fuse = [sb(f"fuse{j}", [128, HH2], BF16) for j in range(2)]
                            psG = ph.enter_context(
                                nc.psum_tensor(f"psG_f1{hh}", [128, HH2], F32))
                            psU = ph.enter_context(
                                nc.psum_tensor(f"psU_f1{hh}", [128, HH2], F32))

                            sch = new_sched()
                            if first_af:
                                sch.add("g", lambda h: h.dma_start(sg_sb[:, :], sg2),
                                        incs=[(c_w2, 16)])
                                sch.add("g", lambda h: h.dma_start(su_sb[:, :], su2),
                                        incs=[(c_w2, 16)])
                                first_af = False
                            WF_N = c_w2.n

                            for ic in range(64):
                                ig = hh * 64 + ic
                                i2 = ic % 2

                                phase_gate = (
                                    [(c_xn2Te, XN2TE_HALF * (hh + 1))]
                                    + [(cc, 16 * (TPH // 2) * (hh + 1))
                                       for cc in spx2.c]
                                    if "A2" in phases else [])

                                def f_lg(h, ic=ic, i2=i2):
                                    return h.dma_start(wg_sb[i2][:, :], wgt[ic, :, :])
                                sch.add("g", f_lg,
                                        waits=[(c_gr, max(0, ig - 1))] + phase_gate,
                                        incs=[wgr.inc(ig)])

                                def f_lu(h, ic=ic, i2=i2):
                                    return h.dma_start(wu_sb[i2][:, :], wut[ic, :, :])
                                sch.add("g", f_lu, waits=[(c_ur, max(0, ig - 1))],
                                        incs=[wur.inc(ig)])

                                def f_g(h, i2=i2):
                                    last = None
                                    for nb in range(HH2 // 512):
                                        for c in range(16):
                                            last = h.matmul(
                                                psG[:, nb * 512:(nb + 1) * 512],
                                                wg_sb[i2][:, c * 128:(c + 1) * 128],
                                                xn2T[:, c, nb * 512:(nb + 1) * 512],
                                                start=(c == 0), stop=(c == 15))
                                    return last
                                sch.add("t", f_g,
                                        waits=[wgr.wait_done(ig),
                                               (c_xn2Te, XN2TE_HALF * (hh + 1)),
                                               (c_gact, ig)],
                                        incs=[(c_gr, 1)])

                                def f_u(h, i2=i2):
                                    last = None
                                    for nb in range(HH2 // 512):
                                        for c in range(16):
                                            last = h.matmul(
                                                psU[:, nb * 512:(nb + 1) * 512],
                                                wu_sb[i2][:, c * 128:(c + 1) * 128],
                                                xn2T[:, c, nb * 512:(nb + 1) * 512],
                                                start=(c == 0), stop=(c == 15))
                                    return last
                                sch.add("t", f_u,
                                        waits=[wur.wait_done(ig), (c_us, ig)],
                                        incs=[(c_ur, 1)])

                                def f_ga(h, ic=ic, i2=i2):
                                    return h.activation(gact[i2][:, :], psG[:, :],
                                                        AF.Silu,
                                                        scale=sg_sb[:, ic:ic + 1])
                                sch.add("a", f_ga,
                                        waits=[(c_gr, ig + 1), (c_w2, WF_N),
                                               (c_fu, max(0, ig - 1))],
                                        incs=[(c_gact, 1)])

                                def f_us(h, ic=ic, i2=i2):
                                    return h.tensor_scalar(usb[i2][:, :], psU[:, :],
                                                           su_sb[:, ic:ic + 1],
                                                           None, ALU.mult)
                                sch.add("v", f_us,
                                        waits=[(c_ur, ig + 1), (c_w2, WF_N),
                                               (c_fu, max(0, ig - 1))],
                                        incs=[(c_us, 1)])

                                def f_fu(h, i2=i2):
                                    return h.tensor_mul(fuse[i2][:, :],
                                                        gact[i2][:, :],
                                                        usb[i2][:, :])
                                sch.add("v", f_fu,
                                        waits=[(c_gact, ig + 1),
                                               spf.wait_done(ig - 2)],
                                        incs=[(c_fu, 1)])

                                def f_sp(h, ic=ic, i2=i2, hh=hh):
                                    ntl = HH2 // 128
                                    return h.dma_start(
                                        fs[:, hh * ntl:(hh + 1) * ntl, ic, :],
                                        fuse[i2][:, :].rearrange(
                                            "p (t s) -> p t s", s=128))
                                sch.add("s", f_sp, waits=[(c_fu, ig + 1)],
                                        incs=[spf.inc(ig)])
                            _emit(nc, sch)
        AF_DONE = spx2.totals() + spf.totals()

        if "F2" in phases:
            with ExitStack() as ph:
                sb = lambda n, shp, dt: ph.enter_context(nc.sbuf_tensor(n, shp, dt))
                wd_sb = sb("wd_sb", [128, 64, 1024], BF16)
                fuse_st = [sb(f"fuse_st{j}", [128, 64, 128], BF16) for j in range(3)]
                x2t = [sb(f"x2t{j}", [128, 1024], F32) for j in range(2)]
                ty = [sb(f"ty{j}", [128, 1024], F32) for j in range(2)]
                sd_sb = sb("sd_sb", [128, D], F32)
                psD = [ph.enter_context(nc.psum_tensor(f"psD{j}", [128, 1024], F32))
                       for j in range(2)]

                c_wd = ctr("c_wd")
                fldr = DmaRing(ctr, "fld", 3)
                x2ldr = DmaRing(ctr, "x2ld", 2)
                c_dr = ctr("c_dr")
                c_dev = ctr("c_dev")

                sch = new_sched()
                sch.add("g", lambda h: h.dma_start(sd_sb[:, :], sd_bc),
                        incs=[(c_wd, 16)])

                for eh in range(2):
                    for qq in range(4):
                        def f_wd(h, eh=eh, qq=qq):
                            return h.dma_start(
                                wd_sb[:, 16 * qq:16 * qq + 16, :],
                                wd.rearrange("(a p) e -> p a e", p=128)
                                [:, 16 * qq:16 * qq + 16,
                                 eh * 1024:(eh + 1) * 1024])
                        sch.add("g", f_wd,
                                waits=[(c_dr, NST * eh)] if qq == 0 else [],
                                incs=[(c_wd, 16)])

                    for st in range(NST):
                        ii = eh * NST + st
                        s2 = st % 2
                        sf3 = ii % 3

                        def f_fl(h, st=st, sf3=sf3):
                            return h.dma_start(fuse_st[sf3][:, :, :],
                                               fs[:, st, :, :])
                        sch.add("s", f_fl,
                                waits=AF_DONE + [(c_dr, max(0, ii - 2))],
                                incs=[fldr.inc(ii)])

                        def f_xl(h, st=st, eh=eh, s2=s2):
                            return h.dma_start(
                                x2t[s2][:, :],
                                x2s[st * 128:(st + 1) * 128,
                                    eh * 1024:(eh + 1) * 1024])
                        sch.add("a", f_xl,
                                waits=AF_DONE + [(c_dev, max(0, ii - 1))],
                                incs=[x2ldr.inc(ii)])

                        for nb in range(2):
                            for qq in range(4):
                                def f_mm(h, s2=s2, sf3=sf3, nb=nb, qq=qq):
                                    last = None
                                    for ic in range(16 * qq, 16 * qq + 16):
                                        last = h.matmul(
                                            psD[s2][:, nb * 512:(nb + 1) * 512],
                                            fuse_st[sf3][:, ic, :],
                                            wd_sb[:, ic, nb * 512:(nb + 1) * 512],
                                            start=(ic == 0), stop=(ic == 63))
                                    return last
                                w_mm = [(c_wd, 16 * (2 + 4 * eh + qq))]
                                if nb == 0 and qq == 0:
                                    w_mm += [fldr.wait_done(ii),
                                             (c_dev, max(0, ii - 1))]
                                sch.add("t", f_mm, waits=w_mm,
                                        incs=[(c_dr, 1)]
                                        if (nb == 1 and qq == 3) else [])

                        def f_ev(h, eh=eh, s2=s2):
                            h.tensor_mul(ty[s2][:, :], psD[s2][:, :],
                                         sd_sb[:, eh * 1024:(eh + 1) * 1024])
                            return h.tensor_add(ty[s2][:, :], ty[s2][:, :],
                                                x2t[s2][:, :])
                        sch.add("v", f_ev,
                                waits=[(c_dr, ii + 1), x2ldr.wait_done(ii),
                                       spy.wait_done(ii - 2)],
                                incs=[(c_dev, 1)])

                        def f_sp(h, st=st, eh=eh, s2=s2):
                            return h.dma_start(
                                y[st * 128:(st + 1) * 128,
                                  eh * 1024:(eh + 1) * 1024],
                                ty[s2][:, :])
                        sch.add("a", f_sp, waits=[(c_dev, ii + 1)],
                                incs=[spy.inc(ii)])

                sch.add("a", lambda h: h.nop(), waits=spy.totals())
                _emit(nc, sch)

        tail = Sched()
        tail_waits = (spkq.totals() + spv.totals() + spo.totals()
                      + spx2.totals() + spf.totals() + spy.totals())
        tail.add("a", lambda h: h.nop(), waits=tail_waits)
        tail.add("s", lambda h: h.nop(), waits=tail_waits)
        _emit(nc, tail)

    return nc



def _rope_tables():
    f = np.arange(64, dtype=np.float64)
    inv = 10000.0 ** (-f / 64.0)
    pos = np.arange(S, dtype=np.float64)
    ang = inv[:, None] * pos[None, :]
    cos = np.cos(ang)
    sin = np.sin(ang)
    return (np.concatenate([cos, cos], axis=0),
            np.concatenate([-sin, sin], axis=0))


def host_prepare(inputs):
    x = np.asarray(inputs["L0_attn_x"], dtype=np.float32)
    qw = np.asarray(inputs["L0_attn_q_w"], dtype=np.float32)
    qb = np.asarray(inputs["L0_attn_q_b"], dtype=np.float32)
    kvw = np.asarray(inputs["L0_attn_kv_w"], dtype=np.float32)
    kvb = np.asarray(inputs["L0_attn_kv_b"], dtype=np.float32)
    ow = np.asarray(inputs["L0_attn_o_w"], dtype=np.float32)
    wg = np.asarray(inputs["L0_ffn_w_gate"], dtype=np.int8)
    wu = np.asarray(inputs["L0_ffn_w_up"], dtype=np.int8)
    wdn = np.asarray(inputs["L0_ffn_w_down"], dtype=np.int8)
    sg = np.asarray(inputs["L0_ffn_s_gate"], dtype=np.float32)
    su = np.asarray(inputs["L0_ffn_s_up"], dtype=np.float32)
    sd = np.asarray(inputs["L0_ffn_s_down"], dtype=np.float32)

    cosT64, sinT64 = _rope_tables()

    shared = dict(
        qwT=np.ascontiguousarray(qw.T),
        kvwT=np.ascontiguousarray(kvw.T),
        owT=np.ascontiguousarray(ow.T),
        wgt=np.ascontiguousarray(
            wg.reshape(16, 128, 64, 128).transpose(2, 1, 0, 3)).reshape(64, 128, D),
        wut=np.ascontiguousarray(
            wu.reshape(16, 128, 64, 128).transpose(2, 1, 0, 3)).reshape(64, 128, D),
        wd=wdn,
        qb2=np.ascontiguousarray(qb.reshape(16, 128).T),
        kvb2=np.ascontiguousarray(kvb.reshape(32, 128).T),
        sg2=np.ascontiguousarray(sg.reshape(64, 128).T),
        su2=np.ascontiguousarray(su.reshape(64, 128).T),
        vb_bc=np.ascontiguousarray(np.broadcast_to(kvb[D:], (128, D))),
        sd_bc=np.ascontiguousarray(np.broadcast_to(sd, (128, D))),
        identt=np.eye(128, dtype=ml_dtypes.bfloat16),
        constt=np.repeat(np.array([[1e-5, 0.0, 0.0, 0.0]], np.float32), 128, 0),
    )

    def _mk_tab(c64, s64):
        t = np.stack([c64.reshape(128, S // CH, CH),
                      s64.reshape(128, S // CH, CH)], axis=2)
        return np.ascontiguousarray(t).astype(ml_dtypes.bfloat16)

    in_maps = []
    for c in range(8):
        b, h = c // 2, c % 2
        m = dict(shared)
        if h == 0:
            m["x_all"] = np.ascontiguousarray(x[b])
            m["costab"] = _mk_tab(cosT64, sinT64)
        else:
            perm = np.concatenate([np.arange(HS, S), np.arange(0, HS)])
            m["x_all"] = np.ascontiguousarray(x[b][perm])
            m["costab"] = _mk_tab(np.ascontiguousarray(cosT64[:, perm]),
                                  np.ascontiguousarray(sinT64[:, perm]))
        in_maps.append(m)
    return in_maps


def kernel(**inputs):
    in_maps = host_prepare(inputs)
    nc = build_program()
    res = run_bass_kernel_spmd(nc, in_maps, core_ids=list(range(8)))
    out = np.empty((4, S, D), dtype=np.float32)
    for c in range(8):
        b, h = c // 2, c % 2
        out[b, h * HS:(h + 1) * HS, :] = np.asarray(res.results[c]["y"],
                                                    dtype=np.float32)
    return out


# version 16 (final)
# speedup vs baseline: 1.1832x; 1.0223x over previous
import numpy as np
import ml_dtypes
from contextlib import ExitStack

import concourse.bass as bass
import concourse.mybir as mybir
from concourse.bass_utils import run_bass_kernel_spmd

F32 = mybir.dt.float32
BF16 = mybir.dt.bfloat16
I8 = mybir.dt.int8
AF = mybir.ActivationFunctionType
ALU = mybir.AluOpType
AX = mybir.AxisListType.X

D = 2048
S = 8192
HS = 4096
INTER = 8192
NT = S // 128
CH = 512
NCH = S // CH
TPC = CH // 128
NQCH = HS // CH
SCALE = float(128 ** -0.5)
NG = S // 1024
NSB = HS // 512
IPB = NG * 4
TPH = HS // 256
HH2 = HS // 2
NST = HS // 128
BIG = 1.0e30
DBG_NOFLASH = False
DBG_UNNORM = False


class Ctr:
    def __init__(self, nc, ctx, name):
        self.sem = ctx.enter_context(nc.semaphore(name))
        self.n = 0


class DmaRing:

    def __init__(self, ctr, name, depth):
        self.c = [ctr(f"{name}{j}") for j in range(depth)]
        self.depth = depth

    def inc(self, i):
        return (self.c[i % self.depth], 16)

    def wait_done(self, i):
        if i < 0:
            return None
        return (self.c[i % self.depth], 16 * (i // self.depth + 1))

    def totals(self):
        return [(c, c.n) for c in self.c if c.n > 0]


class Sched:
    def __init__(self):
        self.q = {e: [] for e in "tvasg"}

    def add(self, eng, fn, waits=(), incs=()):
        w = [cv for cv in waits if cv is not None and cv[1] > 0]
        for c, k in incs:
            c.n += k
        self.q[eng].append((fn, w, list(incs)))


def _emit(nc, sched):
    with nc.Block() as blk:
        def runner(name):
            def run(h):
                for fn, waits, incs in sched.q[name]:
                    for c, v in waits:
                        h.wait_ge(c.sem, v)
                    last = fn(h)
                    for c, k in incs:
                        last.then_inc(c.sem, k)
            return run
        blk.tensor(runner("t"))
        blk.vector(runner("v"))
        blk.scalar(runner("a"))
        blk.sync(runner("s"))
        blk.gpsimd(runner("g"))


def build_program(phases=("P0", "A", "A2", "F1", "F2"), expose=None):
    expose = expose or {}
    nc = bass.Bass("TRN2", target_bir_lowering=False)

    def din(name, shape, dtype):
        return nc.dram_tensor(name, shape, dtype, kind="ExternalInput").ap()

    def dscr(name, shape, dtype):
        if name in expose:
            kind = "ExternalInput" if expose[name] == "in" else "ExternalOutput"
            return nc.dram_tensor(name, shape, dtype, kind=kind).ap()
        return nc.dram_tensor(name, shape, dtype).ap()

    x_all = din("x_all", [S, D], F32)
    qwT = din("qwT", [D, D], F32)
    kvwT = din("kvwT", [D, 2 * D], F32)
    owT = din("owT", [D, D], F32)
    wgt = din("wgt", [64, 128, D], I8)
    wut = din("wut", [64, 128, D], I8)
    wd = din("wd", [INTER, D], I8)
    qb2 = din("qb2", [128, 16], F32)
    kvb2 = din("kvb2", [128, 32], F32)
    sg2 = din("sg2", [128, 64], F32)
    su2 = din("su2", [128, 64], F32)
    vb_bc = din("vb_bc", [128, D], F32)
    sd_bc = din("sd_bc", [128, D], F32)
    costab = din("costab", [128, S // CH, 2, CH], BF16)
    identt = din("identt", [128, 128], BF16)
    constt = din("constt", [128, 4], F32)

    kTs = dscr("kTs", [128, 16, S], BF16)
    qTs = dscr("qTs", [128, 16, HS], BF16)
    vs = dscr("vs", [S, D], BF16)
    ons = dscr("ons", [HS, D], BF16)
    x2s = dscr("x2s", [HS, D], F32)
    fs = dscr("fs", [128, NST, 64, 128], BF16)
    y = nc.dram_tensor("y", [HS, D], F32, kind="ExternalOutput").ap()

    with ExitStack() as top:
        ident_sb = top.enter_context(nc.sbuf_tensor("ident_sb", [128, 128], BF16))
        const_sb = top.enter_context(nc.sbuf_tensor("const_sb", [128, 4], F32))
        eps_ap = const_sb[:, 0:1]
        ctr = lambda name: Ctr(nc, top, name)

        c_base = ctr("c_base")
        spkq = DmaRing(ctr, "spkq", 4)
        spv = DmaRing(ctr, "spv", 4)
        xldr = DmaRing(ctr, "xld", 4)
        spo = DmaRing(ctr, "spo", 2)
        spx2 = DmaRing(ctr, "spx2", 2)
        spf = DmaRing(ctr, "spf", 2)
        spy = DmaRing(ctr, "spy", 2)

        base = Sched()
        base.add("g", lambda h: h.dma_start(ident_sb[:, :], identt), incs=[(c_base, 16)])
        base.add("g", lambda h: h.dma_start(const_sb[:, :], constt), incs=[(c_base, 16)])
        BASE_N = c_base.n
        _first = [base]

        def new_sched():
            if _first:
                return _first.pop()
            return Sched()

        def emit_p0_pass(kind):
            with ExitStack() as ph:
                sb = lambda n, shp, dt: ph.enter_context(
                    nc.sbuf_tensor(n + "_" + kind, shp, dt))
                if kind == "kq":
                    qw_sb = sb("qw_sb", [128, 16, D], BF16)
                    wk_sb = sb("wk_sb", [128, 16, D], BF16)
                    qb_sb = sb("qb_sb", [128, 16], F32)
                    kvb_sb = sb("kvb_sb", [128, 32], F32)
                    tb_sl = [sb(f"tb{j}", [128, 2, CH], BF16) for j in range(2)]
                    kq_sb = [sb(f"kq{j}", [128, CH], BF16) for j in range(4)]
                    swp = sb("swp", [128, CH], BF16)
                else:
                    wv_sb = sb("wv_sb", [128, 16, D], BF16)
                    vb_sb = sb("vb_sb", [128, D], F32)
                    v_sb = [sb(f"vsb{j}", [128, CH], BF16) for j in range(4)]
                xnT = [sb(f"xnT{p}", [128, 16, CH], BF16) for p in range(2)]
                xbf = [sb(f"xbf{j}", [128, D], BF16) for j in range(4)]
                junk = sb("junk", [128, D], BF16)
                xn = [sb(f"xn{j}", [128, D], BF16) for j in range(4)]
                sxp = sb("sxp", [128, 4], F32)
                sxxp = sb("sxxp", [128, 4], F32)
                meanp = sb("meanp", [128, 4], F32)
                msqp = sb("msqp", [128, 4], F32)
                mvarp = sb("mvarp", [128, 4], F32)
                stdp = sb("stdp", [128, 4], F32)
                rsp = sb("rsp", [128, 4], F32)
                pst = [ph.enter_context(
                    nc.psum_tensor(f"pst{j}_{kind}", [128, 8, 128], BF16))
                    for j in range(2)]
                psr = [ph.enter_context(
                    nc.psum_tensor(f"psr{j}_{kind}", [128, CH], F32))
                    for j in range(4)]

                c_w = ctr("c_w_" + kind)
                xoff = 0 if kind == "kq" else NT
                c_sq = ctr("c_sq_" + kind)
                c_mn = ctr("c_mn_" + kind)
                c_msq = ctr("c_msq_" + kind)
                c_st1 = ctr("c_st1_" + kind)
                c_std = ctr("c_std_" + kind)
                c_xn = ctr("c_xn_" + kind)
                c_T0 = ctr("c_T0_" + kind)
                c_xnTe = ctr("c_xnTe_" + kind)
                c_rnd = ctr("c_rnd_" + kind)
                c_rev = ctr("c_rev_" + kind)
                if kind == "kq":
                    c_rope = ctr("c_rope_" + kind)
                    tbr = DmaRing(ctr, "tbr_" + kind, 2)

                sch = new_sched()
                gate = spkq.totals() if kind == "v" else []

                kv_view = kvwT.rearrange("(c p) j -> p c j", p=128)
                wload_q = []
                if kind == "kq":
                    qview = qwT.rearrange("(c p) j -> p c j", p=128)
                    for pc in range(4):
                        wload_q.append(lambda h, pc=pc: h.dma_start(
                            wk_sb[:, pc * 4:(pc + 1) * 4, :],
                            kv_view[:, pc * 4:(pc + 1) * 4, 0:D]))
                    wload_q.append(lambda h: h.dma_start(kvb_sb[:, :], kvb2))
                    wload_q.append(lambda h: h.dma_start(qb_sb[:, :], qb2))
                    for pc in range(4):
                        wload_q.append(lambda h, pc=pc: h.dma_start(
                            qw_sb[:, pc * 4:(pc + 1) * 4, :],
                            qview[:, pc * 4:(pc + 1) * 4, :]))
                else:
                    for pc in range(4):
                        wload_q.append(lambda h, pc=pc: h.dma_start(
                            wv_sb[:, pc * 4:(pc + 1) * 4, :],
                            kv_view[:, pc * 4:(pc + 1) * 4, D:2 * D]))
                    wload_q.append(lambda h: h.dma_start(vb_sb[:, :], vb_bc))

                def pump_wload():
                    if wload_q:
                        fn = wload_q.pop(0)
                        sch.add("g", fn, waits=gate if not wl_started else [],
                                incs=[(c_w, 16)])
                        wl_started.append(1)
                wl_started = []
                W_N = 16 * len(wload_q)
                WK_N = 16 * 6 if kind == "kq" else W_N

                if kind == "kq":
                    rounds_in = lambda ch: 16 + (16 if ch < NQCH else 0)
                    ropes_in = lambda ch: 16 + (16 if ch < NQCH else 0)
                else:
                    rounds_in = lambda ch: 16
                    ropes_in = lambda ch: 0
                rounds_before = [0]
                ropes_thru = [0]
                for ch in range(NCH):
                    rounds_before.append(rounds_before[-1] + rounds_in(ch))
                    ropes_thru.append(ropes_thru[-1] + ropes_in(ch))

                def ln_tile(i):
                    s3, s2 = i % 4, i % 4

                    def f_load(h, i=i, s3=s3):
                        return h.dma_start(xbf[s3][:, :],
                                           x_all[i * 128:(i + 1) * 128, :])
                    sch.add("g", f_load,
                            waits=[(c_xn, max(0, i - 3)), (c_base, BASE_N)] + gate,
                            incs=[xldr.inc(xoff + i)])
                    pump_wload()

                    def f_cp(h, s3=s3, s2=s2):
                        return h.activation(junk[:, :], xbf[s3][:, :], AF.Copy,
                                            accum_out=sxp[:, s2:s2 + 1])
                    sch.add("a", f_cp,
                            waits=[xldr.wait_done(xoff + i), (c_st1, max(0, i - 3))])

                    def f_sq(h, s3=s3, s2=s2):
                        return h.activation(junk[:, :], xbf[s3][:, :], AF.Square,
                                            accum_out=sxxp[:, s2:s2 + 1])
                    sch.add("a", f_sq, incs=[(c_sq, 1)])

                    def f_mean(h, s2=s2):
                        return h.tensor_scalar(meanp[:, s2:s2 + 1],
                                               sxp[:, s2:s2 + 1],
                                               1.0 / D, None, ALU.mult)
                    sch.add("v", f_mean,
                            waits=[(c_sq, i + 1), (c_msq, max(0, i - 3))],
                            incs=[(c_mn, 1)])

                    def f_msq(h, s2=s2):
                        return h.activation(msqp[:, s2:s2 + 1],
                                            meanp[:, s2:s2 + 1], AF.Square)
                    sch.add("a", f_msq, waits=[(c_mn, i + 1)], incs=[(c_msq, 1)])

                    def f_var(h, s2=s2):
                        return h.scalar_tensor_tensor(
                            mvarp[:, s2:s2 + 1], sxxp[:, s2:s2 + 1], 1.0 / D,
                            msqp[:, s2:s2 + 1], ALU.mult, ALU.subtract)
                    sch.add("v", f_var, waits=[(c_msq, i + 1)], incs=[(c_st1, 1)])

                    def f_sqrt(h, s2=s2):
                        h.activation(stdp[:, s2:s2 + 1], mvarp[:, s2:s2 + 1],
                                     AF.Ln, bias=eps_ap)
                        return h.activation(rsp[:, s2:s2 + 1], stdp[:, s2:s2 + 1],
                                            AF.Exp, scale=-0.5)
                    sch.add("a", f_sqrt,
                            waits=[(c_st1, i + 1), (c_xn, max(0, i - 3))],
                            incs=[(c_std, 1)])

                    def f_xn(h, s3=s3, s2=s2):
                        return h.tensor_scalar(xn[s3][:, :], xbf[s3][:, :],
                                               meanp[:, s2:s2 + 1],
                                               rsp[:, s2:s2 + 1],
                                               ALU.subtract, ALU.mult)
                    sch.add("v", f_xn,
                            waits=[(c_std, i + 1), (c_T0, max(0, 2 * (i - 4) + 2))],
                            incs=[(c_xn, 1)])

                def t_batch(i, b):
                    ch = i // TPC
                    par = ch % 2
                    st = i % TPC

                    def f_T(h, i=i, b=b):
                        last = None
                        for j in range(8):
                            cc0 = 8 * b + j
                            last = h.transpose(pst[b][:, j, :],
                                               xn[i % 4][:, cc0 * 128:(cc0 + 1) * 128],
                                               ident_sb[:, :])
                        return last
                    w = [(c_xn, i + 1), (c_xnTe, max(0, 2 * (i - 1) + b + 1))]
                    if ch >= 2:
                        w.append((c_rnd, rounds_before[ch - 1]))
                    sch.add("t", f_T, waits=w, incs=[(c_T0, 1)])

                    def f_ev(h, par=par, b=b, st=st):
                        return h.tensor_copy(
                            xnT[par][:, 8 * b:8 * b + 8, st * 128:(st + 1) * 128],
                            pst[b][:, :, :])
                    sch.add("v", f_ev, waits=[(c_T0, 2 * i + b + 1)],
                            incs=[(c_xnTe, 1)])

                def tbl_load(ch):
                    sl = ch % 2
                    w = [(c_rope, ropes_thru[ch - 1])] if ch >= 2 else []

                    def f_c(h, ch=ch, sl=sl):
                        return h.dma_start(tb_sl[sl][:, :, :], costab[:, ch, :, :])
                    sch.add("a", f_c, waits=w, incs=[tbr.inc(ch)])

                state = {"r": 0, "skq": 0, "sv": 0, "rp": 0}

                def do_round(knd, ch, a1, a2=0):
                    par = ch % 2
                    rr = state["r"]
                    pp = rr % 4

                    if knd in ("k", "q"):
                        oc = a1
                        wsb = wk_sb if knd == "k" else qw_sb

                        def f_mm(h, wsb=wsb, oc=oc, par=par, pp=pp):
                            last = None
                            for c in range(16):
                                last = h.matmul(psr[pp][:, :],
                                                wsb[:, c, oc * 128:(oc + 1) * 128],
                                                xnT[par][:, c, :],
                                                start=(c == 0), stop=(c == 15))
                            return last
                    else:
                        st4, eq = a1, a2

                        def f_mm(h, st4=st4, eq=eq, par=par, pp=pp):
                            last = None
                            for c in range(16):
                                last = h.matmul(psr[pp][:, :],
                                                xnT[par][:, c, st4 * 128:(st4 + 1) * 128],
                                                wv_sb[:, c, eq * CH:(eq + 1) * CH],
                                                start=(c == 0), stop=(c == 15))
                            return last
                    wneed = WK_N if knd in ("k", "v") else W_N
                    sch.add("t", f_mm,
                            waits=[(c_xnTe, 8 * (ch + 1)), (c_w, wneed),
                                   (c_rev, max(0, rr - 3))],
                            incs=[(c_rnd, 1)])

                    if knd in ("k", "q"):
                        skq = state["skq"]
                        es = skq % 4
                        bias = (kvb_sb[:, a1:a1 + 1] if knd == "k"
                                else qb_sb[:, a1:a1 + 1])

                        def f_ev(h, es=es, pp=pp, bias=bias):
                            return h.activation(kq_sb[es][:, :], psr[pp][:, :],
                                                AF.Identity, bias=bias)
                        sch.add("a", f_ev,
                                waits=[(c_rnd, rr + 1), spkq.wait_done(skq - 4)],
                                incs=[(c_rev, 1)])

                        sl = ch % 2

                        def f_rope(h, es=es, sl=sl):
                            h.tensor_copy(swp[0:64, :], kq_sb[es][64:128, :])
                            h.tensor_copy(swp[64:128, :], kq_sb[es][0:64, :])
                            h.tensor_mul(swp[:, :], swp[:, :], tb_sl[sl][:, 1, :])
                            h.tensor_mul(kq_sb[es][:, :], kq_sb[es][:, :],
                                         tb_sl[sl][:, 0, :])
                            return h.tensor_add(kq_sb[es][:, :], kq_sb[es][:, :],
                                                swp[:, :])
                        sch.add("v", f_rope,
                                waits=[(c_rev, rr + 1), tbr.wait_done(ch)],
                                incs=[(c_rope, 1)])

                        dst = (kTs[:, a1, ch * CH:(ch + 1) * CH] if knd == "k"
                               else qTs[:, a1, ch * CH:(ch + 1) * CH])

                        def f_sp(h, es=es, dst=dst):
                            return h.dma_start(dst, kq_sb[es][:, :])
                        sch.add("s", f_sp, waits=[(c_rope, state["rp"] + 1)],
                                incs=[spkq.inc(skq)])
                        state["skq"] += 1
                        state["rp"] += 1
                    else:
                        st4, eq = a1, a2
                        sv = state["sv"]
                        es = sv % 4

                        def f_ev(h, es=es, pp=pp, eq=eq):
                            return h.tensor_add(v_sb[es][:, :], psr[pp][:, :],
                                                vb_sb[:, eq * CH:(eq + 1) * CH])
                        sch.add("v", f_ev,
                                waits=[(c_rnd, rr + 1), spv.wait_done(sv - 4)],
                                incs=[(c_rev, 1)])

                        row0 = (ch * TPC + st4) * 128

                        def f_sp(h, es=es, row0=row0, eq=eq):
                            return h.dma_start(
                                vs[row0:row0 + 128, eq * CH:(eq + 1) * CH],
                                v_sb[es][:, :])
                        sch.add("s", f_sp, waits=[(c_rev, rr + 1)],
                                incs=[spv.inc(sv)])
                        state["sv"] += 1
                    state["r"] += 1

                for ch in range(NCH + 1):
                    if ch >= 1:
                        while wload_q:
                            pump_wload()
                    if ch < NCH and kind == "kq":
                        tbl_load(ch)
                    if ch == 0:
                        for st in range(TPC):
                            ln_tile(st)
                        for st in range(TPC):
                            for b in range(2):
                                t_batch(st, b)
                    if ch >= 1:
                        cur = ch - 1
                        pend_ln = ([ch * TPC + st for st in range(TPC)]
                                   if ch < NCH else [])
                        pend_tb = ([(ch * TPC + st, b) for st in range(TPC)
                                    for b in range(2)] if ch < NCH else [])
                        if kind == "kq":
                            rlist = ([("k", cur, oc) for oc in range(16)]
                                     + ([("q", cur, oc) for oc in range(16)]
                                        if cur < NQCH else []))
                        else:
                            rlist = [("v", cur, st4, eq)
                                     for st4 in range(TPC) for eq in range(TPC)]
                        step = max(1, len(rlist) // 16)
                        LEAD = 4
                        slot = 0
                        for rn, r in enumerate(rlist):
                            do_round(r[0], r[1], r[2], r[3] if len(r) > 3 else 0)
                            if rn % step == step - 1:
                                slot += 1
                                if pend_ln:
                                    ln_tile(pend_ln.pop(0))
                                if pend_tb and slot > LEAD:
                                    t_batch(*pend_tb.pop(0))
                        while pend_ln:
                            ln_tile(pend_ln.pop(0))
                        while pend_tb:
                            t_batch(*pend_tb.pop(0))

                _emit(nc, sch)

        if "P0" in phases:
            emit_p0_pass("kq")
            emit_p0_pass("v")
        P0_DONE = spkq.totals() + spv.totals()

        if "A" in phases:
            with ExitStack() as ph:
                sb = lambda n, shp, dt: ph.enter_context(nc.sbuf_tensor(n, shp, dt))
                qT_sb = sb("qT_sb", [128, 16, 512], BF16)
                kT_b = [sb(f"kT_b{j}", [128, 16, 1024], BF16) for j in range(2)]
                v_b = [sb(f"v_b{j}", [128, 8, D], BF16) for j in range(2)]
                o_acc = sb("o_acc", [128, 4, D], F32)
                p_sb = [sb(f"p_sb{j}", [128, 1024], BF16) for j in range(2)]
                pT_sb = [sb(f"pT_sb{j}", [128, 8, 128], BF16) for j in range(2)]
                nm = sb("nm", [128, 4, 2], F32)
                pm = sb("pm", [128, 4, 2], F32)
                lsum = sb("lsum", [128, 4], F32)
                tmaxb = sb("tmaxb", [128, 1], F32)
                ccb = sb("ccb", [128, 2], F32)
                lpart = sb("lpart", [128, 2], F32)
                rlb = sb("rlb", [128, 2], F32)
                rlt = sb("rlt", [128, 1], F32)
                spcr = sb("spcr", [128, 8], F32)
                obf = [sb(f"obf{j}", [128, D], BF16) for j in range(2)]
                psS = ph.enter_context(nc.psum_tensor("psS", [128, 1024], F32))
                psT = ph.enter_context(nc.psum_tensor("psT", [128, 8, 128], BF16))
                psO = ph.enter_context(nc.psum_tensor("psO", [128, D], F32))

                qldr = DmaRing(ctr, "qld", 4)
                kldr = DmaRing(ctr, "kld", 2)
                vldr = DmaRing(ctr, "vld", 2)
                c_sc = ctr("c_sc")
                c_nm = ctr("c_nm")
                c_cc = ctr("c_cc")
                c_exp = ctr("c_exp")
                c_lup = ctr("c_lup")
                c_T = ctr("c_T")
                c_pTe = ctr("c_pTe")
                c_pv = ctr("c_pv")
                c_ofl = ctr("c_ofl")
                c_obf = ctr("c_obf")
                c_rm = ctr("c_rm")
                c_rl = ctr("c_rl")
                c_ms = ctr("c_ms")

                sch = new_sched()

                def f_init(h):
                    h.memset(o_acc[:, :, :], 0.0)
                    h.memset(lsum[:, :], 0.0)
                    h.memset(pm[:, :, :], -BIG)
                    return h.memset(nm[:, :, :], BIG)
                sch.add("v", f_init, incs=[(c_ms, 1)])

                def sc_round(sbk, idx):
                    gidx = sbk * IPB + idx
                    g = idx // 4
                    gg = sbk * NG + g
                    gp = g % 2
                    st = idx % 4
                    g2 = idx % 2

                    def f(h, gp=gp, st=st):
                        last = None
                        for nb in range(2):
                            for c in range(16):
                                last = h.matmul(
                                    psS[:, nb * 512:(nb + 1) * 512],
                                    qT_sb[:, c, st * 128:(st + 1) * 128],
                                    kT_b[gp][:, c, nb * 512:(nb + 1) * 512],
                                    start=(c == 0), stop=(c == 15))
                        return last
                    sch.add("t", f,
                            waits=[qldr.wait_done(sbk * 4 + st), kldr.wait_done(gg),
                                   (c_exp, gidx)],
                            incs=[(c_sc, 1)])

                    def f_rm(h):
                        return h.reduce_max(tmaxb[:, :], psS[:, :], axis=AX)
                    if not DBG_NOFLASH:
                        sch.add("v", f_rm, waits=[(c_sc, gidx + 1)],
                                incs=[(c_rm, 1)])

                    def f_cc(h, st=st, gp=gp, g2=g2):
                        return h.activation(ccb[:, g2:g2 + 1],
                                            pm[:, st, 1 - gp:2 - gp], AF.Exp,
                                            bias=nm[:, st, gp:gp + 1])
                    if not DBG_NOFLASH:
                        sch.add("a", f_cc,
                                waits=[(c_nm, gidx + 1), (c_ofl, max(0, gidx - 1))],
                                incs=[(c_cc, 1)])

                    if DBG_NOFLASH:
                        def f_exp(h, st=st, gp=gp, g2=g2):
                            return h.activation(p_sb[g2][:, :], psS[:, :], AF.Exp,
                                                scale=SCALE,
                                                accum_out=lpart[:, g2:g2 + 1])
                        sch.add("a", f_exp,
                                waits=[(c_sc, gidx + 1),
                                       (c_T, max(0, gidx - 1)),
                                       (c_lup, max(0, gidx - 1))],
                                incs=[(c_exp, 1)])
                    else:
                        def f_exp(h, st=st, gp=gp, g2=g2):
                            return h.activation(p_sb[g2][:, :], psS[:, :], AF.Exp,
                                                bias=nm[:, st, gp:gp + 1], scale=SCALE,
                                                accum_out=lpart[:, g2:g2 + 1])
                        sch.add("a", f_exp,
                                waits=[(c_T, max(0, gidx - 1)),
                                       (c_lup, max(0, gidx - 1))],
                                incs=[(c_exp, 1)])

                def nm_update(sbk, idx):
                    if DBG_NOFLASH:
                        return
                    gidx = sbk * IPB + idx
                    g = idx // 4
                    gp = g % 2
                    st = idx % 4
                    g2 = idx % 2

                    def f_nmu(h, st=st, gp=gp):
                        h.tensor_scalar(nm[:, st, gp:gp + 1], tmaxb[:, :],
                                        -SCALE, nm[:, st, 1 - gp:2 - gp],
                                        ALU.mult, ALU.min)
                        return h.tensor_scalar(pm[:, st, gp:gp + 1], tmaxb[:, :],
                                               SCALE, pm[:, st, 1 - gp:2 - gp],
                                               ALU.mult, ALU.max)
                    sch.add("v", f_nmu,
                            waits=[(c_rm, gidx + 1), (c_ms, c_ms.n),
                                   (c_cc, max(0, gidx - 1))],
                            incs=[(c_nm, 1)])

                def spacer3(h):
                    h.memset(spcr[:, :], 0.0)
                    h.memset(spcr[:, :], 1.0)
                    return h.memset(spcr[:, :], 2.0)

                def t_pv_pTe(sbk, idx):
                    gidx = sbk * IPB + idx
                    g2 = idx % 2

                    def f_T(h, g2=g2):
                        last = None
                        for tc in range(8):
                            last = h.transpose(psT[:, tc, :],
                                               p_sb[g2][:, tc * 128:(tc + 1) * 128],
                                               ident_sb[:, :])
                        return last
                    sch.add("t", f_T, waits=[(c_exp, gidx + 1), (c_pTe, gidx)],
                            incs=[(c_T, 1)])

                    def f_pTe(h, g2=g2):
                        return h.tensor_copy(pT_sb[g2][:, :, :], psT[:, :, :])
                    sch.add("v", f_pTe,
                            waits=[(c_T, gidx + 1), (c_pv, max(0, gidx - 1))],
                            incs=[(c_pTe, 1)])

                def t_pv_a(sbk, idx):
                    gidx = sbk * IPB + idx
                    g2 = idx % 2
                    st = idx % 4

                    cscal = 1.0 if DBG_NOFLASH else ccb[:, g2:g2 + 1]

                    def f_lup(h, st=st, g2=g2, cscal=cscal):
                        return h.scalar_tensor_tensor(
                            lsum[:, st:st + 1], lsum[:, st:st + 1],
                            cscal, lpart[:, g2:g2 + 1],
                            ALU.mult, ALU.add)
                    lw = [(c_exp, gidx + 1)]
                    if not DBG_NOFLASH:
                        lw.append((c_cc, gidx + 1))
                    sch.add("v", f_lup, waits=lw, incs=[(c_lup, 1)])

                def t_pv_b(sbk, idx):
                    gidx = sbk * IPB + idx
                    g = idx // 4
                    gg = sbk * NG + g
                    gp = g % 2
                    st = idx % 4
                    g2 = idx % 2

                    def f_pv(h, g2=g2, gp=gp):
                        last = None
                        for nb in range(4):
                            for tc in range(8):
                                last = h.matmul(
                                    psO[:, nb * 512:(nb + 1) * 512],
                                    pT_sb[g2][:, tc, :],
                                    v_b[gp][:, tc, nb * 512:(nb + 1) * 512],
                                    start=(tc == 0), stop=(tc == 7))
                        return last
                    sch.add("t", f_pv,
                            waits=[(c_pTe, gidx + 1), vldr.wait_done(gg),
                                   (c_ofl, gidx)],
                            incs=[(c_pv, 1)])

                    cscal2 = 1.0 if DBG_NOFLASH else ccb[:, g2:g2 + 1]

                    def f_ofl(h, st=st, g2=g2, cscal2=cscal2):
                        return h.scalar_tensor_tensor(
                            o_acc[:, st, :], o_acc[:, st, :],
                            cscal2, psO[:, :], ALU.mult, ALU.add)
                    sch.add("v", f_ofl, waits=[(c_pv, gidx + 1)],
                            incs=[(c_ofl, 1)])

                for sbk in range(NSB):
                    for g in range(2):
                        gg = sbk * NG + g
                        gp = g % 2

                        def f_k(h, g=g, gp=gp):
                            return h.dma_start(kT_b[gp][:, :, :],
                                               kTs[:, :, g * 1024:(g + 1) * 1024])
                        sch.add("s", f_k,
                                waits=P0_DONE + [(c_sc, max(0, 4 * (gg - 1)))],
                                incs=[kldr.inc(gg)])

                        def f_v(h, g=g, gp=gp):
                            return h.dma_start(
                                v_b[gp][:, :, :],
                                vs[g * 1024:(g + 1) * 1024, :]
                                .rearrange("(tc p) e -> p tc e", p=128))
                        sch.add("g", f_v,
                                waits=P0_DONE + [(c_pv, max(0, 4 * (gg - 1)))],
                                incs=[vldr.inc(gg)])

                    for qp in range(4):
                        def f_q(h, sbk=sbk, qp=qp):
                            return h.dma_start(
                                qT_sb[:, :, qp * 128:(qp + 1) * 128],
                                qTs[:, :, sbk * 512 + qp * 128:
                                    sbk * 512 + (qp + 1) * 128])
                        w = [(c_sc, max(0, (sbk - 1) * IPB + IPB - 4 + qp + 1))]
                        sch.add("a", f_q, waits=P0_DONE + w,
                                incs=[qldr.inc(sbk * 4 + qp)])

                    if sbk > 0:
                        def f_ms(h):
                            if DBG_NOFLASH:
                                h.memset(o_acc[:, :, :], 0.0)
                                h.memset(lsum[:, :], 0.0)
                            h.memset(pm[:, :, :], -BIG)
                            return h.memset(nm[:, :, :], BIG)
                        msw = [(c_exp, IPB * sbk)]
                        if DBG_NOFLASH:
                            msw.append((c_obf, 4 * sbk))
                        sch.add("v", f_ms, waits=msw, incs=[(c_ms, 1)])

                    for g in range(2, NG):
                        gg = sbk * NG + g
                        gp = g % 2

                        def f_k(h, g=g, gp=gp):
                            return h.dma_start(kT_b[gp][:, :, :],
                                               kTs[:, :, g * 1024:(g + 1) * 1024])
                        sch.add("s", f_k,
                                waits=P0_DONE + [(c_sc, max(0, 4 * (gg - 1)))],
                                incs=[kldr.inc(gg)])

                        def f_v(h, g=g, gp=gp):
                            return h.dma_start(
                                v_b[gp][:, :, :],
                                vs[g * 1024:(g + 1) * 1024, :]
                                .rearrange("(tc p) e -> p tc e", p=128))
                        sch.add("g", f_v,
                                waits=P0_DONE + [(c_pv, max(0, 4 * (gg - 1)))],
                                incs=[vldr.inc(gg)])

                    sc_round(sbk, 0)
                    if not DBG_NOFLASH:
                        sch.add("v", spacer3)
                    nm_update(sbk, 0)
                    for idx in range(1, IPB):
                        t_pv_pTe(sbk, idx - 1)
                        sc_round(sbk, idx)
                        t_pv_a(sbk, idx - 1)
                        if not DBG_NOFLASH:
                            sch.add("v", lambda h: h.memset(spcr[:, :], 0.0))
                        nm_update(sbk, idx)
                        t_pv_b(sbk, idx - 1)
                    t_pv_pTe(sbk, IPB - 1)
                    t_pv_a(sbk, IPB - 1)
                    t_pv_b(sbk, IPB - 1)

                    for st in range(4):
                        oi = sbk * 4 + st
                        os2 = oi % 2

                        def f_rl(h, st=st, os2=os2):
                            h.activation(rlt[:, :], lsum[:, st:st + 1], AF.Ln)
                            return h.activation(rlb[:, os2:os2 + 1], rlt[:, :],
                                                AF.Exp, scale=-1.0)
                        sch.add("a", f_rl,
                                waits=[(c_lup, IPB * (sbk + 1)),
                                       (c_obf, max(0, oi - 1))],
                                incs=[(c_rl, 1)])

                        def f_on(h, st=st, os2=os2):
                            if DBG_UNNORM:
                                return h.tensor_scalar(obf[os2][:, :],
                                                       o_acc[:, st, :],
                                                       1.0, None, ALU.mult)
                            return h.tensor_scalar(obf[os2][:, :], o_acc[:, st, :],
                                                   rlb[:, os2:os2 + 1], None,
                                                   ALU.mult)
                        sch.add("v", f_on,
                                waits=[(c_ofl, IPB * (sbk + 1)), (c_rl, oi + 1),
                                       spo.wait_done(oi - 2)],
                                incs=[(c_obf, 1)])

                        def f_sp(h, oi=oi, os2=os2):
                            return h.dma_start(ons[oi * 128:(oi + 1) * 128, :],
                                               obf[os2][:, :])
                        sch.add("a", f_sp, waits=[(c_obf, oi + 1)],
                                incs=[spo.inc(oi)])

                _emit(nc, sch)
        A_DONE = spo.totals()

        if "A2" in phases or "F1" in phases:
            with ExitStack() as af:
                ow_sb = af.enter_context(nc.sbuf_tensor("ow_sb", [128, 16, D], BF16))
                xn2T = af.enter_context(nc.sbuf_tensor("xn2T", [128, 16, HH2], BF16))
                sg_sb = af.enter_context(nc.sbuf_tensor("sg_sb", [128, 64], F32))
                su_sb = af.enter_context(nc.sbuf_tensor("su_sb", [128, 64], F32))

                c_w2 = ctr("c_w2")
                oldr = DmaRing(ctr, "old", 2)
                xld2r = DmaRing(ctr, "xld2", 2)
                c_T2 = ctr("c_T2")
                c_oTe = ctr("c_oTe")
                c_opr = ctr("c_opr")
                c_x2 = ctr("c_x2")
                c_sq2 = ctr("c_sq2")
                c_mn2 = ctr("c_mn2")
                c_msq2 = ctr("c_msq2")
                c_st2 = ctr("c_st2")
                c_std2 = ctr("c_std2")
                c_xn2 = ctr("c_xn2")
                c_T2b = ctr("c_T2b")
                c_xn2Te = ctr("c_xn2Te")
                wgr = DmaRing(ctr, "wgr", 2)
                wur = DmaRing(ctr, "wur", 2)
                c_gr = ctr("c_gr")
                c_ur = ctr("c_ur")
                c_gact = ctr("c_gact")
                c_us = ctr("c_us")
                c_fu = ctr("c_fu")

                XN2TE_HALF = 2 * TPH if "A2" in phases else 0

                first_af = True
                for hh in range(2):
                    if "A2" in phases:
                        with ExitStack() as ph:
                            sb = lambda n, shp, dt: ph.enter_context(
                                nc.sbuf_tensor(f"{n}_a2{hh}", shp, dt))
                            oT_sb = sb("oT_sb", [128, 32, 128], BF16)
                            ob_in = [sb(f"ob_in{j}", [128, D], BF16) for j in range(2)]
                            x2sb = [sb(f"x2sb{j}", [128, D], F32) for j in range(2)]
                            x2bf = [sb(f"x2bf{j}", [128, D], BF16) for j in range(2)]
                            xn2 = [sb(f"xn2_{j}", [128, D], BF16) for j in range(2)]
                            junk2 = sb("junk2", [128, D], BF16)
                            sxp2 = sb("sxp2", [128, 2], F32)
                            sxxp2 = sb("sxxp2", [128, 2], F32)
                            meanp2 = sb("meanp2", [128, 2], F32)
                            msqp2 = sb("msqp2", [128, 2], F32)
                            mvarp2 = sb("mvarp2", [128, 2], F32)
                            stdp2 = sb("stdp2", [128, 2], F32)
                            rsp2 = sb("rsp2", [128, 2], F32)
                            pstA = [ph.enter_context(
                                nc.psum_tensor(f"pstA{j}_a2{hh}", [128, 8, 128],
                                               BF16)) for j in range(2)]
                            pstB = [ph.enter_context(
                                nc.psum_tensor(f"pstB{j}_a2{hh}", [128, 8, 128],
                                               BF16)) for j in range(2)]
                            psAt = [ph.enter_context(
                                nc.psum_tensor(f"psAt{j}_a2{hh}", [128, 1024], F32))
                                for j in range(2)]

                            sch = new_sched()
                            if first_af:
                                for qq in range(4):
                                    def f_ow(h, qq=qq):
                                        return h.dma_start(
                                            ow_sb[:, 4 * qq:4 * qq + 4, :],
                                            owT.rearrange("(c p) j -> p c j", p=128)
                                            [:, 4 * qq:4 * qq + 4, :])
                                    sch.add("g", f_ow,
                                            waits=A_DONE if qq == 0 else [],
                                            incs=[(c_w2, 16)])
                                sch.add("g", lambda h: h.dma_start(sg_sb[:, :], sg2),
                                        incs=[(c_w2, 16)])
                                sch.add("g", lambda h: h.dma_start(su_sb[:, :], su2),
                                        incs=[(c_w2, 16)])
                                first_af = False
                            W2_N = c_w2.n

                            def emit_ln2(tt2):
                                tg2 = hh * TPH + tt2
                                t22 = tt2 % 2

                                def f_cp(h, t22=t22):
                                    return h.activation(
                                        x2bf[t22][:, :], x2sb[t22][:, :], AF.Copy,
                                        accum_out=sxp2[:, t22:t22 + 1])
                                sch.add("a", f_cp,
                                        waits=[(c_x2, 2 * tg2 + 2),
                                               (c_xn2, max(0, tg2 - 1)),
                                               (c_st2, max(0, tg2 - 1))])

                                def f_sq(h, t22=t22):
                                    return h.activation(
                                        junk2[:, :], x2sb[t22][:, :], AF.Square,
                                        accum_out=sxxp2[:, t22:t22 + 1])
                                sch.add("a", f_sq, incs=[(c_sq2, 1)])

                                def f_mean(h, t22=t22):
                                    return h.tensor_scalar(
                                        meanp2[:, t22:t22 + 1], sxp2[:, t22:t22 + 1],
                                        1.0 / D, None, ALU.mult)
                                sch.add("v", f_mean,
                                        waits=[(c_sq2, tg2 + 1),
                                               (c_std2, max(0, tg2 - 1))],
                                        incs=[(c_mn2, 1)])

                                def f_msq(h, t22=t22):
                                    return h.activation(
                                        msqp2[:, t22:t22 + 1], meanp2[:, t22:t22 + 1],
                                        AF.Square)
                                sch.add("a", f_msq, waits=[(c_mn2, tg2 + 1)],
                                        incs=[(c_msq2, 1)])

                                def f_var(h, t22=t22):
                                    return h.scalar_tensor_tensor(
                                        mvarp2[:, t22:t22 + 1], sxxp2[:, t22:t22 + 1],
                                        1.0 / D, msqp2[:, t22:t22 + 1],
                                        ALU.mult, ALU.subtract)
                                sch.add("v", f_var, waits=[(c_msq2, tg2 + 1)],
                                        incs=[(c_st2, 1)])

                                def f_sqrt(h, t22=t22):
                                    h.activation(
                                        stdp2[:, t22:t22 + 1], mvarp2[:, t22:t22 + 1],
                                        AF.Ln, bias=eps_ap)
                                    return h.activation(
                                        rsp2[:, t22:t22 + 1], stdp2[:, t22:t22 + 1],
                                        AF.Exp, scale=-0.5)
                                sch.add("a", f_sqrt,
                                        waits=[(c_st2, tg2 + 1),
                                               (c_xn2, max(0, tg2 - 1))],
                                        incs=[(c_std2, 1)])

                                def f_xn(h, t22=t22):
                                    return h.tensor_scalar(
                                        xn2[t22][:, :], x2bf[t22][:, :],
                                        meanp2[:, t22:t22 + 1], rsp2[:, t22:t22 + 1],
                                        ALU.subtract, ALU.mult)
                                sch.add("v", f_xn,
                                        waits=[(c_std2, tg2 + 1),
                                               (c_T2b, max(0, 2 * (tg2 - 1)))],
                                        incs=[(c_xn2, 1)])

                            def emit_xn2T(tt2):
                                tg2 = hh * TPH + tt2
                                t22 = tt2 % 2
                                for b, pst2 in ((0, pstA[tt2 % 2]),
                                                (1, pstB[tt2 % 2])):
                                    def f_T(h, t22=t22, b=b, pst2=pst2):
                                        last = None
                                        for j in range(8):
                                            cc0 = 8 * b + j
                                            last = h.transpose(
                                                pst2[:, j, :],
                                                xn2[t22][:, cc0 * 128:(cc0 + 1) * 128],
                                                ident_sb[:, :])
                                        return last
                                    sch.add("t", f_T,
                                            waits=[(c_xn2, tg2 + 1),
                                                   (c_oTe, 2 * tg2 + b + 1)],
                                            incs=[(c_T2b, 1)])

                                    def f_ev(h, b=b, pst2=pst2, tt2=tt2, hh=hh):
                                        return h.tensor_copy(
                                            xn2T[:, 8 * b:8 * b + 8,
                                                 tt2 * 128:(tt2 + 1) * 128],
                                            pst2[:, :, :])
                                    sch.add("v", f_ev,
                                            waits=[(c_T2b, 2 * tg2 + b + 1),
                                                   (c_gr, 64 * hh),
                                                   (c_ur, 64 * hh)],
                                            incs=[(c_xn2Te, 1)])

                            for tt in range(TPH):
                                tg = hh * TPH + tt
                                t2 = tt % 2

                                def f_ol(h, tg=tg, t2=t2):
                                    return h.dma_start(
                                        ob_in[t2][:, :],
                                        ons[tg * 128:(tg + 1) * 128, :])
                                f1gate = ([(cc, 16 * 32 * hh) for cc in spf.c]
                                          if "F1" in phases and hh else [])
                                sch.add("s", f_ol,
                                        waits=A_DONE + f1gate
                                        + [(c_T2, max(0, 2 * (tg - 1)))],
                                        incs=[oldr.inc(tg)])

                                for b, pst2 in ((0, pstA[tt % 2]), (1, pstB[tt % 2])):
                                    def f_T(h, t2=t2, b=b, pst2=pst2):
                                        last = None
                                        for j in range(8):
                                            cc0 = 8 * b + j
                                            last = h.transpose(
                                                pst2[:, j, :],
                                                ob_in[t2][:, cc0 * 128:(cc0 + 1) * 128],
                                                ident_sb[:, :])
                                        return last
                                    sch.add("t", f_T,
                                            waits=[oldr.wait_done(tg),
                                                   (c_xn2Te, max(0, 2 * (tg - 2) + b + 1))],
                                            incs=[(c_T2, 1)])

                                    def f_ev(h, t2=t2, b=b, pst2=pst2):
                                        return h.tensor_copy(
                                            oT_sb[:, t2 * 16 + 8 * b:t2 * 16 + 8 * b + 8, :],
                                            pst2[:, :, :])
                                    sch.add("v", f_ev,
                                            waits=[(c_T2, 2 * tg + b + 1),
                                                   (c_opr, max(0, 2 * (tg - 1)))],
                                            incs=[(c_oTe, 1)])

                                if tt >= 1:
                                    emit_ln2(tt - 1)

                                def f_xl(h, tg=tg, t2=t2):
                                    return h.dma_start(
                                        x2sb[t2][:, :],
                                        x_all[tg * 128:(tg + 1) * 128, :])
                                sch.add("s", f_xl,
                                        waits=A_DONE + f1gate
                                        + [(c_sq2, max(0, tg - 1)),
                                           spx2.wait_done(tg - 2)],
                                        incs=[xld2r.inc(tg)])

                                for rr in range(2):
                                    for nb in range(2):
                                        for qq in range(2):
                                            def f_pr(h, t2=t2, rr=rr, nb=nb, qq=qq):
                                                last = None
                                                for c in range(8 * qq, 8 * qq + 8):
                                                    last = h.matmul(
                                                        psAt[rr][:, nb * 512:(nb + 1) * 512],
                                                        oT_sb[:, t2 * 16 + c, :],
                                                        ow_sb[:, c, rr * 1024 + nb * 512:
                                                              rr * 1024 + (nb + 1) * 512],
                                                        start=(c == 0), stop=(c == 15))
                                                return last
                                            w_pr = [(c_w2, 32 * (qq + 1))]
                                            if nb == 0 and qq == 0:
                                                w_pr += [(c_oTe, 2 * tg + 2),
                                                         (c_x2, max(0, 2 * tg + rr - 1))]
                                            sch.add("t", f_pr, waits=w_pr,
                                                    incs=[(c_opr, 1)]
                                                    if (nb == 1 and qq == 1) else [])

                                    def f_x2(h, t2=t2, rr=rr):
                                        return h.tensor_add(
                                            x2sb[t2][:, rr * 1024:(rr + 1) * 1024],
                                            psAt[rr][:, :],
                                            x2sb[t2][:, rr * 1024:(rr + 1) * 1024])
                                    sch.add("v", f_x2,
                                            waits=[(c_opr, 2 * tg + rr + 1),
                                                   xld2r.wait_done(tg)],
                                            incs=[(c_x2, 1)])

                                if tt >= 1:
                                    emit_xn2T(tt - 1)

                                def f_spx(h, tg=tg, t2=t2):
                                    return h.dma_start(
                                        x2s[tg * 128:(tg + 1) * 128, :],
                                        x2sb[t2][:, :])
                                sch.add("a", f_spx, waits=[(c_x2, 2 * tg + 2)],
                                        incs=[spx2.inc(tg)])

                            emit_ln2(TPH - 1)
                            emit_xn2T(TPH - 1)
                            _emit(nc, sch)

                    if "F1" in phases:
                        with ExitStack() as ph:
                            sb = lambda n, shp, dt: ph.enter_context(
                                nc.sbuf_tensor(f"{n}_f1{hh}", shp, dt))
                            wg_sb = [sb(f"wg_sb{j}", [128, D], BF16) for j in range(2)]
                            wu_sb = [sb(f"wu_sb{j}", [128, D], BF16) for j in range(2)]
                            gact = [sb(f"gact{j}", [128, HH2], BF16) for j in range(2)]
                            usb = [sb(f"usb{j}", [128, HH2], BF16) for j in range(2)]
                            fuse = [sb(f"fuse{j}", [128, HH2], BF16) for j in range(2)]
                            psG = ph.enter_context(
                                nc.psum_tensor(f"psG_f1{hh}", [128, HH2], F32))
                            psU = ph.enter_context(
                                nc.psum_tensor(f"psU_f1{hh}", [128, HH2], F32))

                            sch = new_sched()
                            if first_af:
                                sch.add("g", lambda h: h.dma_start(sg_sb[:, :], sg2),
                                        incs=[(c_w2, 16)])
                                sch.add("g", lambda h: h.dma_start(su_sb[:, :], su2),
                                        incs=[(c_w2, 16)])
                                first_af = False
                            WF_N = c_w2.n

                            for ic in range(64):
                                ig = hh * 64 + ic
                                i2 = ic % 2

                                phase_gate = (
                                    [(c_xn2Te, XN2TE_HALF * (hh + 1))]
                                    + [(cc, 16 * (TPH // 2) * (hh + 1))
                                       for cc in spx2.c]
                                    if "A2" in phases else [])

                                def f_lg(h, ic=ic, i2=i2):
                                    return h.dma_start(wg_sb[i2][:, :], wgt[ic, :, :])
                                sch.add("g", f_lg,
                                        waits=[(c_gr, max(0, ig - 1))] + phase_gate,
                                        incs=[wgr.inc(ig)])

                                def f_lu(h, ic=ic, i2=i2):
                                    return h.dma_start(wu_sb[i2][:, :], wut[ic, :, :])
                                sch.add("g", f_lu, waits=[(c_ur, max(0, ig - 1))],
                                        incs=[wur.inc(ig)])

                                def f_g(h, i2=i2):
                                    last = None
                                    for nb in range(HH2 // 512):
                                        for c in range(16):
                                            last = h.matmul(
                                                psG[:, nb * 512:(nb + 1) * 512],
                                                wg_sb[i2][:, c * 128:(c + 1) * 128],
                                                xn2T[:, c, nb * 512:(nb + 1) * 512],
                                                start=(c == 0), stop=(c == 15))
                                    return last
                                sch.add("t", f_g,
                                        waits=[wgr.wait_done(ig),
                                               (c_xn2Te, XN2TE_HALF * (hh + 1)),
                                               (c_gact, ig)],
                                        incs=[(c_gr, 1)])

                                def f_u(h, i2=i2):
                                    last = None
                                    for nb in range(HH2 // 512):
                                        for c in range(16):
                                            last = h.matmul(
                                                psU[:, nb * 512:(nb + 1) * 512],
                                                wu_sb[i2][:, c * 128:(c + 1) * 128],
                                                xn2T[:, c, nb * 512:(nb + 1) * 512],
                                                start=(c == 0), stop=(c == 15))
                                    return last
                                sch.add("t", f_u,
                                        waits=[wur.wait_done(ig), (c_us, ig)],
                                        incs=[(c_ur, 1)])

                                def f_ga(h, ic=ic, i2=i2):
                                    return h.activation(gact[i2][:, :], psG[:, :],
                                                        AF.Silu,
                                                        scale=sg_sb[:, ic:ic + 1])
                                sch.add("a", f_ga,
                                        waits=[(c_gr, ig + 1), (c_w2, WF_N),
                                               (c_fu, max(0, ig - 1))],
                                        incs=[(c_gact, 1)])

                                def f_us(h, ic=ic, i2=i2):
                                    return h.tensor_scalar(usb[i2][:, :], psU[:, :],
                                                           su_sb[:, ic:ic + 1],
                                                           None, ALU.mult)
                                sch.add("v", f_us,
                                        waits=[(c_ur, ig + 1), (c_w2, WF_N),
                                               (c_fu, max(0, ig - 1))],
                                        incs=[(c_us, 1)])

                                def f_fu(h, i2=i2):
                                    return h.tensor_mul(fuse[i2][:, :],
                                                        gact[i2][:, :],
                                                        usb[i2][:, :])
                                sch.add("v", f_fu,
                                        waits=[(c_gact, ig + 1),
                                               spf.wait_done(ig - 2)],
                                        incs=[(c_fu, 1)])

                                def f_sp(h, ic=ic, i2=i2, hh=hh):
                                    ntl = HH2 // 128
                                    return h.dma_start(
                                        fs[:, hh * ntl:(hh + 1) * ntl, ic, :],
                                        fuse[i2][:, :].rearrange(
                                            "p (t s) -> p t s", s=128))
                                sch.add("s", f_sp, waits=[(c_fu, ig + 1)],
                                        incs=[spf.inc(ig)])
                            _emit(nc, sch)
        AF_DONE = spx2.totals() + spf.totals()

        if "F2" in phases:
            with ExitStack() as ph:
                sb = lambda n, shp, dt: ph.enter_context(nc.sbuf_tensor(n, shp, dt))
                wd_sb = sb("wd_sb", [128, 64, 1024], BF16)
                fuse_st = [sb(f"fuse_st{j}", [128, 64, 128], BF16) for j in range(3)]
                x2t = [sb(f"x2t{j}", [128, 1024], F32) for j in range(2)]
                ty = [sb(f"ty{j}", [128, 1024], F32) for j in range(2)]
                sd_sb = sb("sd_sb", [128, D], F32)
                psD = [ph.enter_context(nc.psum_tensor(f"psD{j}", [128, 1024], F32))
                       for j in range(2)]

                c_wd = ctr("c_wd")
                fldr = DmaRing(ctr, "fld", 3)
                x2ldr = DmaRing(ctr, "x2ld", 2)
                c_dr = ctr("c_dr")
                c_dev = ctr("c_dev")

                sch = new_sched()
                sch.add("g", lambda h: h.dma_start(sd_sb[:, :], sd_bc),
                        incs=[(c_wd, 16)])

                for eh in range(2):
                    for qq in range(4):
                        def f_wd(h, eh=eh, qq=qq):
                            return h.dma_start(
                                wd_sb[:, 16 * qq:16 * qq + 16, :],
                                wd.rearrange("(a p) e -> p a e", p=128)
                                [:, 16 * qq:16 * qq + 16,
                                 eh * 1024:(eh + 1) * 1024])
                        sch.add("g", f_wd,
                                waits=[(c_dr, NST * eh)] if qq == 0 else [],
                                incs=[(c_wd, 16)])

                    for st in range(NST):
                        ii = eh * NST + st
                        s2 = st % 2
                        sf3 = ii % 3

                        def f_fl(h, st=st, sf3=sf3):
                            return h.dma_start(fuse_st[sf3][:, :, :],
                                               fs[:, st, :, :])
                        sch.add("s", f_fl,
                                waits=AF_DONE + [(c_dr, max(0, ii - 2))],
                                incs=[fldr.inc(ii)])

                        def f_xl(h, st=st, eh=eh, s2=s2):
                            return h.dma_start(
                                x2t[s2][:, :],
                                x2s[st * 128:(st + 1) * 128,
                                    eh * 1024:(eh + 1) * 1024])
                        sch.add("a", f_xl,
                                waits=AF_DONE + [(c_dev, max(0, ii - 1))],
                                incs=[x2ldr.inc(ii)])

                        for nb in range(2):
                            for qq in range(4):
                                def f_mm(h, s2=s2, sf3=sf3, nb=nb, qq=qq):
                                    last = None
                                    for ic in range(16 * qq, 16 * qq + 16):
                                        last = h.matmul(
                                            psD[s2][:, nb * 512:(nb + 1) * 512],
                                            fuse_st[sf3][:, ic, :],
                                            wd_sb[:, ic, nb * 512:(nb + 1) * 512],
                                            start=(ic == 0), stop=(ic == 63))
                                    return last
                                w_mm = [(c_wd, 16 * (2 + 4 * eh + qq))]
                                if nb == 0 and qq == 0:
                                    w_mm += [fldr.wait_done(ii),
                                             (c_dev, max(0, ii - 1))]
                                sch.add("t", f_mm, waits=w_mm,
                                        incs=[(c_dr, 1)]
                                        if (nb == 1 and qq == 3) else [])

                        def f_ev(h, eh=eh, s2=s2):
                            h.tensor_mul(ty[s2][:, :], psD[s2][:, :],
                                         sd_sb[:, eh * 1024:(eh + 1) * 1024])
                            return h.tensor_add(ty[s2][:, :], ty[s2][:, :],
                                                x2t[s2][:, :])
                        sch.add("v", f_ev,
                                waits=[(c_dr, ii + 1), x2ldr.wait_done(ii),
                                       spy.wait_done(ii - 2)],
                                incs=[(c_dev, 1)])

                        def f_sp(h, st=st, eh=eh, s2=s2):
                            return h.dma_start(
                                y[st * 128:(st + 1) * 128,
                                  eh * 1024:(eh + 1) * 1024],
                                ty[s2][:, :])
                        sch.add("a", f_sp, waits=[(c_dev, ii + 1)],
                                incs=[spy.inc(ii)])

                sch.add("a", lambda h: h.nop(), waits=spy.totals())
                _emit(nc, sch)

        tail = Sched()
        tail_waits = (spkq.totals() + spv.totals() + spo.totals()
                      + spx2.totals() + spf.totals() + spy.totals())
        tail.add("a", lambda h: h.nop(), waits=tail_waits)
        tail.add("s", lambda h: h.nop(), waits=tail_waits)
        _emit(nc, tail)

    return nc



def _rope_tables():
    f = np.arange(64, dtype=np.float64)
    inv = 10000.0 ** (-f / 64.0)
    pos = np.arange(S, dtype=np.float64)
    ang = inv[:, None] * pos[None, :]
    cos = np.cos(ang)
    sin = np.sin(ang)
    return (np.concatenate([cos, cos], axis=0),
            np.concatenate([-sin, sin], axis=0))


def host_prepare(inputs):
    x = np.asarray(inputs["L0_attn_x"], dtype=np.float32)
    qw = np.asarray(inputs["L0_attn_q_w"], dtype=np.float32)
    qb = np.asarray(inputs["L0_attn_q_b"], dtype=np.float32)
    kvw = np.asarray(inputs["L0_attn_kv_w"], dtype=np.float32)
    kvb = np.asarray(inputs["L0_attn_kv_b"], dtype=np.float32)
    ow = np.asarray(inputs["L0_attn_o_w"], dtype=np.float32)
    wg = np.asarray(inputs["L0_ffn_w_gate"], dtype=np.int8)
    wu = np.asarray(inputs["L0_ffn_w_up"], dtype=np.int8)
    wdn = np.asarray(inputs["L0_ffn_w_down"], dtype=np.int8)
    sg = np.asarray(inputs["L0_ffn_s_gate"], dtype=np.float32)
    su = np.asarray(inputs["L0_ffn_s_up"], dtype=np.float32)
    sd = np.asarray(inputs["L0_ffn_s_down"], dtype=np.float32)

    cosT64, sinT64 = _rope_tables()

    shared = dict(
        qwT=np.ascontiguousarray(qw.T),
        kvwT=np.ascontiguousarray(kvw.T),
        owT=np.ascontiguousarray(ow.T),
        wgt=np.ascontiguousarray(
            wg.reshape(16, 128, 64, 128).transpose(2, 1, 0, 3)).reshape(64, 128, D),
        wut=np.ascontiguousarray(
            wu.reshape(16, 128, 64, 128).transpose(2, 1, 0, 3)).reshape(64, 128, D),
        wd=wdn,
        qb2=np.ascontiguousarray(qb.reshape(16, 128).T),
        kvb2=np.ascontiguousarray(kvb.reshape(32, 128).T),
        sg2=np.ascontiguousarray(sg.reshape(64, 128).T),
        su2=np.ascontiguousarray(su.reshape(64, 128).T),
        vb_bc=np.ascontiguousarray(np.broadcast_to(kvb[D:], (128, D))),
        sd_bc=np.ascontiguousarray(np.broadcast_to(sd, (128, D))),
        identt=np.eye(128, dtype=ml_dtypes.bfloat16),
        constt=np.repeat(np.array([[1e-5, 0.0, 0.0, 0.0]], np.float32), 128, 0),
    )

    def _mk_tab(c64, s64):
        t = np.stack([c64.reshape(128, S // CH, CH),
                      s64.reshape(128, S // CH, CH)], axis=2)
        return np.ascontiguousarray(t).astype(ml_dtypes.bfloat16)

    in_maps = []
    for c in range(8):
        b, h = c // 2, c % 2
        m = dict(shared)
        if h == 0:
            m["x_all"] = np.ascontiguousarray(x[b])
            m["costab"] = _mk_tab(cosT64, sinT64)
        else:
            perm = np.concatenate([np.arange(HS, S), np.arange(0, HS)])
            m["x_all"] = np.ascontiguousarray(x[b][perm])
            m["costab"] = _mk_tab(np.ascontiguousarray(cosT64[:, perm]),
                                  np.ascontiguousarray(sinT64[:, perm]))
        in_maps.append(m)
    return in_maps


def kernel(**inputs):
    in_maps = host_prepare(inputs)
    nc = build_program()
    res = run_bass_kernel_spmd(nc, in_maps, core_ids=list(range(8)))
    out = np.empty((4, S, D), dtype=np.float32)
    for c in range(8):
        b, h = c // 2, c % 2
        out[b, h * HS:(h + 1) * HS, :] = np.asarray(res.results[c]["y"],
                                                    dtype=np.float32)
    return out
